# version 06
# baseline (speedup 1.0000x reference)
import numpy as np
import ml_dtypes
from contextlib import ExitStack

import concourse.bass as bass
import concourse.mybir as mybir
from concourse.bass_utils import run_bass_kernel_spmd

F32 = mybir.dt.float32
BF16 = mybir.dt.bfloat16
AF = mybir.ActivationFunctionType
ALU = mybir.AluOpType
AX = mybir.AxisListType

D = 2048
ND = 16
SEQ = 4096
OWN = 2048
NT_OWN = 16
NT_ALL = 32
HD_A = 64
NH_A = 32
SC_A = 0.125
HD_F = 128
NH_F = 16
SC_F = 1.0 / float(np.sqrt(128.0))
NEXP = 16
I_E = 1408
NI_E = 11
SI = 2816
NI_S = 22
EPS_LN = 1e-5
EPS_RMS = 1e-6
KTA = 24
KTB = 8
SUBI = 6
CONSTS = (EPS_LN, EPS_RMS, 1.0 / D, -0.5, SC_A, SC_F)

SEMS = ("pe", "dve", "act", "pool", "vsf", "ssf", "psf")


class _MI:
    def then_inc(self, *a, **k):
        return self


class MockEng:
    def __getattr__(self, name):
        return lambda *a, **k: _MI()


class Em:

    def __init__(self):
        self.lab = {}
        self.cnt = {n: 0 for n in SEMS}
        self.hs = {}
        self.mode = 1
        self._snap = None
        self._p1end = None
        self.nc = None
        self.stack = None

    def attach(self, nc, stack):
        self.nc = nc
        self.stack = stack
        for n in SEMS:
            self.hs[n] = stack.enter_context(nc.semaphore(n))

    def _s(self, name):
        if name not in self.cnt:
            self.cnt[name] = 0
        if self.mode == 2 and name not in self.hs:
            self.hs[name] = self.stack.enter_context(self.nc.semaphore(name))
        return name

    def p1(self):
        self.mode = 1
        self._snap = dict(self.cnt)

    def p2(self):
        self.mode = 2
        self._p1end = dict(self.cnt)
        self.cnt = dict(self._snap)

    def endphase(self, phase):
        if self.mode == 1:
            self.lab[("__endd", phase)] = dict(self.cnt)
        else:
            assert self.cnt == self._p1end, (phase, self.cnt, self._p1end)

    def _key(self, s, label):
        return (s,) + (label if isinstance(label, tuple) else (label,))

    def i(self, inst, s, k=1, label=None):
        self._s(s)
        self.cnt[s] += k
        if label is not None:
            key = self._key(s, label)
            if self.mode == 1:
                self.lab[key] = self.cnt[s]
            else:
                assert self.lab[key] == self.cnt[s], (key, self.lab[key], self.cnt[s])
        if self.mode == 2:
            inst.then_inc(self.hs[s], k)

    def mark(self, s, label):
        if self.mode == 1:
            self.lab[self._key(s, label)] = self.cnt[s]

    def w(self, eng, s, label):
        self._s(s)
        if self.mode == 2:
            eng.wait_ge(self.hs[s], self.lab[self._key(s, label)])

    def wopt(self, eng, s, label):
        self._s(s)
        if self.mode == 2:
            v = self.lab.get(self._key(s, label))
            if v is not None:
                eng.wait_ge(self.hs[s], v)

    def join(self, eng, prev_phase):
        if self.mode == 2:
            for s, v in sorted(self.lab[("__endd", prev_phase)].items()):
                if v > 0:
                    eng.wait_ge(self.hs[self._s(s)], v)

    def selfsem(self, eng, inst, s):
        self.cnt[s] += 1
        if self.mode == 2:
            inst.then_inc(self.hs[s], 1)
            eng.wait_ge(self.hs[s], self.cnt[s])


class Builder:
    def __init__(self, dev=False):
        self.dev = dev
        self.nc = bass.Bass("TRN2", target_bir_lowering=False)
        self.em = Em()
        self.io = {}
        self.t = {}
        self.guards = {}

    def declare_dram(self):
        nc, io = self.nc, self.io

        def inp(name, shape, dt=F32):
            io[name] = nc.dram_tensor(name, list(shape), dt, kind="ExternalInput").ap()

        inp("x_own", (OWN, D)); inp("x_oth", (OWN, D))
        inp("qbase_col", (128, 1)); inp("othbase_col", (128, 1))
        inp("Wqkv", (D, 3072)); inp("bqkv_col", (128, 24)); inp("bv_row", (1, 512))
        inp("Wo", (D, D)); inp("bo_row", (1, D))
        inp("wq", (D, D)); inp("wk", (D, D)); inp("wv", (D, D)); inp("wo", (D, D))
        inp("ln1_col", (128, 16)); inp("ln2_col", (128, 16))
        inp("gate_T", (D, 16))
        inp("egT", (NEXP, D, I_E)); inp("euT", (NEXP, D, I_E)); inp("edT", (NEXP, I_E, D))
        inp("sgT", (D, SI)); inp("suT", (D, SI)); inp("sdT", (SI, D))
        inp("consts_in", (128, 8))
        inp("ident_bf", (128, 128), BF16)
        inp("ones_bf", (1, 128), BF16)
        inp("ones_f32", (1, 128))

        io["out"] = nc.dram_tensor("out", [OWN, D], F32, kind="ExternalOutput").ap()

        spk = "ExternalOutput" if self.dev else "Internal"

        def spill(name, shape, dt=BF16, kind=None):
            io[name] = nc.dram_tensor(name, list(shape), dt, kind=kind or spk).ap()

        spill("qa_sp", (16, 128, OWN))
        spill("ka_sp", (4, 128, SEQ))
        spill("va_sp", (SEQ, 512))
        spill("qf_sp", (16, 128, OWN))
        spill("kf_sp", (16, 128, SEQ))
        spill("vf_sp", (SEQ, D))
        spill("ao_sp", (OWN, D), F32)
        spill("x2_sp", (OWN, D), F32)
        spill("r1_bounce", (32, 128), BF16, kind="Internal")
        spill("mask_sp", (16, 128, SEQ), BF16, kind="Internal")
        if self.dev:
            spill("h1_tap", (128, ND * SEQ), BF16)
            spill("tfm_tap", (128, ND * OWN), BF16)
            spill("cw_tap", (128, 272), F32)
            spill("ofm_tap", (128, NH_F * OWN), BF16)

    def alloc_persist(self, stack):
        nc, t = self.nc, self.t
        e = stack.enter_context
        t["PS"] = e(nc.psum_tensor("PS", [128, 3584], F32))
        t["negmean_s"] = e(nc.sbuf_tensor("negmean_s", [128, 32], F32))
        t["rstd_s"] = e(nc.sbuf_tensor("rstd_s", [128, 32], F32))
        t["vsum_s"] = e(nc.sbuf_tensor("vsum_s", [128, 32], F32))
        t["bias2_s"] = e(nc.sbuf_tensor("bias2_s", [128, 32], F32))
        t["r1_s"] = e(nc.sbuf_tensor("r1_s", [128, 32], F32))
        t["sb_qbase"] = e(nc.sbuf_tensor("sb_qbase", [128, 1], F32))
        t["sb_othbase"] = e(nc.sbuf_tensor("sb_othbase", [128, 1], F32))
        t["bqkv_col"] = e(nc.sbuf_tensor("sb_bqkv_col", [128, 24], F32))
        t["bv_row"] = e(nc.sbuf_tensor("sb_bv_row", [1, 512], F32))
        t["bo_row"] = e(nc.sbuf_tensor("sb_bo_row", [1, D], F32))
        t["ln1_col"] = e(nc.sbuf_tensor("sb_ln1_col", [128, 16], F32))
        t["ln2_col"] = e(nc.sbuf_tensor("sb_ln2_col", [128, 16], F32))
        t["sgate"] = e(nc.sbuf_tensor("sgate", [128, 256], BF16))
        t["s_cw"] = e(nc.sbuf_tensor("s_cw", [128, 272], F32))
        t["sident"] = e(nc.sbuf_tensor("sident", [128, 128], BF16))
        t["sones_bf"] = e(nc.sbuf_tensor("sones_bf", [1, 128], BF16))
        t["sones_f"] = e(nc.sbuf_tensor("sones_f", [1, 128], F32))
        t["stat8"] = e(nc.sbuf_tensor("stat8", [128, 16], F32))
        t["sconsts"] = e(nc.sbuf_tensor("sconsts", [128, 8], F32))
        for i, val in enumerate(CONSTS):
            nc.const_aps.register(F32, val, t["sconsts"][:, i:i + 1])

    def big_alloc(self, name, shape, dt):
        g = self.nc.sbuf_tensor(name, list(shape), dt)
        h = g.__enter__()
        self.guards[name] = g
        self.t[name] = h
        return h

    def big_free(self, name):
        self.guards.pop(name).__exit__(None, None, None)
        self.t.pop(name, None)

    def p0_y(self, y):
        em, t, io = self.em, self.t, self.io
        for nm, dst in (("ident_bf", "sident"), ("ones_bf", "sones_bf"),
                        ("ones_f32", "sones_f"), ("bqkv_col", "bqkv_col"),
                        ("bv_row", "bv_row"), ("bo_row", "bo_row"),
                        ("ln1_col", "ln1_col"), ("ln2_col", "ln2_col"),
                        ("qbase_col", "sb_qbase"), ("othbase_col", "sb_othbase"),
                        ("consts_in", "sconsts")):
            em.i(y.dma_start(t[dst][:, :], io[nm]), "c0", 16)

    def p0_g(self, g):
        em, t, io = self.em, self.t, self.io
        em.i(g.dma_start(t["sgate"][:, :].rearrange("p (c e) -> p c e", c=16),
                         io["gate_T"].rearrange("(c p) e -> p c e", p=128)),
             "cg0", 16, label="p0gate")

    def p0_v(self, v):
        em, t = self.em, self.t
        em.i(v.memset(t["s_cw"][:, 256:272], 1.0), "dve", 1, label="p0cw1")

    def alloc_p1(self, st):
        nc, t = self.nc, self.t
        e = st.enter_context
        t["PTB"] = e(nc.psum_tensor("PTB1", [128, 1024], BF16))
        t["sxf"] = [e(nc.sbuf_tensor(f"sxf{i}", [128, D], F32)) for i in range(2)]
        t["sxn"] = [e(nc.sbuf_tensor(f"sxn{i}", [128, D], BF16)) for i in range(2)]
        t["swb"] = [e(nc.sbuf_tensor(f"swb{i}", [128, ND * 128], BF16)) for i in range(2)]
        t["swv"] = e(nc.sbuf_tensor("swv1", [128, ND * 512], BF16))
        t["stage"] = [e(nc.sbuf_tensor(f"stage{i}", [128, D], BF16)) for i in range(2)]
        t["vstage"] = [e(nc.sbuf_tensor(f"vstage{i}", [128, 512], BF16)) for i in range(2)]
        t["rs_t"] = e(nc.sbuf_tensor("rs_t", [128, 2], F32))
        t["iota128"] = e(nc.sbuf_tensor("iota128", [128, 128], F32))
        t["thr0"] = e(nc.sbuf_tensor("thr0", [128, 32], F32))
        t["thrq"] = [e(nc.sbuf_tensor(f"thrq{i}", [128, 32], F32)) for i in range(2)]
        t["obq"] = e(nc.sbuf_tensor("obq", [128, 1], F32))

        seq = []
        for oc in range(20):
            for rnd in range(1 if oc < 16 else 2):
                seq.append((oc, rnd))
        self.p1seq = seq

    def _xtile_src(self, r):
        io = self.io
        if r < 16:
            return io["x_own"][r * 128:(r + 1) * 128, :]
        return io["x_oth"][(r - 16) * 128:(r - 16 + 1) * 128, :]

    def p1_y(self, y):
        em, t, io = self.em, self.t, self.io
        em.join(y, "p0")
        for r in range(NT_ALL):
            if r >= 2:
                em.w(y, "act", ("p1xn", r - 2))
            em.i(y.dma_start(t["sxf"][r % 2][:, :], self._xtile_src(r)),
                 f"x{r % 2}", 16, label=("p1x", r))
        for si, (oc, rnd) in enumerate(self.p1seq):
            em.w(y, "dve", ("p1pjev", si))
            st = t["stage"][si % 2]
            if oc < 16:
                for h, srcband in ((2 * oc, 0), (2 * oc + 1, 1)):
                    m, rem = divmod(h, 8)
                    band = rem // 4
                    rr = rem % 4
                    c_t = 4 * m + rr
                    em.i(y.dma_start(io["qa_sp"][c_t, band * 64:(band + 1) * 64, :],
                                     st[srcband * 64:(srcband + 1) * 64, :]),
                         f"sp{si % 2}", 16,
                         label=("p1sp", si) if h % 2 == 1 else None)
            else:
                em.i(y.dma_start(io["ka_sp"][oc - 16, :, rnd * 2048:(rnd + 1) * 2048],
                                 st[:, :]), f"sp{si % 2}", 16, label=("p1sp", si))
        for r in range(NT_ALL):
            em.w(y, "dve", ("p1vev", r))
            em.i(y.dma_start(io["va_sp"][r * 128:(r + 1) * 128, :], t["vstage"][r % 2][:, :]),
                 f"vs{r % 2}", 16, label=("p1vsp", r))

    def p1_s(self, s):
        em, t = self.em, self.t
        em.join(s, "p0")
        for r in range(NT_ALL):
            em.w(s, "dve", ("p1rs", r))
            inst = s.mul(t["negmean_s"][:, r:r + 1], t["rs_t"][:, r % 2:r % 2 + 1], -1.0 / D)
            em.i(inst, "act", 1, label=("p1nm", r))
            em.w(s, "act", ("p1nm", r))
            em.selfsem(s, s.activation(t["stage"][0][:, :], t["sxf"][r % 2][:, :],
                                       AF.Square,
                                       bias=t["negmean_s"][:, r:r + 1], scale=1.0,
                                       accum_out=t["vsum_s"][:, r:r + 1]), "ssf")
            em.selfsem(s, s.activation(t["stat8"][:, 0:1], t["vsum_s"][:, r:r + 1], AF.Ln,
                                       bias=EPS_LN, scale=1.0 / D), "ssf")
            em.selfsem(s, s.activation(t["rstd_s"][:, r:r + 1], t["stat8"][:, 0:1], AF.Exp,
                                       bias=0.0, scale=-0.5), "ssf")
            em.selfsem(s, s.mul(t["bias2_s"][:, r:r + 1], t["negmean_s"][:, r:r + 1],
                                t["rstd_s"][:, r:r + 1]), "ssf")
            em.wopt(s, "pe", ("p1tB", r - 2))
            em.i(s.activation(t["sxn"][r % 2][:, :], t["sxf"][r % 2][:, :], AF.Identity,
                              bias=t["bias2_s"][:, r:r + 1], scale=t["rstd_s"][:, r:r + 1]),
                 "act", 1, label=("p1xn", r))

    def p1_v(self, v):
        em, t = self.em, self.t
        em.join(v, "p0")
        for qt in range(NT_OWN):
            bi = qt % 2
            em.w(v, "pool", ("p1thr", qt))
            em.wopt(v, f"mks{bi}", ("p1mks", qt - 2))
            mb = t["swv"][:, bi * SEQ:bi * SEQ + SEQ]
            in0 = bass.AP(t["iota128"], 0, [[128, 128], [0, 32], [1, 128]])
            in1 = bass.AP(t["thrq"][bi], 0, [[32, 128], [1, 32], [0, 128]])
            em.i(v.tensor_tensor(mb[:, :].rearrange("p (c q) -> p c q", c=32),
                                 in0, in1, ALU.is_ge),
                 "dve", 1, label=("p1mtt", qt))
        xn_fm = t["xn_fm"]
        for r in range(NT_ALL):
            em.w(v, f"x{r % 2}", ("p1x", r))
            em.wopt(v, "act", ("p1nm", r - 2))
            em.i(v.reduce_sum(t["rs_t"][:, r % 2:r % 2 + 1], t["sxf"][r % 2][:, :], axis=AX.X),
                 "dve", 1, label=("p1rs", r))
            em.w(v, "pe", ("p1tA", r))
            em.i(v.tensor_copy(
                xn_fm[:, :].rearrange("p (c k) -> p c k", c=ND)[:, 0:8, r * 128:(r + 1) * 128],
                t["PTB"][:, :].rearrange("p (a b) -> p a b", a=8)),
                "dve", 1, label=("p1eA", r))
            em.w(v, "pe", ("p1tB", r))
            em.i(v.tensor_copy(
                xn_fm[:, :].rearrange("p (c k) -> p c k", c=ND)[:, 8:16, r * 128:(r + 1) * 128],
                t["PTB"][:, :].rearrange("p (a b) -> p a b", a=8)),
                "dve", 1, label=("p1eB", r))
        PS = t["PS"]
        for si, (oc, rnd) in enumerate(self.p1seq):
            em.w(v, "pe", ("p1pj", si))
            em.wopt(v, f"sp{si % 2}", ("p1sp", si - 2))
            st = t["stage"][si % 2]
            for gi in range(4):
                inst = v.tensor_scalar_add(st[:, gi * 512:(gi + 1) * 512],
                                           PS[:, gi * 512:(gi + 1) * 512],
                                           t["bqkv_col"][:, oc:oc + 1])
            em.i(inst, "dve", 1, label=("p1pjev", si))
        for r in range(NT_ALL):
            em.w(v, "pe", ("p1v", r))
            em.wopt(v, f"vs{r % 2}", ("p1vsp", r - 2))
            em.i(v.tensor_copy(t["vstage"][r % 2][:, :], PS[:, (r % 4) * 512:(r % 4 + 1) * 512]),
                 "dve", 1, label=("p1vev", r))

    def p1_t(self, tt):
        em, t = self.em, self.t
        em.join(tt, "p0")
        for r in range(NT_ALL):
            em.w(tt, "act", ("p1xn", r))
            em.wopt(tt, "dve", ("p1eB", r - 1))
            for k in range(8):
                inst = tt.transpose(t["PTB"][:, k * 128:(k + 1) * 128],
                                    t["sxn"][r % 2][:, k * 128:(k + 1) * 128],
                                    t["sident"][:, :])
            em.i(inst, "pe", 1, label=("p1tA", r))
            em.w(tt, "dve", ("p1eA", r))
            for k in range(8):
                inst = tt.transpose(t["PTB"][:, k * 128:(k + 1) * 128],
                                    t["sxn"][r % 2][:, (8 + k) * 128:(9 + k) * 128],
                                    t["sident"][:, :])
            em.i(inst, "pe", 1, label=("p1tB", r))
        PS = t["PS"]
        xn_fm = t["xn_fm"]
        for si, (oc, rnd) in enumerate(self.p1seq):
            em.w(tt, f"w{oc % 2}", ("p1w", oc))
            if si == 0:
                em.w(tt, "dve", ("p1eB", NT_ALL - 1))
            em.wopt(tt, "dve", ("p1pjev", si - 1))
            for dc in range(ND):
                for gi in range(4):
                    col = dc * SEQ + rnd * 2048 + gi * 512
                    inst = tt.matmul(PS[:, gi * 512:(gi + 1) * 512],
                                     t["swb"][oc % 2][:, dc * 128:(dc + 1) * 128],
                                     xn_fm[:, col:col + 512],
                                     start=(dc == 0), stop=(dc == ND - 1))
            em.i(inst, "pe", 1, label=("p1pj", si))
            if rnd == (0 if oc < 16 else 1):
                em.mark("pe", ("p1pjoc", oc))
        em.w(tt, "wv", ("p1wv", ND - 1))
        em.w(tt, "dve", ("p1pjev", len(self.p1seq) - 1))
        for r in range(NT_ALL):
            em.wopt(tt, "dve", ("p1vev", r - 4))
            bank = r % 4
            inst = tt.matmul(PS[:, bank * 512:(bank + 1) * 512], t["sones_f"][:, :],
                             t["bv_row"][:, :], start=True, stop=False)
            for dc in range(ND):
                inst = tt.matmul(PS[:, bank * 512:(bank + 1) * 512],
                                 xn_fm[:, dc * SEQ + r * 128: dc * SEQ + (r + 1) * 128],
                                 t["swv"][:, dc * 512:(dc + 1) * 512],
                                 start=False, stop=(dc == ND - 1))
            em.i(inst, "pe", 1, label=("p1v", r))

    def p1_g(self, g):
        em, t, io = self.em, self.t, self.io
        em.join(g, "p0")
        em.selfsem(g, g.iota(t["iota128"][:, :], [[1, 128]], channel_multiplier=0,
                             allow_small_or_imprecise_dtypes=True), "psf")
        em.selfsem(g, g.iota(t["thr0"][:, :], [[128, 32]], channel_multiplier=1,
                             allow_small_or_imprecise_dtypes=True), "psf")
        em.selfsem(g, g.tensor_sub(t["obq"][:, :], t["sb_othbase"][:, :],
                                   t["sb_qbase"][:, :]), "psf")
        em.mark("pool", "p1iotas")
        for qt in range(NT_OWN):
            bi = qt % 2
            thrq = t["thrq"][bi]
            em.wopt(g, "dve", ("p1mtt", qt - 2))
            em.selfsem(g, g.tensor_scalar_add(thrq[:, 0:16], t["thr0"][:, 0:16],
                                              float(-128 * qt)), "psf")
            em.selfsem(g, g.tensor_scalar_add(thrq[:, 16:32], t["thr0"][:, 0:16],
                                              float(2048 - 128 * qt)), "psf")
            em.selfsem(g, g.tensor_scalar_add(thrq[:, 16:32], thrq[:, 16:32],
                                              t["obq"][:, :]), "psf")
            em.mark("pool", ("p1thr", qt))
            em.i(g.engine_nop() if False else g.tensor_copy(t["obq"][:, :], t["obq"][:, :]),
                 "pool", 1, label=("p1thr", qt))
            em.w(g, "dve", ("p1mtt", qt))
            mb = t["swv"][:, bi * SEQ:bi * SEQ + SEQ]
            em.i(g.dma_start(io["mask_sp"][qt, :, :], mb),
                 f"mks{qt % 2}", 16, label=("p1mks", qt))
        for oc in range(20):
            if oc >= 2:
                em.w(g, "pe", ("p1pjoc", oc - 2))
            em.i(g.dma_start(
                t["swb"][oc % 2][:, :].rearrange("p (c o) -> p c o", c=ND),
                io["Wqkv"][:, oc * 128:(oc + 1) * 128].rearrange("(c p) o -> p c o", p=128)),
                f"w{oc % 2}", 16, label=("p1w", oc))
        em.w(g, "mks0", ("p1mks", NT_OWN - 2))
        em.w(g, "mks1", ("p1mks", NT_OWN - 1))
        for dc in range(ND):
            em.i(g.dma_start(t["swv"][:, dc * 512:(dc + 1) * 512],
                             io["Wqkv"][dc * 128:(dc + 1) * 128, 2560:3072]),
                 "wv", 16, label=("p1wv", dc))

    def alloc_p2(self, st):
        nc, t = self.nc, self.t
        e = st.enter_context
        t["PTB"] = e(nc.psum_tensor("PTB2", [128, 1024], BF16))
        t["r1b"] = e(nc.sbuf_tensor("r1b", [128, 32], BF16))
        t["r1T"] = e(nc.sbuf_tensor("r1T", [32, 128], BF16))
        t["r1row"] = e(nc.sbuf_tensor("r1row", [1, SEQ], BF16))
        t["st2"] = e(nc.sbuf_tensor("st2", [128, 32], F32))

    def p2_v(self, v):
        em, t = self.em, self.t
        em.join(v, "p1")
        em.selfsem(v, v.tensor_mul(t["st2"][:, :], t["rstd_s"][:, :], t["rstd_s"][:, :]),
                   "vsf")
        em.i(v.tensor_mul(t["st2"][:, :], t["st2"][:, :], t["vsum_s"][:, :]),
             "dve", 1, label="p2t2")
        em.w(v, "act", "p2r1")
        em.i(v.tensor_copy(t["r1b"][:, :], t["r1_s"][:, :]), "dve", 1, label="p2r1b")
        em.w(v, "pe", "p2tr")
        em.i(v.tensor_copy(t["r1T"][0:32, :], t["PTB"][0:32, 0:128]),
             "dve", 1, label="p2trev")
        xn_fm = t["xn_fm"]
        for dc in range(ND):
            inst = v.tensor_scalar_mul(xn_fm[:, dc * SEQ:(dc + 1) * SEQ],
                                       xn_fm[:, dc * SEQ:(dc + 1) * SEQ],
                                       t["ln1_col"][:, dc:dc + 1])
            em.i(inst, "dve", 1, label=("p2ln1", dc))
        em.w(v, "dve", ("p2ln1", ND - 1))
        for r in range(NT_ALL):
            em.w(v, "pe", ("p2bc", r))
            for dc in range(ND):
                inst = v.tensor_mul(xn_fm[:, dc * SEQ + r * 128: dc * SEQ + (r + 1) * 128],
                                    xn_fm[:, dc * SEQ + r * 128: dc * SEQ + (r + 1) * 128],
                                    t["PS"][:, (r % 2) * 512:(r % 2) * 512 + 128])
            em.i(inst, "dve", 1, label=("p2h1", r))

    def p2_s(self, s):
        em, t = self.em, self.t
        em.join(s, "p1")
        em.w(s, "dve", "p2t2")
        em.selfsem(s, s.activation(t["st2"][:, :], t["st2"][:, :], AF.Ln,
                                   bias=EPS_RMS, scale=1.0 / D), "ssf")
        em.i(s.activation(t["r1_s"][:, :], t["st2"][:, :], AF.Exp, bias=0.0, scale=-0.5),
             "act", 1, label="p2r1")

    def p2_t(self, tt):
        em, t = self.em, self.t
        em.join(tt, "p1")
        em.w(tt, "dve", "p2r1b")
        em.i(tt.transpose(t["PTB"][0:32, 0:128], t["r1b"][:, :], t["sident"][:, :]),
             "pe", 1, label="p2tr")
        for r in range(NT_ALL):
            em.w(tt, "bn", "p2r1row")
            em.w(tt, "dve", ("p2ln1", ND - 1))
            em.wopt(tt, "dve", ("p2h1", r - 2))
            em.i(tt.matmul(t["PS"][:, (r % 2) * 512:(r % 2) * 512 + 128],
                           t["sones_bf"][:, :], t["r1row"][0:1, r * 128:(r + 1) * 128],
                           start=True, stop=True),
                 "pe", 1, label=("p2bc", r))

    def p2_y(self, y):
        em, t, io = self.em, self.t, self.io
        em.join(y, "p1")
        em.w(y, "dve", "p2trev")
        em.i(y.dma_start(io["r1_bounce"], t["r1T"][0:32, :]), "bn", 16, label="p2bsp")
        em.w(y, "bn", "p2bsp")
        em.i(y.dma_start(t["r1row"][0:1, :],
                         io["r1_bounce"].rearrange("a b -> (a b)")),
             "bn", 16, label="p2r1row")

    def alloc_p3(self, st):
        nc, t = self.nc, self.t
        e = st.enter_context
        t["swb"] = [e(nc.sbuf_tensor(f"p3swb{i}", [128, ND * 128], BF16)) for i in range(2)]
        t["swv"] = e(nc.sbuf_tensor("p3swv", [128, ND * 512], BF16))
        t["stage"] = [e(nc.sbuf_tensor(f"p3stage{i}", [128, D], BF16)) for i in range(2)]
        t["vstage"] = [e(nc.sbuf_tensor(f"p3vstage{i}", [128, 512], BF16)) for i in range(2)]
        seq = []
        for k in range(32):
            for rnd in range(1 if k < 16 else 2):
                seq.append((k, rnd))
        self.p3seq = seq

    def p3_g(self, g):
        em, t, io = self.em, self.t, self.io
        em.join(g, "p2")
        for k in range(32):
            wname = "wq" if k < 16 else "wk"
            oc = k % 16
            if k >= 2:
                em.w(g, "pe", ("p3pjoc", k - 2))
            em.i(g.dma_start(
                t["swb"][k % 2][:, :].rearrange("p (c o) -> p c o", c=ND),
                io[wname][:, oc * 128:(oc + 1) * 128].rearrange("(c p) o -> p c o", p=128)),
                f"w{k % 2}", 16, label=("p3w", k))
        for og in range(4):
            em.w(g, "pe", ("p3vwfree", og))
            for dc in range(ND):
                em.i(g.dma_start(t["swv"][:, dc * 512:(dc + 1) * 512],
                                 io["wv"][dc * 128:(dc + 1) * 128, og * 512:(og + 1) * 512]),
                     "wv", 16, label=("p3wv", og, dc))

    def p3_t(self, tt):
        em, t = self.em, self.t
        em.join(tt, "p2")
        PS = t["PS"]
        xn_fm = t["xn_fm"]
        for si, (k, rnd) in enumerate(self.p3seq):
            em.w(tt, f"w{k % 2}", ("p3w", k))
            em.wopt(tt, "dve", ("p3pjev", si - 1))
            for dc in range(ND):
                for gi in range(4):
                    col = dc * SEQ + rnd * 2048 + gi * 512
                    inst = tt.matmul(PS[:, gi * 512:(gi + 1) * 512],
                                     t["swb"][k % 2][:, dc * 128:(dc + 1) * 128],
                                     xn_fm[:, col:col + 512],
                                     start=(dc == 0), stop=(dc == ND - 1))
            em.i(inst, "pe", 1, label=("p3pj", si))
            if rnd == (0 if k < 16 else 1):
                em.mark("pe", ("p3pjoc", k))
        em.w(tt, "dve", ("p3pjev", len(self.p3seq) - 1))
        for og in range(4):
            em.mark("pe", ("p3vwfree", og))
            em.w(tt, "wv", ("p3wv", og, ND - 1))
            for r in range(NT_ALL):
                gr = og * NT_ALL + r
                em.wopt(tt, "dve", ("p3vev", gr - 4))
                bank = r % 4
                for dc in range(ND):
                    inst = tt.matmul(PS[:, bank * 512:(bank + 1) * 512],
                                     xn_fm[:, dc * SEQ + r * 128: dc * SEQ + (r + 1) * 128],
                                     t["swv"][:, dc * 512:(dc + 1) * 512],
                                     start=(dc == 0), stop=(dc == ND - 1))
                em.i(inst, "pe", 1, label=("p3v", gr))

    def p3_v(self, v):
        em, t = self.em, self.t
        em.join(v, "p2")
        PS = t["PS"]
        for si, (k, rnd) in enumerate(self.p3seq):
            em.w(v, "pe", ("p3pj", si))
            em.wopt(v, f"sp{si % 2}", ("p3sp", si - 2))
            st = t["stage"][si % 2]
            for gi in range(4):
                inst = v.tensor_copy(st[:, gi * 512:(gi + 1) * 512],
                                     PS[:, gi * 512:(gi + 1) * 512])
            em.i(inst, "dve", 1, label=("p3pjev", si))
        for gr in range(4 * NT_ALL):
            em.w(v, "pe", ("p3v", gr))
            em.wopt(v, f"vs{gr % 2}", ("p3vsp", gr - 2))
            em.i(v.tensor_copy(t["vstage"][gr % 2][:, :],
                               PS[:, (gr % 4) * 512:(gr % 4 + 1) * 512]),
                 "dve", 1, label=("p3vev", gr))

    def p3_y(self, y):
        em, t, io = self.em, self.t, self.io
        em.join(y, "p2")
        for si, (k, rnd) in enumerate(self.p3seq):
            em.w(y, "dve", ("p3pjev", si))
            st = t["stage"][si % 2]
            if k < 16:
                em.i(y.dma_start(io["qf_sp"][k, :, :], st[:, :]), f"sp{si % 2}", 16,
                     label=("p3sp", si))
            else:
                em.i(y.dma_start(io["kf_sp"][k - 16, :, rnd * 2048:(rnd + 1) * 2048],
                                 st[:, :]), f"sp{si % 2}", 16, label=("p3sp", si))
        for gr in range(4 * NT_ALL):
            og, r = divmod(gr, NT_ALL)
            em.w(y, "dve", ("p3vev", gr))
            em.i(y.dma_start(io["vf_sp"][r * 128:(r + 1) * 128, og * 512:(og + 1) * 512],
                             t["vstage"][gr % 2][:, :]), f"vs{gr % 2}", 16,
                 label=("p3vsp", gr))
        if self.dev:
            em.w(y, "dve", ("p2h1", NT_ALL - 1))
            em.i(y.dma_start(io["h1_tap"], t["xn_fm"][:, :]), "tap", 16)


    def _attn_unit_t_pass1(self, tt, u, cfg):
        em, t = self.em, self.t
        PS = t["PS"]
        nm = cfg["nm"]
        uid = u["uid"]
        if cfg.get("pre_sa") is not None:
            cfg["pre_sa"](tt, uid)
        for ks in range(8):
            b = ks % 4
            prev = (nm + "bm", uid, ks - 4) if ks >= 4 else (nm + "bm", uid - 1, ks + 4)
            em.wopt(tt, "dve", prev)
            inst = tt.matmul(PS[:, b * 512:(b + 1) * 512], u["lhsT"], u["krhs"](ks),
                             start=True, stop=True)
            em.i(inst, "pe", 1, label=(nm + "s1", uid, ks))

    def _attn_unit_t_main(self, tt, u, cfg):
        em, t = self.em, self.t
        PS = t["PS"]
        PSB = t["PSB"]
        nm = cfg["nm"]
        hd = cfg["hd"]
        avw = cfg["avw"]
        uid = u["uid"]
        par = uid % 2
        em.w(tt, "dve", (nm + "ng", uid))
        em.wopt(tt, "dve", (nm + "ngr", uid - 1))
        if cfg.get("nt_wait") is not None:
            cfg["nt_wait"](tt, uid)
        em.i(tt.transpose(PSB[0:1, par * 128:(par + 1) * 128], u["ngb"],
                          t["sident"][:, :]),
             "pe", 1, label=(nm + "nt", uid))
        for g in range(8):
            b2 = 2048 + (g % 2) * 512
            em.wopt(tt, "act", (nm + "ea", uid, g - 2))
            if g == 0:
                em.w(tt, "dve", (nm + "ngr", uid))
            for k4 in range(4):
                kt = 4 * g + k4
                dst = PS[:, b2 + k4 * 128: b2 + (k4 + 1) * 128]
                tt.matmul(dst, u["kt128"](kt), u["qrhs"], start=True, stop=False)
                inst = tt.matmul(dst, t["sones_bf"][:, :],
                                 u["ngrow"], start=False, stop=True)
            em.i(inst, "pe", 1, label=(nm + "s2", uid, g))
        avps = PS[:, 3072:3072 + avw]
        for g in range(8):
            if cfg["mask"]:
                em.w(tt, "dve", (nm + "pa", uid, g // 4))
            else:
                em.w(tt, "act", (nm + "ea", uid, g))
            if g == 0:
                em.wopt(tt, "dve", (nm + "oev", uid - 1))
            for k4 in range(4):
                kt = 4 * g + k4
                inst = tt.matmul(avps, u["ptbuf"][:, kt * 128:(kt + 1) * 128],
                                 u["vrhs"](kt), start=(kt == 0), stop=(kt == 31))
            if g == 7:
                em.i(inst, "pe", 1, label=(nm + "av", uid))

    def _attn_unit_v(self, v, u, cfg):
        em, t = self.em, self.t
        PS = t["PS"]
        PSB = t["PSB"]
        nm = cfg["nm"]
        hd = cfg["hd"]
        avw = cfg["avw"]
        uid = u["uid"]
        par = uid % 2
        sb = u["sb"]
        for ks in range(8):
            em.w(v, "pe", (nm + "s1", uid, ks))
            inst = v.reduce_max(sb["mad"][:, ks:ks + 1],
                                PS[:, (ks % 4) * 512:(ks % 4 + 1) * 512], axis=AX.X)
            em.i(inst, "dve", 1, label=(nm + "bm", uid, ks))
        em.w(v, "dve", (nm + "bm", uid, 7))
        em.selfsem(v, v.reduce_max(sb["m"], sb["mad"][:, 0:8], axis=AX.X), "vsf")
        em.i(v.tensor_scalar_mul(u["ngb"], sb["m"], -1.0), "dve", 1,
             label=(nm + "ng", uid))

    def _attn_unit_v_ngr(self, v, u, cfg):
        em, t = self.em, self.t
        PSB = t["PSB"]
        nm = cfg["nm"]
        uid = u["uid"]
        par = uid % 2
        em.w(v, "pe", (nm + "nt", uid))
        em.i(v.tensor_copy(u["ngrow"], PSB[0:1, par * 128:(par + 1) * 128]),
             "dve", 1, label=(nm + "ngr", uid))

    def _attn_unit_v_post(self, v, u, cfg):
        em, t = self.em, self.t
        PS = t["PS"]
        nm = cfg["nm"]
        hd = cfg["hd"]
        avw = cfg["avw"]
        uid = u["uid"]
        sb = u["sb"]
        if cfg["mask"]:
            for hh in range(2):
                em.w(v, "act", (nm + "ea", uid, 4 * hh + 3))
                em.w(v, cfg["mkld_sem"](uid), (nm + "mk", uid))
                em.i(v.tensor_mul(u["ptbuf"][:, hh * 2048:(hh + 1) * 2048],
                                  u["ptbuf"][:, hh * 2048:(hh + 1) * 2048],
                                  u["mask"][:, hh * 2048:(hh + 1) * 2048]),
                     "dve", 1, label=(nm + "pa", uid, hh))
        em.w(v, "pe", (nm + "av", uid))
        em.selfsem(v, v.reciprocal(sb["il"], PS[:, 3072 + hd:3072 + hd + 1]), "vsf")
        if cfg.get("oev_wait") is not None:
            cfg["oev_wait"](v, uid)
        em.i(v.tensor_scalar_mul(u["odst"], PS[:, 3072:3072 + hd], sb["il"]),
             "dve", 1, label=(nm + "oev", uid))

    def _attn_unit_s(self, s, u, cfg):
        em, t = self.em, self.t
        PS = t["PS"]
        nm = cfg["nm"]
        sc = cfg["scale"]
        uid = u["uid"]
        for g in range(8):
            em.w(s, "pe", (nm + "s2", uid, g))
            if g == 0:
                em.wopt(s, "pe", (nm + "av", uid - 2))
            b2 = 2048 + (g % 2) * 512
            em.i(s.activation(u["ptbuf"][:, g * 512:(g + 1) * 512],
                              PS[:, b2:b2 + 512], AF.Exp, bias=0.0, scale=sc),
                 "act", 1, label=(nm + "ea", uid, g))

    def alloc_p4(self, st):
        nc, t = self.nc, self.t
        e = st.enter_context
        t["PSB"] = e(nc.psum_tensor("PSB4", [128, 512], BF16))
        t["ska"] = e(nc.sbuf_tensor("ska", [128, 4 * SEQ], BF16))
        t["sva"] = e(nc.sbuf_tensor("sva", [128, 32 * 520], BF16))
        t["swo"] = e(nc.sbuf_tensor("swo", [128, ND * D], BF16))
        t["sq"] = [e(nc.sbuf_tensor(f"sq{i}", [128, ND * 128], BF16)) for i in range(2)]
        t["ptbuf"] = [e(nc.sbuf_tensor(f"ptbuf{i}", [128, SEQ], BF16)) for i in range(2)]
        t["otm"] = e(nc.sbuf_tensor("otm", [128, D], BF16))
        t["ofmt"] = e(nc.sbuf_tensor("ofmt", [128, ND * 128], BF16))
        t["aostage"] = [e(nc.sbuf_tensor("aostage0", [128, D], F32))] * 2
        t["ast"] = [e(nc.sbuf_tensor(f"ast{i}", [128, 16], F32)) for i in range(2)]
        t["ngb4"] = e(nc.sbuf_tensor("ngb4", [128, 2], BF16))
        t["ngrow4"] = e(nc.sbuf_tensor("ngrow4", [1, 256], BF16))

    def _p4_cfg(self):
        em = self.em

        def pre_sa(tt, uid):
            if uid % NH_A == 1:
                for og in range(4):
                    em.wopt(tt, "dve", ("p4aoev", uid // NH_A - 1, og))

        def oev_wait(v, uid):
            em.wopt(v, "pe", ("p4oft", uid // NH_A - 1, 3))

        def nt_wait(tt, uid):
            if uid % NH_A == 0:
                em.wopt(tt, "dve", ("p4oftev", uid // NH_A - 1, 3))

        return dict(nm="A4", hd=HD_A, avw=HD_A + 1, scale=SC_A, mask=False,
                    pre_sa=pre_sa, oev_wait=oev_wait, mkld_sem=None, nt_wait=nt_wait)

    def _p4_unit(self, qt, h):
        t = self.t
        uid = qt * NH_A + h
        par = uid % 2
        b = (h // 4) % 2
        c = 4 * (h // 8) + (h % 8) % 4
        jc = (h // 4) // 2
        j = h // 4
        sq = t["sq"][qt % 2]
        ska, sva = t["ska"], t["sva"]
        st = t["ast"][par]
        sb = dict(m=st[:, 8:9], il=st[:, 9:10], mad=st[:, 0:8])
        return dict(
            uid=uid, par=par,
            lhsT=sq[b * 64:(b + 1) * 64, c * 128:(c + 1) * 128],
            qrhs=sq[b * 64:(b + 1) * 64, c * 128:(c + 1) * 128],
            krhs=lambda ks: ska[b * 64:(b + 1) * 64,
                                jc * SEQ + ks * 512: jc * SEQ + (ks + 1) * 512],
            kt128=lambda kt: ska[b * 64:(b + 1) * 64,
                                 jc * SEQ + kt * 128: jc * SEQ + (kt + 1) * 128],
            vrhs=lambda kt: sva[:, kt * 520 + j * 65: kt * 520 + (j + 1) * 65],
            ngb=t["ngb4"][:, par:par + 1],
            ngrow=t["ngrow4"][0:1, par * 128:(par + 1) * 128],
            ptbuf=t["ptbuf"][par],
            odst=t["otm"][:, h * 64:(h + 1) * 64],
            sb=sb, mask=None)

    def p4_y(self, y):
        em, t, io = self.em, self.t, self.io
        em.join(y, "p3")
        em.i(y.dma_start(t["ska"][:, :].rearrange("p (c k) -> p c k", c=4),
                         io["ka_sp"].rearrange("c p k -> p c k")), "c4", 16)
        for j in range(8):
            em.i(y.dma_start(
                bass.AP(t["sva"], j * 65, [[32 * 520, 128], [520, 32], [1, 64]]),
                io["va_sp"].rearrange("(c p) (j i) -> p c j i", p=128, j=8)[:, :, j, :]),
                "c4", 16)
        em.mark("c4", "p4kv")
        for qt in range(NT_OWN):
            if qt >= 2:
                em.w(y, "pe", ("A4s1", (qt - 1) * NH_A, 0))
            em.i(y.dma_start(t["sq"][qt % 2][:, :].rearrange("p (c q) -> p c q", c=ND),
                             io["qa_sp"][:, :, qt * 128:(qt + 1) * 128]
                             .rearrange("c p q -> p c q")),
                 f"q{qt % 2}", 16, label=("p4q", qt))
            if qt >= 1:
                for og in range(4):
                    em.w(y, "dve", ("p4aoev", qt - 1, og))
                em.i(y.dma_start(io["ao_sp"][(qt - 1) * 128:qt * 128, :],
                                 t["aostage"][(qt - 1) % 2][:, :]), "ao4", 16,
                     label=("p4aosp", qt - 1))
        for og in range(4):
            em.w(y, "dve", ("p4aoev", NT_OWN - 1, og))
        em.i(y.dma_start(io["ao_sp"][(NT_OWN - 1) * 128:NT_OWN * 128, :],
                         t["aostage"][(NT_OWN - 1) % 2][:, :]), "ao4", 16,
             label=("p4aosp", NT_OWN - 1))

    def p4_g(self, g):
        em, t, io = self.em, self.t, self.io
        em.join(g, "p3")
        for dc in range(ND):
            em.i(g.dma_start(t["swo"][:, dc * D:(dc + 1) * D],
                             io["Wo"][dc * 128:(dc + 1) * 128, :]),
                 "wo4", 16, label=("p4wo", dc))
        em.w(g, "c4", "p4kv")
        em.i(g.memset(bass.AP(t["sva"], 64, [[32 * 520, 128], [520, 32], [65, 8], [1, 1]]),
                      1.0), "pool", 1, label="p4ones")

    def p4_t(self, tt):
        em, t = self.em, self.t
        cfg = self._p4_cfg()
        em.join(tt, "p3")
        em.w(tt, "c4", "p4kv")
        em.w(tt, "pool", "p4ones")
        em.w(tt, "wo4", ("p4wo", ND - 1))
        PS = t["PS"]
        em.w(tt, f"q{0}", ("p4q", 0))
        self._attn_unit_t_pass1(tt, self._p4_unit(0, 0), cfg)
        for qt in range(NT_OWN):
            for h in range(NH_A):
                uid = qt * NH_A + h
                nqt, nh = divmod(uid + 1, NH_A)
                if nqt < NT_OWN:
                    if nh == 0:
                        em.w(tt, f"q{nqt % 2}", ("p4q", nqt))
                    self._attn_unit_t_pass1(tt, self._p4_unit(nqt, nh), cfg)
                self._attn_unit_t_main(tt, self._p4_unit(qt, h), cfg)
            em.w(tt, "dve", ("A4oev", qt * NH_A + NH_A - 1))
            em.wopt(tt, "dve", ("A4ngr", qt * NH_A + NH_A - 1))
            for rnd in range(4):
                if rnd > 0:
                    em.w(tt, "dve", ("p4oftev", qt, rnd - 1))
                elif qt > 0:
                    em.w(tt, "dve", ("p4oftev", qt - 1, 3))
                for k in range(4):
                    inst = tt.transpose(
                        t["PSB"][:, k * 128:(k + 1) * 128],
                        t["otm"][:, (rnd * 4 + k) * 128:(rnd * 4 + k + 1) * 128],
                        t["sident"][:, :])
                em.i(inst, "pe", 1, label=("p4oft", qt, rnd))
            em.w(tt, "dve", ("p4oftev", qt, 3))
            for og in range(4):
                em.wopt(tt, "dve", ("p4aoev", qt - 1, og))
                em.wopt(tt, "dve", ("A4bm", qt * NH_A + NH_A - 1, og))
                em.wopt(tt, "dve", ("A4bm", qt * NH_A + NH_A - 1, og + 4))
                em.wopt(tt, "dve", ("A4bm", (qt + 1) * NH_A, og))
                em.wopt(tt, "dve", ("A4bm", (qt + 1) * NH_A, og + 4))
                inst = tt.matmul(PS[:, og * 512:(og + 1) * 512], t["sones_f"][:, :],
                                 t["bo_row"][:, og * 512:(og + 1) * 512],
                                 start=True, stop=False)
                for dc in range(ND):
                    inst = tt.matmul(PS[:, og * 512:(og + 1) * 512],
                                     t["ofmt"][:, dc * 128:(dc + 1) * 128],
                                     t["swo"][:, dc * D + og * 512: dc * D + (og + 1) * 512],
                                     start=False, stop=(dc == ND - 1))
                em.i(inst, "pe", 1, label=("p4wom", qt, og))

    def p4_v(self, v):
        em, t = self.em, self.t
        cfg = self._p4_cfg()
        em.join(v, "p3")
        self._attn_unit_v(v, self._p4_unit(0, 0), cfg)
        for qt in range(NT_OWN):
            for h in range(NH_A):
                uid = qt * NH_A + h
                nqt, nh = divmod(uid + 1, NH_A)
                if nqt < NT_OWN:
                    self._attn_unit_v(v, self._p4_unit(nqt, nh), cfg)
                self._attn_unit_v_ngr(v, self._p4_unit(qt, h), cfg)
                self._attn_unit_v_post(v, self._p4_unit(qt, h), cfg)
            for rnd in range(4):
                em.w(v, "pe", ("p4oft", qt, rnd))
                em.i(v.tensor_copy(t["ofmt"][:, rnd * 512:(rnd + 1) * 512],
                                   t["PSB"][:, :]),
                     "dve", 1, label=("p4oftev", qt, rnd))
            em.wopt(v, "ao4", ("p4aosp", qt - 1))
            for og in range(4):
                em.w(v, "pe", ("p4wom", qt, og))
                em.i(v.tensor_copy(t["aostage"][qt % 2][:, og * 512:(og + 1) * 512],
                                   t["PS"][:, og * 512:(og + 1) * 512]),
                     "dve", 1, label=("p4aoev", qt, og))

    def p4_s(self, s):
        em = self.em
        cfg = self._p4_cfg()
        em.join(s, "p3")
        for qt in range(NT_OWN):
            for h in range(NH_A):
                self._attn_unit_s(s, self._p4_unit(qt, h), cfg)

    def alloc_p5(self, st):
        nc, t = self.nc, self.t
        e = st.enter_context
        t["PSB"] = e(nc.psum_tensor("PSB5", [128, 512], BF16))
        t["skf"] = [e(nc.sbuf_tensor(f"skf{i}", [128, SEQ], BF16)) for i in range(2)]
        t["svf"] = [e(nc.sbuf_tensor(f"svf{i}", [128, 32 * 129], BF16)) for i in range(2)]
        t["sqf"] = [e(nc.sbuf_tensor(f"sqf{i}", [128, OWN], BF16)) for i in range(2)]
        t["ptbuf"] = [e(nc.sbuf_tensor(f"p5ptbuf{i}", [128, SEQ], BF16)) for i in range(2)]
        t["mask_t"] = [e(nc.sbuf_tensor(f"mask{i}", [128, SEQ], BF16)) for i in range(2)]
        t["avtm"] = e(nc.sbuf_tensor("avtm", [128, 128], BF16))
        t["fst"] = [e(nc.sbuf_tensor(f"fst{i}", [128, 16], F32)) for i in range(2)]
        t["ngb5"] = e(nc.sbuf_tensor("ngb5", [128, 2], BF16))
        t["ngrow5"] = e(nc.sbuf_tensor("ngrow5", [1, 256], BF16))

    def _p5_cfg(self):
        em = self.em

        def oev_wait(v, uid):
            em.wopt(v, "pe", ("p5ot", uid - 1))

        def nt_wait(tt, uid):
            em.wopt(tt, "dve", ("p5otev", uid - 1))

        return dict(nm="F5", hd=HD_F, avw=HD_F + 1, scale=SC_F, mask=True,
                    pre_sa=None, oev_wait=oev_wait,
                    mkld_sem=lambda uid: f"mk{uid % 2}", nt_wait=nt_wait)

    def _p5_unit(self, h, qt):
        t = self.t
        uid = h * NT_OWN + qt
        par = uid % 2
        st = t["fst"][par]
        sb = dict(m=st[:, 8:9], il=st[:, 9:10], mad=st[:, 0:8])
        skf = t["skf"][h % 2]
        svf = t["svf"][h % 2]
        return dict(
            uid=uid, par=par,
            lhsT=t["sqf"][h % 2][:, qt * 128:(qt + 1) * 128],
            qrhs=t["sqf"][h % 2][:, qt * 128:(qt + 1) * 128],
            krhs=lambda ks: skf[:, ks * 512:(ks + 1) * 512],
            kt128=lambda kt: skf[:, kt * 128:(kt + 1) * 128],
            vrhs=lambda kt: svf[:, kt * 129:(kt + 1) * 129],
            ngb=t["ngb5"][:, par:par + 1],
            ngrow=t["ngrow5"][0:1, par * 128:(par + 1) * 128],
            ptbuf=t["ptbuf"][par],
            odst=t["avtm"][:, :],
            sb=sb, mask=t["mask_t"][par])

    def p5_y(self, y):
        em, t, io = self.em, self.t, self.io
        em.join(y, "p4")
        for h in range(NH_F):
            if h >= 2:
                em.w(y, "pe", ("F5s1", (h - 1) * NT_OWN, 0))
            em.i(y.dma_start(t["skf"][h % 2][:, :], io["kf_sp"][h, :, :]),
                 f"kv{h % 2}", 16)
            em.i(y.dma_start(
                t["svf"][h % 2][:, :].rearrange("p (c k) -> p c k", c=32)[:, :, 0:128],
                io["vf_sp"][:, h * 128:(h + 1) * 128]
                .rearrange("(c p) k -> p c k", p=128)),
                 f"kv{h % 2}", 16)
            em.i(y.dma_start(t["sqf"][h % 2][:, :], io["qf_sp"][h, :, :]),
                 f"kv{h % 2}", 16, label=("p5kv", h))
            for qt in range(NT_OWN):
                uid = h * NT_OWN + qt
                em.wopt(y, "dve", ("F5pa", uid - 2, 1))
                em.i(y.dma_start(t["mask_t"][uid % 2][:, :], io["mask_sp"][qt, :, :]),
                     f"mk{uid % 2}", 16, label=("F5mk", uid))
        if self.dev:
            em.w(y, "dve", ("p5otev", NH_F * NT_OWN - 1))
            em.i(y.dma_start(io["ofm_tap"], t["OFM"][:, :]), "tap", 16)

    def p5_g(self, p):
        em, t = self.em, self.t
        em.join(p, "p4")
        for i in range(2):
            em.i(p.memset(t["svf"][i][:, :].rearrange("p (c k) -> p c k", c=32)[:, :, 128:129],
                          1.0), "pool", 1, label=("p5ones", i))

    def p5_t(self, tt):
        em, t = self.em, self.t
        cfg = self._p5_cfg()
        em.join(tt, "p4")
        em.w(tt, "pool", ("p5ones", 1))
        em.w(tt, "kv0", ("p5kv", 0))
        self._attn_unit_t_pass1(tt, self._p5_unit(0, 0), cfg)
        for h in range(NH_F):
            for qt in range(NT_OWN):
                uid = h * NT_OWN + qt
                u = self._p5_unit(h, qt)
                nh, nqt = divmod(uid + 1, NT_OWN)
                if nh < NH_F:
                    if nqt == 0:
                        em.w(tt, f"kv{nh % 2}", ("p5kv", nh))
                    self._attn_unit_t_pass1(tt, self._p5_unit(nh, nqt), cfg)
                self._attn_unit_t_main(tt, u, cfg)
                em.w(tt, "dve", ("F5oev", uid))
                em.wopt(tt, "dve", ("p5otev", uid - 1))
                em.i(tt.transpose(t["PSB"][:, 256:384], t["avtm"][:, :],
                                  t["sident"][:, :]),
                     "pe", 1, label=("p5ot", uid))

    def p5_v(self, v):
        em, t = self.em, self.t
        cfg = self._p5_cfg()
        em.join(v, "p4")
        ofm = t["OFM"]
        self._attn_unit_v(v, self._p5_unit(0, 0), cfg)
        for h in range(NH_F):
            for qt in range(NT_OWN):
                uid = h * NT_OWN + qt
                u = self._p5_unit(h, qt)
                nh, nqt = divmod(uid + 1, NT_OWN)
                if nh < NH_F:
                    self._attn_unit_v(v, self._p5_unit(nh, nqt), cfg)
                self._attn_unit_v_ngr(v, u, cfg)
                self._attn_unit_v_post(v, u, cfg)
                em.w(v, "pe", ("p5ot", uid))
                em.i(v.tensor_copy(ofm[:, h * OWN + qt * 128: h * OWN + (qt + 1) * 128],
                                   t["PSB"][:, 256:384]),
                     "dve", 1, label=("p5otev", uid))

    def p5_s(self, s):
        em = self.em
        cfg = self._p5_cfg()
        em.join(s, "p4")
        for h in range(NH_F):
            for qt in range(NT_OWN):
                self._attn_unit_s(s, self._p5_unit(h, qt), cfg)

    def alloc_p6(self, st):
        nc, t = self.nc, self.t
        e = st.enter_context
        t["PTB"] = e(nc.psum_tensor("PTB6", [128, 1024], BF16))
        t["swo6"] = e(nc.sbuf_tensor("swo6", [128, ND * D], BF16))
        t["sxf6"] = [e(nc.sbuf_tensor(f"sxf6{i}", [128, D], F32)) for i in range(2)]
        t["xn6"] = [e(nc.sbuf_tensor(f"xn6{i}", [128, D], F32)) for i in range(2)]
        t["x26"] = [e(nc.sbuf_tensor(f"x26{i}", [128, D], F32)) for i in range(2)]
        t["tt6"] = e(nc.sbuf_tensor("tt6", [128, D], BF16))
        t["Lb"] = e(nc.sbuf_tensor("Lb", [128, 16], F32))
        t["curb"] = e(nc.sbuf_tensor("curb", [128, 16], F32))
        t["maskb"] = e(nc.sbuf_tensor("maskb", [128, 16], F32))
        t["Eb"] = e(nc.sbuf_tensor("Eb", [128, 16], F32))
        t["mjs"] = e(nc.sbuf_tensor("mjs", [128, 8], F32))
        t["g6"] = e(nc.sbuf_tensor("g6", [128, 8], F32))
        t["junk"] = e(nc.sbuf_tensor("junk6", [128, 2048], F32))

    def p6_g(self, g):
        em, t, io = self.em, self.t, self.io
        em.join(g, "p5")
        for dc in range(ND):
            em.i(g.dma_start(t["swo6"][:, dc * D:(dc + 1) * D],
                             io["wo"][dc * 128:(dc + 1) * 128, :]),
                 "wo6", 16, label=("p6wo", dc))

    def p6_y(self, y):
        em, t, io = self.em, self.t, self.io
        em.join(y, "p5")
        for qt in range(NT_OWN):
            if qt >= 2:
                em.w(y, "act", ("p6sq", qt - 2))
            em.i(y.dma_start(t["sxf6"][qt % 2][:, :], io["x_own"][qt * 128:(qt + 1) * 128, :]),
                 f"x{qt % 2}", 16, label=("p6x", qt))
            if qt >= 1:
                em.w(y, "dve", ("p6x2", qt - 1, 3))
                em.i(y.dma_start(io["x2_sp"][(qt - 1) * 128:qt * 128, :],
                                 t["x26"][(qt - 1) % 2][:, :]),
                     f"x2s{(qt - 1) % 2}", 16, label=("p6x2sp", qt - 1))
        em.w(y, "dve", ("p6x2", NT_OWN - 1, 3))
        em.i(y.dma_start(io["x2_sp"][(NT_OWN - 1) * 128:NT_OWN * 128, :],
                         t["x26"][(NT_OWN - 1) % 2][:, :]),
             f"x2s{(NT_OWN - 1) % 2}", 16, label=("p6x2sp", NT_OWN - 1))
        if self.dev:
            em.w(y, "dve", ("p6cw", NT_OWN - 1))
            em.i(y.dma_start(io["tfm_tap"], t["OFM"][:, :]), "tap", 16)
            em.i(y.dma_start(io["cw_tap"], t["s_cw"][:, :]), "tap", 16)

    def p6_t(self, tt):
        em, t = self.em, self.t
        em.join(tt, "p5")
        PS = t["PS"]
        buf = t["OFM"]
        em.w(tt, "wo6", ("p6wo", ND - 1))
        for qt in range(NT_OWN):
            for og in range(4):
                em.wopt(tt, "dve", ("p6x2", qt - 1, og))
                for dc in range(ND):
                    inst = tt.matmul(PS[:, og * 512:(og + 1) * 512],
                                     buf[:, dc * OWN + qt * 128: dc * OWN + (qt + 1) * 128],
                                     t["swo6"][:, dc * D + og * 512: dc * D + (og + 1) * 512],
                                     start=(dc == 0), stop=(dc == ND - 1))
                em.i(inst, "pe", 1, label=("p6wom", qt, og))
            em.w(tt, "act", ("p6tt", qt))
            for half in range(2):
                if half == 0:
                    em.wopt(tt, "dve", ("p6ttev", qt - 1, 1))
                else:
                    em.w(tt, "dve", ("p6ttev", qt, 0))
                for k in range(8):
                    inst = tt.transpose(
                        t["PTB"][:, k * 128:(k + 1) * 128],
                        t["tt6"][:, (half * 8 + k) * 128:(half * 8 + k + 1) * 128],
                        t["sident"][:, :])
                em.i(inst, "pe", 1, label=("p6ttr", qt, half))
            em.w(tt, "dve", ("p6ttev", qt, 1))
            em.wopt(tt, "dve", ("p6gev", qt - 1))
            for dc in range(ND):
                inst = tt.matmul(PS[:, 2048:2064],
                                 buf[:, dc * OWN + qt * 128: dc * OWN + (qt + 1) * 128],
                                 t["sgate"][:, dc * 16:(dc + 1) * 16],
                                 start=(dc == 0), stop=(dc == ND - 1))
            em.i(inst, "pe", 1, label=("p6gate", qt))

    def p6_s(self, s):
        em, t = self.em, self.t
        em.join(s, "p5")
        for qt in range(NT_OWN):
            em.w(s, f"x{qt % 2}", ("p6x", qt))
            em.wopt(s, "dve", ("p6x2", qt - 2, 3))
            em.i(s.activation(t["xn6"][qt % 2][:, :], t["sxf6"][qt % 2][:, :], AF.Identity,
                              bias=t["bias2_s"][:, qt:qt + 1], scale=t["rstd_s"][:, qt:qt + 1]),
                 "act", 1, label=("p6sq", qt))
            em.w(s, "dve", ("p6x2", qt, 3))
            em.selfsem(s, s.activation(t["junk"][:, :], t["x26"][qt % 2][:, :], AF.Square,
                                       bias=0.0, scale=1.0,
                                       accum_out=t["g6"][:, 0:1]), "ssf")
            em.selfsem(s, s.activation(t["g6"][:, 1:2], t["g6"][:, 0:1], AF.Ln,
                                       bias=EPS_RMS, scale=1.0 / D), "ssf")
            em.selfsem(s, s.activation(t["g6"][:, 2:3], t["g6"][:, 1:2], AF.Exp,
                                       bias=0.0, scale=-0.5), "ssf")
            em.wopt(s, "pe", ("p6ttr", qt - 1, 1))
            em.i(s.activation(t["tt6"][:, :], t["x26"][qt % 2][:, :], AF.Identity,
                              bias=0.0, scale=t["g6"][:, 2:3]),
                 "act", 1, label=("p6tt", qt))
            em.w(s, "dve", ("p6negm1", qt))
            em.i(s.activation(t["Eb"][:, :], t["Lb"][:, :], AF.Exp,
                              bias=t["g6"][:, 4 + qt % 2: 5 + qt % 2], scale=1.0),
                 "act", 1, label=("p6exp", qt))

    def p6_v(self, v):
        em, t = self.em, self.t
        em.join(v, "p5")
        PS = t["PS"]
        tfm = t["OFM"]
        for qt in range(NT_OWN):
            em.wopt(v, "act", ("p6tt", qt - 2))
            em.wopt(v, f"x2s{qt % 2}", ("p6x2sp", qt - 2))
            for og in range(4):
                em.w(v, "pe", ("p6wom", qt, og))
                em.w(v, "act", ("p6sq", qt))
                em.i(v.scalar_tensor_tensor(
                    t["x26"][qt % 2][:, og * 512:(og + 1) * 512],
                    PS[:, og * 512:(og + 1) * 512], 1.0,
                    t["xn6"][qt % 2][:, og * 512:(og + 1) * 512], ALU.mult, ALU.add),
                    "dve", 1, label=("p6x2", qt, og))
            for half in range(2):
                em.w(v, "pe", ("p6ttr", qt, half))
                for k in range(8):
                    dc = half * 8 + k
                    inst = v.tensor_scalar_mul(
                        tfm[:, dc * OWN + qt * 128: dc * OWN + (qt + 1) * 128],
                        t["PTB"][:, k * 128:(k + 1) * 128],
                        t["ln2_col"][:, dc:dc + 1])
                em.i(inst, "dve", 1, label=("p6ttev", qt, half))
            em.w(v, "pe", ("p6gate", qt))
            em.selfsem(v, v.tensor_copy(t["Lb"][:, :], PS[:, 2048:2064]), "vsf")
            em.selfsem(v, v.tensor_copy(t["curb"][:, :], t["Lb"][:, :]), "vsf")
            em.mark("dve", ("p6gev", qt))
            for j in range(6):
                em.selfsem(v, v.reduce_max(t["mjs"][:, j:j + 1], t["curb"][:, :], axis=AX.X),
                           "vsf")
                if j < 5:
                    em.selfsem(v, v.tensor_single_scalar(t["maskb"][:, :], t["curb"][:, :],
                                                         t["mjs"][:, j:j + 1], ALU.is_ge),
                               "vsf")
                    em.selfsem(v, v.scalar_tensor_tensor(t["curb"][:, :], t["maskb"][:, :],
                                                         -1e30, t["curb"][:, :],
                                                         ALU.mult, ALU.add), "vsf")
            em.selfsem(v, v.tensor_single_scalar(t["maskb"][:, :], t["Lb"][:, :],
                                                 t["mjs"][:, 5:6], ALU.is_ge), "vsf")
            em.wopt(v, "act", ("p6exp", qt - 2))
            em.i(v.tensor_scalar_mul(t["g6"][:, 4 + qt % 2: 5 + qt % 2],
                                     t["mjs"][:, 0:1], -1.0),
                 "dve", 1, label=("p6negm1", qt))
            em.w(v, "act", ("p6exp", qt))
            em.selfsem(v, v.tensor_mul(t["Eb"][:, :], t["Eb"][:, :], t["maskb"][:, :]), "vsf")
            em.selfsem(v, v.reduce_sum(t["g6"][:, 6:7], t["Eb"][:, :], axis=AX.X), "vsf")
            em.selfsem(v, v.reciprocal(t["g6"][:, 7:8], t["g6"][:, 6:7]), "vsf")
            em.i(v.tensor_scalar_mul(t["s_cw"][:, qt * 16:(qt + 1) * 16], t["Eb"][:, :],
                                     t["g6"][:, 7:8]),
                 "dve", 1, label=("p6cw", qt))

    def alloc_p7(self, st):
        nc, t = self.nc, self.t
        e = st.enter_context
        t["PS1"] = e(nc.psum_tensor("PS1", [128, 512], F32))
        t["hh"] = e(nc.sbuf_tensor("hh", [128, SUBI * 1024], BF16))
        t["hsil"] = e(nc.sbuf_tensor("hsil", [128, 1024], BF16))
        t["seg"] = [e(nc.sbuf_tensor(f"seg{i}", [128, ND * 128], BF16)) for i in range(2)]
        t["seu"] = [e(nc.sbuf_tensor(f"seu{i}", [128, ND * 128], BF16)) for i in range(2)]
        t["sed"] = e(nc.sbuf_tensor("sed", [128, SUBI * D], BF16))
        t["acc"] = e(nc.sbuf_tensor("acc", [128, 8 * D], F32))
        t["fxs"] = e(nc.sbuf_tensor("fxs", [128, D], F32))
        self.p7gu = []
        self.p7dn = []
        gix = 0
        dix = 0
        for half in range(2):
            for e_ in range(NEXP + 1):
                ni = NI_E if e_ < NEXP else NI_S
                subs = []
                s0 = 0
                while s0 < ni:
                    subs.append((s0, min(SUBI, ni - s0)))
                    s0 += SUBI
                for si, (s0, slen) in enumerate(subs):
                    for k in range(slen):
                        self.p7gu.append((half, e_, si, s0 + k, gix))
                        gix += 1
                    self.p7dn.append((half, e_, si, s0, slen, dix))
                    dix += 1
        self.p7subs = {}
        for half, e_, si, s0, slen, dix_ in self.p7dn:
            self.p7subs.setdefault((half, e_), []).append((si, s0, slen, dix_))

    def _wsl(self, name, e_, oc):
        io = self.io
        if e_ < NEXP:
            w = io["egT"] if name == "g" else io["euT"]
            return w[e_, :, oc * 128:(oc + 1) * 128].rearrange("(c p) o -> p c o", p=128)
        w = io["sgT"] if name == "g" else io["suT"]
        return w[:, oc * 128:(oc + 1) * 128].rearrange("(c p) o -> p c o", p=128)

    def _wdl(self, e_, ic):
        io = self.io
        if e_ < NEXP:
            return io["edT"][e_, ic * 128:(ic + 1) * 128, :]
        return io["sdT"][ic * 128:(ic + 1) * 128, :]

    def p7_g(self, g):
        em, t, io = self.em, self.t, self.io
        em.join(g, "p6")
        dn_by_gix = {}
        for half, e_, si, s0, slen, dix in self.p7dn:
            dn_by_gix[(half, e_, si)] = (s0, slen, dix)
        for half in range(2):
            for half2, e_, si, oc, gix in self.p7gu:
                if half2 != half:
                    continue
                em.wopt(g, "pe", ("p7um", gix - 2))
                em.i(g.dma_start(t["seg"][gix % 2][:, :].rearrange("p (c o) -> p c o", c=ND),
                                 self._wsl("g", e_, oc)), f"gw{gix % 2}", 16)
                em.i(g.dma_start(t["seu"][gix % 2][:, :].rearrange("p (c o) -> p c o", c=ND),
                                 self._wsl("u", e_, oc)), f"gw{gix % 2}", 16,
                     label=("p7wu", gix))
                s0, slen, dix = dn_by_gix[(half, e_, si)]
                if oc == s0 + slen - 1:
                    em.wopt(g, "pe", ("p7dnl", dix - 1))
                    for k in range(slen):
                        em.i(g.dma_start(t["sed"][:, k * D:(k + 1) * D],
                                         self._wdl(e_, s0 + k)),
                             "sed", 16, label=("p7wd", dix, k))
            for ttl in range(8):
                r = half * 8 + ttl
                em.w(g, "fx7", ("p7f1", half, ttl))
                em.i(g.dma_start(t["fxs"][:, :], io["ao_sp"][r * 128:(r + 1) * 128, :],
                                 accum_op=ALU.add), "ac7", 16, label=("p7ac1", half, ttl))
                em.w(g, "ac7", ("p7ac1", half, ttl))
                em.i(g.dma_start(t["fxs"][:, :], io["x2_sp"][r * 128:(r + 1) * 128, :],
                                 accum_op=ALU.add), "ac7", 16, label=("p7ac", half, ttl))

    def p7_t(self, tt):
        em, t = self.em, self.t
        em.join(tt, "p6")
        PS, PS1 = t["PS"], t["PS1"]
        tfm = t["OFM"]
        pg = [PS[:, 0:512], PS[:, 512:1024]]
        pu = [PS[:, 1024:1536], PS[:, 1536:2048]]
        pdown = [PS[:, 2048:2560], PS[:, 2560:3072], PS[:, 3072:3584], PS1[:, :]]

        gu_iter = iter(self.p7gu)
        for half, e_, subs_key in [(h, e, None) for h in range(2) for e in range(NEXP + 1)]:
            subs = self.p7subs[(half, e_)]
            for si, s0, slen, dix in subs:
                for k in range(slen):
                    half2, e2, si2, oc, gix = next(gu_iter)
                    assert (half2, e2, si2, oc) == (half, e_, si, s0 + k)
                    em.w(tt, f"gw{gix % 2}", ("p7wu", gix))
                    em.wopt(tt, "act", ("p7sil", gix - 1))
                    for gr in range(2):
                        for dc in range(ND):
                            col = dc * OWN + half * 1024 + gr * 512
                            inst = tt.matmul(pg[gr],
                                             t["seg"][gix % 2][:, dc * 128:(dc + 1) * 128],
                                             tfm[:, col:col + 512],
                                             start=(dc == 0), stop=(dc == ND - 1))
                    em.i(inst, "pe", 1, label=("p7gm", gix))
                    em.wopt(tt, "dve", ("p7hh", gix - 1))
                    for gr in range(2):
                        for dc in range(ND):
                            col = dc * OWN + half * 1024 + gr * 512
                            inst = tt.matmul(pu[gr],
                                             t["seu"][gix % 2][:, dc * 128:(dc + 1) * 128],
                                             tfm[:, col:col + 512],
                                             start=(dc == 0), stop=(dc == ND - 1))
                    em.i(inst, "pe", 1, label=("p7um", gix))
                em.w(tt, "sed", ("p7wd", dix, slen - 1))
                for ttl in range(8):
                    if ttl > 0:
                        em.w(tt, "dve", ("p7ev", dix, ttl - 1))
                    else:
                        em.wopt(tt, "dve", ("p7ev", dix - 1, 7))
                    for k in range(slen):
                        em.w(tt, "dve", ("p7hh", self._p7gix(half, e_, si, s0 + k)))
                        for dg in range(4):
                            inst = tt.matmul(
                                pdown[dg],
                                t["hh"][:, k * 1024 + ttl * 128: k * 1024 + ttl * 128 + 128],
                                t["sed"][:, k * D + dg * 512: k * D + (dg + 1) * 512],
                                start=(k == 0), stop=(k == slen - 1))
                    em.i(inst, "pe", 1, label=("p7dn", dix, ttl))
                em.mark("pe", ("p7dnl", dix))

    def _p7gix(self, half, e_, si, oc):
        if not hasattr(self, "_p7gixmap"):
            self._p7gixmap = {}
            for h2, e2, si2, oc2, gix in self.p7gu:
                self._p7gixmap[(h2, e2, si2, oc2)] = gix
        return self._p7gixmap[(half, e_, si, oc)]

    def p7_s(self, s):
        em, t = self.em, self.t
        em.join(s, "p6")
        for half, e_, si, oc, gix in self.p7gu:
            em.w(s, "pe", ("p7gm", gix))
            em.wopt(s, "dve", ("p7hh", gix - 1))
            em.i(s.activation(t["hsil"][:, :], t["PS"][:, 0:1024], AF.Silu,
                              bias=0.0, scale=1.0),
                 "act", 1, label=("p7sil", gix))

    def p7_v(self, v):
        em, t = self.em, self.t
        em.join(v, "p6")
        PS, PS1 = t["PS"], t["PS1"]
        pdown = [PS[:, 2048:2560], PS[:, 2560:3072], PS[:, 3072:3584], PS1[:, :]]
        gu_iter = iter(self.p7gu)
        first_writer = {}
        for half in range(2):
            for e_ in range(NEXP + 1):
                subs = self.p7subs[(half, e_)]
                for si, s0, slen, dix in subs:
                    for k in range(slen):
                        half2, e2, si2, oc, gix = next(gu_iter)
                        em.w(v, "act", ("p7sil", gix))
                        em.w(v, "pe", ("p7um", gix))
                        em.i(v.tensor_mul(t["hh"][:, k * 1024:(k + 1) * 1024],
                                          t["hsil"][:, :], PS[:, 1024:2048]),
                             "dve", 1, label=("p7hh", gix))
                    for ttl in range(8):
                        em.w(v, "pe", ("p7dn", dix, ttl))
                        em.wopt(v, "dve", ("p7ev", dix - 1, ttl))
                        cwcol = 256 if e_ == NEXP else (half * 8 + ttl) * 16 + e_
                        fw = first_writer.get((half, ttl))
                        for dg in range(4):
                            if fw is None:
                                inst = v.tensor_scalar_mul(
                                    t["acc"][:, ttl * D + dg * 512: ttl * D + (dg + 1) * 512],
                                    pdown[dg], t["s_cw"][:, cwcol:cwcol + 1])
                            else:
                                inst = v.scalar_tensor_tensor(
                                    t["acc"][:, ttl * D + dg * 512: ttl * D + (dg + 1) * 512],
                                    pdown[dg], t["s_cw"][:, cwcol:cwcol + 1],
                                    t["acc"][:, ttl * D + dg * 512: ttl * D + (dg + 1) * 512],
                                    ALU.mult, ALU.add)
                        first_writer[(half, ttl)] = True
                        em.i(inst, "dve", 1, label=("p7ev", dix, ttl))
            for ttl in range(8):
                last_dix = self.p7subs[(half, NEXP)][-1][3]
                em.w(v, "ac7", ("p7ac", half, ttl))
                em.w(v, "dve", ("p7ev", last_dix, ttl))
                em.i(v.tensor_add(t["fxs"][:, :], t["fxs"][:, :],
                                  t["acc"][:, ttl * D:(ttl + 1) * D]),
                     "dve", 1, label=("p7out", half, ttl))
            first_writer = {}

    def p7_y(self, y):
        em, t, io = self.em, self.t, self.io
        em.join(y, "p6")
        for half in range(2):
            for ttl in range(8):
                r = half * 8 + ttl
                if r >= 1:
                    em.w(y, "st7", ("p7st", (r - 1) // 8, (r - 1) % 8))
                em.i(y.dma_start(t["fxs"][:, :], io["x_own"][r * 128:(r + 1) * 128, :]),
                     "fx7", 16, label=("p7f1", half, ttl))
                em.w(y, "dve", ("p7out", half, ttl))
                em.i(y.dma_start(io["out"][r * 128:(r + 1) * 128, :], t["fxs"][:, :]),
                     "st7", 16, label=("p7st", half, ttl))
        em.w(y, "st7", ("p7st", 1, 7))

    def build(self, nphases=99):
        self.declare_dram()
        nc, em = self.nc, self.em
        with ExitStack() as persist:
            self.alloc_persist(persist)
            em.attach(nc, persist)
            block = persist.enter_context(nc.Block())
            mocks = {k: MockEng() for k in "tvsgy"}

            def run_phase(name, fns, allocf=None, pre=None, post=None):
                with ExitStack() as st:
                    if allocf:
                        allocf(st)
                    if pre:
                        pre()
                    em.p1()
                    for key in ("t", "v", "s", "g", "y"):
                        if key in fns:
                            fns[key](mocks[key])
                    em.endphase(name)
                    em.p2()
                    for key, section in (("t", block.tensor), ("v", block.vector),
                                         ("s", block.scalar), ("g", block.gpsimd),
                                         ("y", block.sync)):
                        if key in fns:
                            section(fns[key])
                    em.endphase(name)
                if post:
                    post()

            plist = [
                ("p0", lambda: run_phase("p0", dict(v=self.p0_v, g=self.p0_g, y=self.p0_y))),
                ("p1", lambda: (self.big_alloc("xn_fm", (128, ND * SEQ), BF16),
                                run_phase("p1", dict(t=self.p1_t, v=self.p1_v, s=self.p1_s,
                                                     g=self.p1_g, y=self.p1_y), self.alloc_p1))),
                ("p2", lambda: run_phase("p2", dict(t=self.p2_t, v=self.p2_v, s=self.p2_s,
                                                    y=self.p2_y), self.alloc_p2)),
                ("p3", lambda: run_phase("p3", dict(t=self.p3_t, v=self.p3_v, g=self.p3_g,
                                                    y=self.p3_y), self.alloc_p3,
                                         post=lambda: self.big_free("xn_fm"))),
                ("p4", lambda: run_phase("p4", dict(t=self.p4_t, v=self.p4_v, s=self.p4_s,
                                                    g=self.p4_g, y=self.p4_y), self.alloc_p4)),
                ("p5", lambda: (self.big_alloc("OFM", (128, NH_F * OWN), BF16),
                                run_phase("p5", dict(t=self.p5_t, v=self.p5_v, s=self.p5_s,
                                                     g=self.p5_g, y=self.p5_y), self.alloc_p5))),
                ("p6", lambda: run_phase("p6", dict(t=self.p6_t, v=self.p6_v, s=self.p6_s,
                                                    g=self.p6_g, y=self.p6_y), self.alloc_p6)),
                ("p7", lambda: run_phase("p7", dict(t=self.p7_t, v=self.p7_v, s=self.p7_s,
                                                    g=self.p7_g, y=self.p7_y), self.alloc_p7,
                                         post=lambda: self.big_free("OFM"))),
            ]
            lastnm = None
            for i, (nm_, fn_) in enumerate(plist):
                if i < nphases:
                    fn_()
                    lastnm = nm_
            fin = lastnm

            def mkfin(engname):
                def f(eng):
                    em.join(eng, fin)
                return f
            run_phase("fin", {k: mkfin(k) for k in ("t", "v", "s", "g", "y")})
            for nm_ in list(reversed(list(self.guards.keys()))):
                self.big_free(nm_)
        return nc


def build_program(dev=False, nphases=99):
    return Builder(dev=dev).build(nphases)


_BF = ml_dtypes.bfloat16


def _prep_weights(inputs):
    w = {}
    w["Wqkv"] = np.ascontiguousarray(inputs["L0_attn_Wqkv"], np.float32)
    bqkv = np.asarray(inputs["L0_attn_bqkv"], np.float32)
    w["bqkv_col"] = np.ascontiguousarray(bqkv.reshape(24, 128).T)
    w["bv_row"] = np.ascontiguousarray(bqkv[2560:3072][None, :])
    w["Wo"] = np.ascontiguousarray(inputs["L0_attn_Wo"], np.float32)
    w["bo_row"] = np.ascontiguousarray(np.asarray(inputs["L0_attn_bo"], np.float32)[None, :])
    w["wq"] = np.ascontiguousarray(inputs["L0_ffn_wq"], np.float32)
    w["wk"] = np.ascontiguousarray(inputs["L0_ffn_wk"], np.float32)
    w["wv"] = np.ascontiguousarray(inputs["L0_ffn_wv"], np.float32)
    w["wo"] = np.ascontiguousarray(inputs["L0_ffn_wo"], np.float32)
    w["ln1_col"] = np.ascontiguousarray(
        np.asarray(inputs["L0_ffn_ln1_w"], np.float32).reshape(16, 128).T)
    w["ln2_col"] = np.ascontiguousarray(
        np.asarray(inputs["L0_ffn_ln2_w"], np.float32).reshape(16, 128).T)
    w["gate_T"] = np.ascontiguousarray(np.asarray(inputs["L0_ffn_gate_w"], np.float32).T)
    w["egT"] = np.ascontiguousarray(
        np.asarray(inputs["L0_ffn_eg_w"], np.float32).transpose(0, 2, 1))
    w["euT"] = np.ascontiguousarray(
        np.asarray(inputs["L0_ffn_eu_w"], np.float32).transpose(0, 2, 1))
    w["edT"] = np.ascontiguousarray(
        np.asarray(inputs["L0_ffn_ed_w"], np.float32).transpose(0, 2, 1))
    w["sgT"] = np.ascontiguousarray(np.asarray(inputs["L0_ffn_sg_w"], np.float32).T)
    w["suT"] = np.ascontiguousarray(np.asarray(inputs["L0_ffn_su_w"], np.float32).T)
    w["sdT"] = np.ascontiguousarray(np.asarray(inputs["L0_ffn_sd_w"], np.float32).T)
    w["consts_in"] = np.tile(np.array([[EPS_LN, EPS_RMS, 1.0 / D, -0.5, SC_A, SC_F,
                                        0.0, 0.0]], np.float32), (128, 1))
    w["ident_bf"] = np.eye(128, dtype=_BF)
    w["ones_bf"] = np.ones((1, 128), _BF)
    w["ones_f32"] = np.ones((1, 128), np.float32)
    return w


def kernel(**inputs):
    x = np.asarray(inputs["L0_attn_x"], np.float32)
    w = _prep_weights(inputs)
    nc = build_program(dev=False)
    in_maps = []
    for c in range(8):
        b = c // 2
        h = c % 2
        m = dict(w)
        m["x_own"] = np.ascontiguousarray(x[b, h * 2048:(h + 1) * 2048, :])
        m["x_oth"] = np.ascontiguousarray(x[b, (1 - h) * 2048:(2 - h) * 2048, :])
        m["qbase_col"] = np.full((128, 1), float(h * 2048), np.float32)
        m["othbase_col"] = np.full((128, 1), float(-h * 2048), np.float32)
        in_maps.append(m)
    res = run_bass_kernel_spmd(nc, in_maps, core_ids=list(range(8)))
    out = np.empty((4, 4096, 2048), np.float32)
    for c in range(8):
        b = c // 2
        h = c % 2
        out[b, h * 2048:(h + 1) * 2048, :] = res.results[c]["out"]
    return out


# version 07
# speedup vs baseline: 1.1535x; 1.1535x over previous
import numpy as np
import ml_dtypes
from contextlib import ExitStack

import concourse.bass as bass
import concourse.mybir as mybir
from concourse.bass_utils import run_bass_kernel_spmd

F32 = mybir.dt.float32
BF16 = mybir.dt.bfloat16
AF = mybir.ActivationFunctionType
ALU = mybir.AluOpType
AX = mybir.AxisListType

D = 2048
ND = 16
SEQ = 4096
OWN = 2048
NT_OWN = 16
NT_ALL = 32
HD_A = 64
NH_A = 32
SC_A = 0.125
HD_F = 128
NH_F = 16
SC_F = 1.0 / float(np.sqrt(128.0))
NEXP = 16
I_E = 1408
NI_E = 11
SI = 2816
NI_S = 22
EPS_LN = 1e-5
EPS_RMS = 1e-6
KTA = 24
KTB = 8
SUBI = 6
CONSTS = (EPS_LN, EPS_RMS, 1.0 / D, -0.5, SC_A, SC_F)

SEMS = ("pe", "dve", "act", "pool", "vsf", "ssf", "psf")


class _MI:
    def then_inc(self, *a, **k):
        return self


class MockEng:
    def __getattr__(self, name):
        return lambda *a, **k: _MI()


class Em:

    def __init__(self):
        self.lab = {}
        self.cnt = {n: 0 for n in SEMS}
        self.hs = {}
        self.mode = 1
        self._snap = None
        self._p1end = None
        self.nc = None
        self.stack = None

    def attach(self, nc, stack):
        self.nc = nc
        self.stack = stack
        for n in SEMS:
            self.hs[n] = stack.enter_context(nc.semaphore(n))

    def _s(self, name):
        if name not in self.cnt:
            self.cnt[name] = 0
        if self.mode == 2 and name not in self.hs:
            self.hs[name] = self.stack.enter_context(self.nc.semaphore(name))
        return name

    def p1(self):
        self.mode = 1
        self._snap = dict(self.cnt)

    def p2(self):
        self.mode = 2
        self._p1end = dict(self.cnt)
        self.cnt = dict(self._snap)

    def endphase(self, phase):
        if self.mode == 1:
            self.lab[("__endd", phase)] = dict(self.cnt)
        else:
            assert self.cnt == self._p1end, (phase, self.cnt, self._p1end)

    def _key(self, s, label):
        return (s,) + (label if isinstance(label, tuple) else (label,))

    def i(self, inst, s, k=1, label=None):
        self._s(s)
        self.cnt[s] += k
        if label is not None:
            key = self._key(s, label)
            if self.mode == 1:
                self.lab[key] = self.cnt[s]
            else:
                assert self.lab[key] == self.cnt[s], (key, self.lab[key], self.cnt[s])
        if self.mode == 2:
            inst.then_inc(self.hs[s], k)

    def mark(self, s, label):
        if self.mode == 1:
            self.lab[self._key(s, label)] = self.cnt[s]

    def w(self, eng, s, label):
        self._s(s)
        if self.mode == 2:
            eng.wait_ge(self.hs[s], self.lab[self._key(s, label)])

    def wopt(self, eng, s, label):
        self._s(s)
        if self.mode == 2:
            v = self.lab.get(self._key(s, label))
            if v is not None:
                eng.wait_ge(self.hs[s], v)

    def join(self, eng, prev_phase):
        if self.mode == 2:
            for s, v in sorted(self.lab[("__endd", prev_phase)].items()):
                if v > 0:
                    eng.wait_ge(self.hs[self._s(s)], v)

    def selfsem(self, eng, inst, s):
        self.cnt[s] += 1
        if self.mode == 2:
            inst.then_inc(self.hs[s], 1)
            eng.wait_ge(self.hs[s], self.cnt[s])


class Builder:
    def __init__(self, dev=False):
        self.dev = dev
        self.nc = bass.Bass("TRN2", target_bir_lowering=False)
        self.em = Em()
        self.io = {}
        self.t = {}
        self.guards = {}

    def declare_dram(self):
        nc, io = self.nc, self.io

        def inp(name, shape, dt=F32):
            io[name] = nc.dram_tensor(name, list(shape), dt, kind="ExternalInput").ap()

        inp("x_own", (OWN, D)); inp("x_oth", (OWN, D))
        inp("qbase_col", (128, 1)); inp("othbase_col", (128, 1))
        inp("Wqkv", (D, 3072)); inp("bqkv_col", (128, 24)); inp("bv_row", (1, 512))
        inp("Wo", (D, D)); inp("bo_row", (1, D))
        inp("wq", (D, D)); inp("wk", (D, D)); inp("wv", (D, D)); inp("wo", (D, D))
        inp("ln1_col", (128, 16)); inp("ln2_col", (128, 16))
        inp("gate_T", (D, 16))
        inp("egT", (NEXP, D, I_E)); inp("euT", (NEXP, D, I_E)); inp("edT", (NEXP, I_E, D))
        inp("sgT", (D, SI)); inp("suT", (D, SI)); inp("sdT", (SI, D))
        inp("consts_in", (128, 8))
        inp("ident_bf", (128, 128), BF16)
        inp("ones_bf", (1, 128), BF16)
        inp("ones_f32", (1, 128))

        io["out"] = nc.dram_tensor("out", [OWN, D], F32, kind="ExternalOutput").ap()

        spk = "ExternalOutput" if self.dev else "Internal"

        def spill(name, shape, dt=BF16, kind=None):
            io[name] = nc.dram_tensor(name, list(shape), dt, kind=kind or spk).ap()

        spill("qa_sp", (16, 128, OWN))
        spill("ka_sp", (4, 128, SEQ))
        spill("va_sp", (SEQ, 512))
        spill("qf_sp", (16, 128, OWN))
        spill("kf_sp", (16, 128, SEQ))
        spill("vf_sp", (SEQ, D))
        spill("ao_sp", (OWN, D), F32)
        spill("x2_sp", (OWN, D), F32)
        spill("r1_bounce", (32, 128), BF16, kind="Internal")
        spill("mask_sp", (16, 128, SEQ), BF16, kind="Internal")
        if self.dev:
            spill("h1_tap", (128, ND * SEQ), BF16)
            spill("tfm_tap", (128, ND * OWN), BF16)
            spill("cw_tap", (128, 272), F32)
            spill("ofm_tap", (128, NH_F * OWN), BF16)

    def alloc_persist(self, stack):
        nc, t = self.nc, self.t
        e = stack.enter_context
        t["PS"] = e(nc.psum_tensor("PS", [128, 3584], F32))
        t["negmean_s"] = e(nc.sbuf_tensor("negmean_s", [128, 32], F32))
        t["rstd_s"] = e(nc.sbuf_tensor("rstd_s", [128, 32], F32))
        t["vsum_s"] = e(nc.sbuf_tensor("vsum_s", [128, 32], F32))
        t["bias2_s"] = e(nc.sbuf_tensor("bias2_s", [128, 32], F32))
        t["r1_s"] = e(nc.sbuf_tensor("r1_s", [128, 32], F32))
        t["sb_qbase"] = e(nc.sbuf_tensor("sb_qbase", [128, 1], F32))
        t["sb_othbase"] = e(nc.sbuf_tensor("sb_othbase", [128, 1], F32))
        t["bqkv_col"] = e(nc.sbuf_tensor("sb_bqkv_col", [128, 24], F32))
        t["bv_row"] = e(nc.sbuf_tensor("sb_bv_row", [1, 512], F32))
        t["bo_row"] = e(nc.sbuf_tensor("sb_bo_row", [1, D], F32))
        t["ln1_col"] = e(nc.sbuf_tensor("sb_ln1_col", [128, 16], F32))
        t["ln2_col"] = e(nc.sbuf_tensor("sb_ln2_col", [128, 16], F32))
        t["sgate"] = e(nc.sbuf_tensor("sgate", [128, 256], BF16))
        t["s_cw"] = e(nc.sbuf_tensor("s_cw", [128, 272], F32))
        t["sident"] = e(nc.sbuf_tensor("sident", [128, 128], BF16))
        t["sones_bf"] = e(nc.sbuf_tensor("sones_bf", [1, 128], BF16))
        t["sones_f"] = e(nc.sbuf_tensor("sones_f", [1, 128], F32))
        t["stat8"] = e(nc.sbuf_tensor("stat8", [128, 16], F32))
        t["sconsts"] = e(nc.sbuf_tensor("sconsts", [128, 8], F32))
        for i, val in enumerate(CONSTS):
            nc.const_aps.register(F32, val, t["sconsts"][:, i:i + 1])

    def big_alloc(self, name, shape, dt):
        g = self.nc.sbuf_tensor(name, list(shape), dt)
        h = g.__enter__()
        self.guards[name] = g
        self.t[name] = h
        return h

    def big_free(self, name):
        self.guards.pop(name).__exit__(None, None, None)
        self.t.pop(name, None)

    def p0_y(self, y):
        em, t, io = self.em, self.t, self.io
        for nm, dst in (("ident_bf", "sident"), ("ones_bf", "sones_bf"),
                        ("ones_f32", "sones_f"), ("bqkv_col", "bqkv_col"),
                        ("bv_row", "bv_row"), ("bo_row", "bo_row"),
                        ("ln1_col", "ln1_col"), ("ln2_col", "ln2_col"),
                        ("qbase_col", "sb_qbase"), ("othbase_col", "sb_othbase"),
                        ("consts_in", "sconsts")):
            em.i(y.dma_start(t[dst][:, :], io[nm]), "c0", 16)

    def p0_g(self, g):
        em, t, io = self.em, self.t, self.io
        em.i(g.dma_start(t["sgate"][:, :].rearrange("p (c e) -> p c e", c=16),
                         io["gate_T"].rearrange("(c p) e -> p c e", p=128)),
             "cg0", 16, label="p0gate")

    def p0_v(self, v):
        em, t = self.em, self.t
        em.i(v.memset(t["s_cw"][:, 256:272], 1.0), "dve", 1, label="p0cw1")

    def alloc_p1(self, st):
        nc, t = self.nc, self.t
        e = st.enter_context
        t["PTB"] = e(nc.psum_tensor("PTB1", [128, 1024], BF16))
        t["sxf"] = [e(nc.sbuf_tensor(f"sxf{i}", [128, D], F32)) for i in range(2)]
        t["sxn"] = [e(nc.sbuf_tensor(f"sxn{i}", [128, D], BF16)) for i in range(2)]
        t["swb"] = [e(nc.sbuf_tensor(f"swb{i}", [128, ND * 128], BF16)) for i in range(2)]
        t["swv"] = e(nc.sbuf_tensor("swv1", [128, ND * 512], BF16))
        t["stage"] = [e(nc.sbuf_tensor(f"stage{i}", [128, D], BF16)) for i in range(2)]
        t["vstage"] = [e(nc.sbuf_tensor(f"vstage{i}", [128, 512], BF16)) for i in range(2)]
        t["rs_t"] = e(nc.sbuf_tensor("rs_t", [128, 2], F32))
        t["iota128"] = e(nc.sbuf_tensor("iota128", [128, 128], F32))
        t["thr0"] = e(nc.sbuf_tensor("thr0", [128, 32], F32))
        t["thrq"] = [e(nc.sbuf_tensor(f"thrq{i}", [128, 32], F32)) for i in range(2)]
        t["obq"] = e(nc.sbuf_tensor("obq", [128, 1], F32))

        seq = []
        for oc in range(20):
            for rnd in range(1 if oc < 16 else 2):
                seq.append((oc, rnd))
        self.p1seq = seq

    def _xtile_src(self, r):
        io = self.io
        if r < 16:
            return io["x_own"][r * 128:(r + 1) * 128, :]
        return io["x_oth"][(r - 16) * 128:(r - 16 + 1) * 128, :]

    def p1_y(self, y):
        em, t, io = self.em, self.t, self.io
        em.join(y, "p0")
        for r in range(NT_ALL):
            if r >= 2:
                em.w(y, "act", ("p1xn", r - 2))
            em.i(y.dma_start(t["sxf"][r % 2][:, :], self._xtile_src(r)),
                 f"x{r % 2}", 16, label=("p1x", r))
        for si, (oc, rnd) in enumerate(self.p1seq):
            em.w(y, "dve", ("p1pjev", si))
            st = t["stage"][si % 2]
            if oc < 16:
                for h, srcband in ((2 * oc, 0), (2 * oc + 1, 1)):
                    m, rem = divmod(h, 8)
                    band = rem // 4
                    rr = rem % 4
                    c_t = 4 * m + rr
                    em.i(y.dma_start(io["qa_sp"][c_t, band * 64:(band + 1) * 64, :],
                                     st[srcband * 64:(srcband + 1) * 64, :]),
                         f"sp{si % 2}", 16,
                         label=("p1sp", si) if h % 2 == 1 else None)
            else:
                em.i(y.dma_start(io["ka_sp"][oc - 16, :, rnd * 2048:(rnd + 1) * 2048],
                                 st[:, :]), f"sp{si % 2}", 16, label=("p1sp", si))
        for r in range(NT_ALL):
            em.w(y, "dve", ("p1vev", r))
            em.i(y.dma_start(io["va_sp"][r * 128:(r + 1) * 128, :], t["vstage"][r % 2][:, :]),
                 f"vs{r % 2}", 16, label=("p1vsp", r))

    def p1_s(self, s):
        em, t = self.em, self.t
        em.join(s, "p0")
        for r in range(NT_ALL):
            em.w(s, "dve", ("p1rs", r))
            inst = s.mul(t["negmean_s"][:, r:r + 1], t["rs_t"][:, r % 2:r % 2 + 1], -1.0 / D)
            em.i(inst, "act", 1, label=("p1nm", r))
            em.w(s, "act", ("p1nm", r))
            em.selfsem(s, s.activation(t["stage"][0][:, :], t["sxf"][r % 2][:, :],
                                       AF.Square,
                                       bias=t["negmean_s"][:, r:r + 1], scale=1.0,
                                       accum_out=t["vsum_s"][:, r:r + 1]), "ssf")
            em.selfsem(s, s.activation(t["stat8"][:, 0:1], t["vsum_s"][:, r:r + 1], AF.Ln,
                                       bias=EPS_LN, scale=1.0 / D), "ssf")
            em.selfsem(s, s.activation(t["rstd_s"][:, r:r + 1], t["stat8"][:, 0:1], AF.Exp,
                                       bias=0.0, scale=-0.5), "ssf")
            em.selfsem(s, s.mul(t["bias2_s"][:, r:r + 1], t["negmean_s"][:, r:r + 1],
                                t["rstd_s"][:, r:r + 1]), "ssf")
            em.wopt(s, "pe", ("p1tB", r - 2))
            em.i(s.activation(t["sxn"][r % 2][:, :], t["sxf"][r % 2][:, :], AF.Identity,
                              bias=t["bias2_s"][:, r:r + 1], scale=t["rstd_s"][:, r:r + 1]),
                 "act", 1, label=("p1xn", r))

    def p1_v(self, v):
        em, t = self.em, self.t
        em.join(v, "p0")
        for qt in range(NT_OWN):
            bi = qt % 2
            em.w(v, "pool", ("p1thr", qt))
            em.wopt(v, f"mks{bi}", ("p1mks", qt - 2))
            mb = t["swv"][:, bi * SEQ:bi * SEQ + SEQ]
            in0 = bass.AP(t["iota128"], 0, [[128, 128], [0, 32], [1, 128]])
            in1 = bass.AP(t["thrq"][bi], 0, [[32, 128], [1, 32], [0, 128]])
            em.i(v.tensor_tensor(mb[:, :].rearrange("p (c q) -> p c q", c=32),
                                 in0, in1, ALU.is_ge),
                 "dve", 1, label=("p1mtt", qt))
        xn_fm = t["xn_fm"]
        for r in range(NT_ALL):
            em.w(v, f"x{r % 2}", ("p1x", r))
            em.wopt(v, "act", ("p1nm", r - 2))
            em.i(v.reduce_sum(t["rs_t"][:, r % 2:r % 2 + 1], t["sxf"][r % 2][:, :], axis=AX.X),
                 "dve", 1, label=("p1rs", r))
            em.w(v, "pe", ("p1tA", r))
            em.i(v.tensor_copy(
                xn_fm[:, :].rearrange("p (c k) -> p c k", c=ND)[:, 0:8, r * 128:(r + 1) * 128],
                t["PTB"][:, :].rearrange("p (a b) -> p a b", a=8)),
                "dve", 1, label=("p1eA", r))
            em.w(v, "pe", ("p1tB", r))
            em.i(v.tensor_copy(
                xn_fm[:, :].rearrange("p (c k) -> p c k", c=ND)[:, 8:16, r * 128:(r + 1) * 128],
                t["PTB"][:, :].rearrange("p (a b) -> p a b", a=8)),
                "dve", 1, label=("p1eB", r))
        PS = t["PS"]
        for si, (oc, rnd) in enumerate(self.p1seq):
            em.w(v, "pe", ("p1pj", si))
            em.wopt(v, f"sp{si % 2}", ("p1sp", si - 2))
            st = t["stage"][si % 2]
            for gi in range(4):
                inst = v.tensor_scalar_add(st[:, gi * 512:(gi + 1) * 512],
                                           PS[:, gi * 512:(gi + 1) * 512],
                                           t["bqkv_col"][:, oc:oc + 1])
            em.i(inst, "dve", 1, label=("p1pjev", si))
        for r in range(NT_ALL):
            em.w(v, "pe", ("p1v", r))
            em.wopt(v, f"vs{r % 2}", ("p1vsp", r - 2))
            em.i(v.tensor_copy(t["vstage"][r % 2][:, :], PS[:, (r % 4) * 512:(r % 4 + 1) * 512]),
                 "dve", 1, label=("p1vev", r))

    def p1_t(self, tt):
        em, t = self.em, self.t
        em.join(tt, "p0")
        for r in range(NT_ALL):
            em.w(tt, "act", ("p1xn", r))
            em.wopt(tt, "dve", ("p1eB", r - 1))
            for k in range(8):
                inst = tt.transpose(t["PTB"][:, k * 128:(k + 1) * 128],
                                    t["sxn"][r % 2][:, k * 128:(k + 1) * 128],
                                    t["sident"][:, :])
            em.i(inst, "pe", 1, label=("p1tA", r))
            em.w(tt, "dve", ("p1eA", r))
            for k in range(8):
                inst = tt.transpose(t["PTB"][:, k * 128:(k + 1) * 128],
                                    t["sxn"][r % 2][:, (8 + k) * 128:(9 + k) * 128],
                                    t["sident"][:, :])
            em.i(inst, "pe", 1, label=("p1tB", r))
        PS = t["PS"]
        xn_fm = t["xn_fm"]
        for si, (oc, rnd) in enumerate(self.p1seq):
            em.w(tt, f"w{oc % 2}", ("p1w", oc))
            if si == 0:
                em.w(tt, "dve", ("p1eB", NT_ALL - 1))
            em.wopt(tt, "dve", ("p1pjev", si - 1))
            for dc in range(ND):
                for gi in range(4):
                    col = dc * SEQ + rnd * 2048 + gi * 512
                    inst = tt.matmul(PS[:, gi * 512:(gi + 1) * 512],
                                     t["swb"][oc % 2][:, dc * 128:(dc + 1) * 128],
                                     xn_fm[:, col:col + 512],
                                     start=(dc == 0), stop=(dc == ND - 1))
            em.i(inst, "pe", 1, label=("p1pj", si))
            if rnd == (0 if oc < 16 else 1):
                em.mark("pe", ("p1pjoc", oc))
        em.w(tt, "wv", ("p1wv", ND - 1))
        em.w(tt, "dve", ("p1pjev", len(self.p1seq) - 1))
        for r in range(NT_ALL):
            em.wopt(tt, "dve", ("p1vev", r - 4))
            bank = r % 4
            inst = tt.matmul(PS[:, bank * 512:(bank + 1) * 512], t["sones_f"][:, :],
                             t["bv_row"][:, :], start=True, stop=False)
            for dc in range(ND):
                inst = tt.matmul(PS[:, bank * 512:(bank + 1) * 512],
                                 xn_fm[:, dc * SEQ + r * 128: dc * SEQ + (r + 1) * 128],
                                 t["swv"][:, dc * 512:(dc + 1) * 512],
                                 start=False, stop=(dc == ND - 1))
            em.i(inst, "pe", 1, label=("p1v", r))

    def p1_g(self, g):
        em, t, io = self.em, self.t, self.io
        em.join(g, "p0")
        em.selfsem(g, g.iota(t["iota128"][:, :], [[1, 128]], channel_multiplier=0,
                             allow_small_or_imprecise_dtypes=True), "psf")
        em.selfsem(g, g.iota(t["thr0"][:, :], [[128, 32]], channel_multiplier=1,
                             allow_small_or_imprecise_dtypes=True), "psf")
        em.selfsem(g, g.tensor_sub(t["obq"][:, :], t["sb_othbase"][:, :],
                                   t["sb_qbase"][:, :]), "psf")
        em.mark("pool", "p1iotas")
        for qt in range(NT_OWN):
            bi = qt % 2
            thrq = t["thrq"][bi]
            em.wopt(g, "dve", ("p1mtt", qt - 2))
            em.selfsem(g, g.tensor_scalar_add(thrq[:, 0:16], t["thr0"][:, 0:16],
                                              float(-128 * qt)), "psf")
            em.selfsem(g, g.tensor_scalar_add(thrq[:, 16:32], t["thr0"][:, 0:16],
                                              float(2048 - 128 * qt)), "psf")
            em.selfsem(g, g.tensor_scalar_add(thrq[:, 16:32], thrq[:, 16:32],
                                              t["obq"][:, :]), "psf")
            em.mark("pool", ("p1thr", qt))
            em.i(g.engine_nop() if False else g.tensor_copy(t["obq"][:, :], t["obq"][:, :]),
                 "pool", 1, label=("p1thr", qt))
            em.w(g, "dve", ("p1mtt", qt))
            mb = t["swv"][:, bi * SEQ:bi * SEQ + SEQ]
            em.i(g.dma_start(io["mask_sp"][qt, :, :], mb),
                 f"mks{qt % 2}", 16, label=("p1mks", qt))
        for oc in range(20):
            if oc >= 2:
                em.w(g, "pe", ("p1pjoc", oc - 2))
            em.i(g.dma_start(
                t["swb"][oc % 2][:, :].rearrange("p (c o) -> p c o", c=ND),
                io["Wqkv"][:, oc * 128:(oc + 1) * 128].rearrange("(c p) o -> p c o", p=128)),
                f"w{oc % 2}", 16, label=("p1w", oc))
        em.w(g, "mks0", ("p1mks", NT_OWN - 2))
        em.w(g, "mks1", ("p1mks", NT_OWN - 1))
        for dc in range(ND):
            em.i(g.dma_start(t["swv"][:, dc * 512:(dc + 1) * 512],
                             io["Wqkv"][dc * 128:(dc + 1) * 128, 2560:3072]),
                 "wv", 16, label=("p1wv", dc))

    def alloc_p2(self, st):
        nc, t = self.nc, self.t
        e = st.enter_context
        t["PTB"] = e(nc.psum_tensor("PTB2", [128, 1024], BF16))
        t["r1b"] = e(nc.sbuf_tensor("r1b", [128, 32], BF16))
        t["r1T"] = e(nc.sbuf_tensor("r1T", [32, 128], BF16))
        t["r1row"] = e(nc.sbuf_tensor("r1row", [1, SEQ], BF16))
        t["st2"] = e(nc.sbuf_tensor("st2", [128, 32], F32))

    def p2_v(self, v):
        em, t = self.em, self.t
        em.join(v, "p1")
        em.selfsem(v, v.tensor_mul(t["st2"][:, :], t["rstd_s"][:, :], t["rstd_s"][:, :]),
                   "vsf")
        em.i(v.tensor_mul(t["st2"][:, :], t["st2"][:, :], t["vsum_s"][:, :]),
             "dve", 1, label="p2t2")
        em.w(v, "act", "p2r1")
        em.i(v.tensor_copy(t["r1b"][:, :], t["r1_s"][:, :]), "dve", 1, label="p2r1b")
        em.w(v, "pe", "p2tr")
        em.i(v.tensor_copy(t["r1T"][0:32, :], t["PTB"][0:32, 0:128]),
             "dve", 1, label="p2trev")
        xn_fm = t["xn_fm"]
        for dc in range(ND):
            inst = v.tensor_scalar_mul(xn_fm[:, dc * SEQ:(dc + 1) * SEQ],
                                       xn_fm[:, dc * SEQ:(dc + 1) * SEQ],
                                       t["ln1_col"][:, dc:dc + 1])
            em.i(inst, "dve", 1, label=("p2ln1", dc))
        em.w(v, "dve", ("p2ln1", ND - 1))
        for r in range(NT_ALL):
            em.w(v, "pe", ("p2bc", r))
            for dc in range(ND):
                inst = v.tensor_mul(xn_fm[:, dc * SEQ + r * 128: dc * SEQ + (r + 1) * 128],
                                    xn_fm[:, dc * SEQ + r * 128: dc * SEQ + (r + 1) * 128],
                                    t["PS"][:, (r % 2) * 512:(r % 2) * 512 + 128])
            em.i(inst, "dve", 1, label=("p2h1", r))

    def p2_s(self, s):
        em, t = self.em, self.t
        em.join(s, "p1")
        em.w(s, "dve", "p2t2")
        em.selfsem(s, s.activation(t["st2"][:, :], t["st2"][:, :], AF.Ln,
                                   bias=EPS_RMS, scale=1.0 / D), "ssf")
        em.i(s.activation(t["r1_s"][:, :], t["st2"][:, :], AF.Exp, bias=0.0, scale=-0.5),
             "act", 1, label="p2r1")

    def p2_t(self, tt):
        em, t = self.em, self.t
        em.join(tt, "p1")
        em.w(tt, "dve", "p2r1b")
        em.i(tt.transpose(t["PTB"][0:32, 0:128], t["r1b"][:, :], t["sident"][:, :]),
             "pe", 1, label="p2tr")
        for r in range(NT_ALL):
            em.w(tt, "bn", "p2r1row")
            em.w(tt, "dve", ("p2ln1", ND - 1))
            em.wopt(tt, "dve", ("p2h1", r - 2))
            em.i(tt.matmul(t["PS"][:, (r % 2) * 512:(r % 2) * 512 + 128],
                           t["sones_bf"][:, :], t["r1row"][0:1, r * 128:(r + 1) * 128],
                           start=True, stop=True),
                 "pe", 1, label=("p2bc", r))

    def p2_y(self, y):
        em, t, io = self.em, self.t, self.io
        em.join(y, "p1")
        em.w(y, "dve", "p2trev")
        em.i(y.dma_start(io["r1_bounce"], t["r1T"][0:32, :]), "bn", 16, label="p2bsp")
        em.w(y, "bn", "p2bsp")
        em.i(y.dma_start(t["r1row"][0:1, :],
                         io["r1_bounce"].rearrange("a b -> (a b)")),
             "bn", 16, label="p2r1row")

    def alloc_p3(self, st):
        nc, t = self.nc, self.t
        e = st.enter_context
        t["swb"] = [e(nc.sbuf_tensor(f"p3swb{i}", [128, ND * 128], BF16)) for i in range(2)]
        t["swv"] = e(nc.sbuf_tensor("p3swv", [128, ND * 512], BF16))
        t["stage"] = [e(nc.sbuf_tensor(f"p3stage{i}", [128, D], BF16)) for i in range(2)]
        t["vstage"] = [e(nc.sbuf_tensor(f"p3vstage{i}", [128, 512], BF16)) for i in range(2)]
        seq = []
        for k in range(32):
            for rnd in range(1 if k < 16 else 2):
                seq.append((k, rnd))
        self.p3seq = seq

    def p3_g(self, g):
        em, t, io = self.em, self.t, self.io
        em.join(g, "p2")
        for k in range(32):
            wname = "wq" if k < 16 else "wk"
            oc = k % 16
            if k >= 2:
                em.w(g, "pe", ("p3pjoc", k - 2))
            em.i(g.dma_start(
                t["swb"][k % 2][:, :].rearrange("p (c o) -> p c o", c=ND),
                io[wname][:, oc * 128:(oc + 1) * 128].rearrange("(c p) o -> p c o", p=128)),
                f"w{k % 2}", 16, label=("p3w", k))
        for og in range(4):
            em.w(g, "pe", ("p3vwfree", og))
            for dc in range(ND):
                em.i(g.dma_start(t["swv"][:, dc * 512:(dc + 1) * 512],
                                 io["wv"][dc * 128:(dc + 1) * 128, og * 512:(og + 1) * 512]),
                     "wv", 16, label=("p3wv", og, dc))

    def p3_t(self, tt):
        em, t = self.em, self.t
        em.join(tt, "p2")
        PS = t["PS"]
        xn_fm = t["xn_fm"]
        for si, (k, rnd) in enumerate(self.p3seq):
            em.w(tt, f"w{k % 2}", ("p3w", k))
            em.wopt(tt, "dve", ("p3pjev", si - 1))
            for dc in range(ND):
                for gi in range(4):
                    col = dc * SEQ + rnd * 2048 + gi * 512
                    inst = tt.matmul(PS[:, gi * 512:(gi + 1) * 512],
                                     t["swb"][k % 2][:, dc * 128:(dc + 1) * 128],
                                     xn_fm[:, col:col + 512],
                                     start=(dc == 0), stop=(dc == ND - 1))
            em.i(inst, "pe", 1, label=("p3pj", si))
            if rnd == (0 if k < 16 else 1):
                em.mark("pe", ("p3pjoc", k))
        em.w(tt, "dve", ("p3pjev", len(self.p3seq) - 1))
        for og in range(4):
            em.mark("pe", ("p3vwfree", og))
            em.w(tt, "wv", ("p3wv", og, ND - 1))
            for r in range(NT_ALL):
                gr = og * NT_ALL + r
                em.wopt(tt, "dve", ("p3vev", gr - 4))
                bank = r % 4
                for dc in range(ND):
                    inst = tt.matmul(PS[:, bank * 512:(bank + 1) * 512],
                                     xn_fm[:, dc * SEQ + r * 128: dc * SEQ + (r + 1) * 128],
                                     t["swv"][:, dc * 512:(dc + 1) * 512],
                                     start=(dc == 0), stop=(dc == ND - 1))
                em.i(inst, "pe", 1, label=("p3v", gr))

    def p3_v(self, v):
        em, t = self.em, self.t
        em.join(v, "p2")
        PS = t["PS"]
        for si, (k, rnd) in enumerate(self.p3seq):
            em.w(v, "pe", ("p3pj", si))
            em.wopt(v, f"sp{si % 2}", ("p3sp", si - 2))
            st = t["stage"][si % 2]
            for gi in range(4):
                inst = v.tensor_copy(st[:, gi * 512:(gi + 1) * 512],
                                     PS[:, gi * 512:(gi + 1) * 512])
            em.i(inst, "dve", 1, label=("p3pjev", si))
        for gr in range(4 * NT_ALL):
            em.w(v, "pe", ("p3v", gr))
            em.wopt(v, f"vs{gr % 2}", ("p3vsp", gr - 2))
            em.i(v.tensor_copy(t["vstage"][gr % 2][:, :],
                               PS[:, (gr % 4) * 512:(gr % 4 + 1) * 512]),
                 "dve", 1, label=("p3vev", gr))

    def p3_y(self, y):
        em, t, io = self.em, self.t, self.io
        em.join(y, "p2")
        for si, (k, rnd) in enumerate(self.p3seq):
            em.w(y, "dve", ("p3pjev", si))
            st = t["stage"][si % 2]
            if k < 16:
                em.i(y.dma_start(io["qf_sp"][k, :, :], st[:, :]), f"sp{si % 2}", 16,
                     label=("p3sp", si))
            else:
                em.i(y.dma_start(io["kf_sp"][k - 16, :, rnd * 2048:(rnd + 1) * 2048],
                                 st[:, :]), f"sp{si % 2}", 16, label=("p3sp", si))
        for gr in range(4 * NT_ALL):
            og, r = divmod(gr, NT_ALL)
            em.w(y, "dve", ("p3vev", gr))
            em.i(y.dma_start(io["vf_sp"][r * 128:(r + 1) * 128, og * 512:(og + 1) * 512],
                             t["vstage"][gr % 2][:, :]), f"vs{gr % 2}", 16,
                 label=("p3vsp", gr))
        if self.dev:
            em.w(y, "dve", ("p2h1", NT_ALL - 1))
            em.i(y.dma_start(io["h1_tap"], t["xn_fm"][:, :]), "tap", 16)


    def _attn_unit_t_pass1(self, tt, u, cfg):
        em, t = self.em, self.t
        PS = t["PS"]
        nm = cfg["nm"]
        uid = u["uid"]
        if cfg.get("pre_sa") is not None:
            cfg["pre_sa"](tt, uid)
        for ks in range(8):
            b = ks % 4
            prev = (nm + "bm", uid, ks - 4) if ks >= 4 else (nm + "bm", uid - 1, ks + 4)
            em.wopt(tt, "dve", prev)
            inst = tt.matmul(PS[:, b * 512:(b + 1) * 512], u["lhsT"], u["krhs"](ks),
                             start=True, stop=True)
            em.i(inst, "pe", 1, label=(nm + "s1", uid, ks))

    def _attn_unit_t_main(self, tt, u, cfg):
        em, t = self.em, self.t
        PS = t["PS"]
        PSB = t["PSB"]
        nm = cfg["nm"]
        hd = cfg["hd"]
        avw = cfg["avw"]
        uid = u["uid"]
        par = uid % 2
        em.w(tt, "dve", (nm + "ng", uid))
        em.wopt(tt, "dve", (nm + "ngr", uid - 1))
        if cfg.get("nt_wait") is not None:
            cfg["nt_wait"](tt, uid)
        em.i(tt.transpose(PSB[0:1, par * 128:(par + 1) * 128], u["ngb"],
                          t["sident"][:, :]),
             "pe", 1, label=(nm + "nt", uid))
        for g in range(8):
            b2 = 2048 + (g % 2) * 512
            em.wopt(tt, "act", (nm + "ea", uid, g - 2))
            if g == 0:
                em.w(tt, "dve", (nm + "ngr", uid))
            tt.matmul(PS[:, b2:b2 + 512], t["sones_bf"][:, :],
                      u["ngrow4x"], start=True, stop=True)
            for k4 in range(4):
                kt = 4 * g + k4
                dst = PS[:, b2 + k4 * 128: b2 + (k4 + 1) * 128]
                inst = tt.matmul(dst, u["kt128"](kt), u["qrhs"], start=False,
                                 stop=False, skip_group_check=True)
            em.i(inst, "pe", 1, label=(nm + "s2", uid, g))
        avps = PS[:, 3072:3072 + avw]
        for g in range(8):
            if cfg["mask"]:
                em.w(tt, "dve", (nm + "pa", uid, g // 4))
            else:
                em.w(tt, "act", (nm + "ea", uid, g))
            if g == 0:
                em.wopt(tt, "dve", (nm + "oev", uid - 1))
            for k4 in range(4):
                kt = 4 * g + k4
                inst = tt.matmul(avps, u["ptbuf"][:, kt * 128:(kt + 1) * 128],
                                 u["vrhs"](kt), start=(kt == 0), stop=(kt == 31))
            if g == 7:
                em.i(inst, "pe", 1, label=(nm + "av", uid))

    def _attn_unit_v(self, v, u, cfg):
        em, t = self.em, self.t
        PS = t["PS"]
        PSB = t["PSB"]
        nm = cfg["nm"]
        hd = cfg["hd"]
        avw = cfg["avw"]
        uid = u["uid"]
        par = uid % 2
        sb = u["sb"]
        for ks in range(8):
            em.w(v, "pe", (nm + "s1", uid, ks))
            inst = v.reduce_max(sb["mad"][:, ks:ks + 1],
                                PS[:, (ks % 4) * 512:(ks % 4 + 1) * 512], axis=AX.X)
            em.i(inst, "dve", 1, label=(nm + "bm", uid, ks))
        em.w(v, "dve", (nm + "bm", uid, 7))
        em.selfsem(v, v.reduce_max(sb["m"], sb["mad"][:, 0:8], axis=AX.X), "vsf")
        em.i(v.tensor_scalar_mul(u["ngb"], sb["m"], -1.0), "dve", 1,
             label=(nm + "ng", uid))

    def _attn_unit_v_ngr(self, v, u, cfg):
        em, t = self.em, self.t
        PSB = t["PSB"]
        nm = cfg["nm"]
        uid = u["uid"]
        par = uid % 2
        em.w(v, "pe", (nm + "nt", uid))
        dst = u["ngrow4x"]
        src_ap = bass.AP(PSB, par * 128, [[512, 1], [0, 4], [1, 128]])
        em.i(v.tensor_copy(dst[0:1, :].rearrange("p (r q) -> p r q", r=4), src_ap),
             "dve", 1, label=(nm + "ngr", uid))

    def _attn_unit_v_post(self, v, u, cfg):
        em, t = self.em, self.t
        PS = t["PS"]
        nm = cfg["nm"]
        hd = cfg["hd"]
        avw = cfg["avw"]
        uid = u["uid"]
        sb = u["sb"]
        if cfg["mask"]:
            for hh in range(2):
                em.w(v, "act", (nm + "ea", uid, 4 * hh + 3))
                em.w(v, cfg["mkld_sem"](uid), (nm + "mk", uid))
                em.i(v.tensor_mul(u["ptbuf"][:, hh * 2048:(hh + 1) * 2048],
                                  u["ptbuf"][:, hh * 2048:(hh + 1) * 2048],
                                  u["mask"][:, hh * 2048:(hh + 1) * 2048]),
                     "dve", 1, label=(nm + "pa", uid, hh))
        em.w(v, "pe", (nm + "av", uid))
        em.selfsem(v, v.reciprocal(sb["il"], PS[:, 3072 + hd:3072 + hd + 1]), "vsf")
        if cfg.get("oev_wait") is not None:
            cfg["oev_wait"](v, uid)
        em.i(v.tensor_scalar_mul(u["odst"], PS[:, 3072:3072 + hd], sb["il"]),
             "dve", 1, label=(nm + "oev", uid))

    def _attn_unit_s(self, s, u, cfg):
        em, t = self.em, self.t
        PS = t["PS"]
        nm = cfg["nm"]
        sc = cfg["scale"]
        uid = u["uid"]
        for g in range(8):
            em.w(s, "pe", (nm + "s2", uid, g))
            if g == 0:
                em.wopt(s, "pe", (nm + "av", uid - 2))
            b2 = 2048 + (g % 2) * 512
            em.i(s.activation(u["ptbuf"][:, g * 512:(g + 1) * 512],
                              PS[:, b2:b2 + 512], AF.Exp, bias=0.0, scale=sc),
                 "act", 1, label=(nm + "ea", uid, g))

    def alloc_p4(self, st):
        nc, t = self.nc, self.t
        e = st.enter_context
        t["PSB"] = e(nc.psum_tensor("PSB4", [128, 512], BF16))
        t["ska"] = e(nc.sbuf_tensor("ska", [128, 4 * SEQ], BF16))
        t["sva"] = e(nc.sbuf_tensor("sva", [128, 32 * 520], BF16))
        t["swo"] = e(nc.sbuf_tensor("swo", [128, ND * D], BF16))
        t["sq"] = [e(nc.sbuf_tensor(f"sq{i}", [128, ND * 128], BF16)) for i in range(2)]
        t["ptbuf"] = [e(nc.sbuf_tensor(f"ptbuf{i}", [128, SEQ], BF16)) for i in range(2)]
        t["otm"] = e(nc.sbuf_tensor("otm", [128, D], BF16))
        t["ofmt"] = e(nc.sbuf_tensor("ofmt", [128, ND * 128], BF16))
        t["aostage"] = [e(nc.sbuf_tensor("aostage0", [128, D], F32))] * 2
        t["ast"] = [e(nc.sbuf_tensor(f"ast{i}", [128, 16], F32)) for i in range(2)]
        t["ngb4"] = e(nc.sbuf_tensor("ngb4", [128, 2], BF16))
        t["ngrow4"] = e(nc.sbuf_tensor("ngrow4", [1, 1024], BF16))

    def _p4_cfg(self):
        em = self.em

        def pre_sa(tt, uid):
            if uid % NH_A == 1:
                for og in range(4):
                    em.wopt(tt, "dve", ("p4aoev", uid // NH_A - 1, og))

        def oev_wait(v, uid):
            em.wopt(v, "pe", ("p4oft", uid // NH_A - 1, 3))

        def nt_wait(tt, uid):
            if uid % NH_A == 0:
                em.wopt(tt, "dve", ("p4oftev", uid // NH_A - 1, 3))

        return dict(nm="A4", hd=HD_A, avw=HD_A + 1, scale=SC_A, mask=False,
                    pre_sa=pre_sa, oev_wait=oev_wait, mkld_sem=None, nt_wait=nt_wait)

    def _p4_unit(self, qt, h):
        t = self.t
        uid = qt * NH_A + h
        par = uid % 2
        b = (h // 4) % 2
        c = 4 * (h // 8) + (h % 8) % 4
        jc = (h // 4) // 2
        j = h // 4
        sq = t["sq"][qt % 2]
        ska, sva = t["ska"], t["sva"]
        st = t["ast"][par]
        sb = dict(m=st[:, 8:9], il=st[:, 9:10], mad=st[:, 0:8])
        return dict(
            uid=uid, par=par,
            lhsT=sq[b * 64:(b + 1) * 64, c * 128:(c + 1) * 128],
            qrhs=sq[b * 64:(b + 1) * 64, c * 128:(c + 1) * 128],
            krhs=lambda ks: ska[b * 64:(b + 1) * 64,
                                jc * SEQ + ks * 512: jc * SEQ + (ks + 1) * 512],
            kt128=lambda kt: ska[b * 64:(b + 1) * 64,
                                 jc * SEQ + kt * 128: jc * SEQ + (kt + 1) * 128],
            vrhs=lambda kt: sva[:, kt * 520 + j * 65: kt * 520 + (j + 1) * 65],
            ngb=t["ngb4"][:, par:par + 1],
            ngrow4x=t["ngrow4"][0:1, par * 512:(par + 1) * 512],
            ptbuf=t["ptbuf"][par],
            odst=t["otm"][:, h * 64:(h + 1) * 64],
            sb=sb, mask=None)

    def p4_y(self, y):
        em, t, io = self.em, self.t, self.io
        em.join(y, "p3")
        em.i(y.dma_start(t["ska"][:, :].rearrange("p (c k) -> p c k", c=4),
                         io["ka_sp"].rearrange("c p k -> p c k")), "c4", 16)
        for j in range(8):
            em.i(y.dma_start(
                bass.AP(t["sva"], j * 65, [[32 * 520, 128], [520, 32], [1, 64]]),
                io["va_sp"].rearrange("(c p) (j i) -> p c j i", p=128, j=8)[:, :, j, :]),
                "c4", 16)
        em.mark("c4", "p4kv")
        for qt in range(NT_OWN):
            if qt >= 2:
                em.w(y, "pe", ("A4s1", (qt - 1) * NH_A, 0))
            em.i(y.dma_start(t["sq"][qt % 2][:, :].rearrange("p (c q) -> p c q", c=ND),
                             io["qa_sp"][:, :, qt * 128:(qt + 1) * 128]
                             .rearrange("c p q -> p c q")),
                 f"q{qt % 2}", 16, label=("p4q", qt))
            if qt >= 1:
                for og in range(4):
                    em.w(y, "dve", ("p4aoev", qt - 1, og))
                em.i(y.dma_start(io["ao_sp"][(qt - 1) * 128:qt * 128, :],
                                 t["aostage"][(qt - 1) % 2][:, :]), "ao4", 16,
                     label=("p4aosp", qt - 1))
        for og in range(4):
            em.w(y, "dve", ("p4aoev", NT_OWN - 1, og))
        em.i(y.dma_start(io["ao_sp"][(NT_OWN - 1) * 128:NT_OWN * 128, :],
                         t["aostage"][(NT_OWN - 1) % 2][:, :]), "ao4", 16,
             label=("p4aosp", NT_OWN - 1))

    def p4_g(self, g):
        em, t, io = self.em, self.t, self.io
        em.join(g, "p3")
        for dc in range(ND):
            em.i(g.dma_start(t["swo"][:, dc * D:(dc + 1) * D],
                             io["Wo"][dc * 128:(dc + 1) * 128, :]),
                 "wo4", 16, label=("p4wo", dc))
        em.w(g, "c4", "p4kv")
        em.i(g.memset(bass.AP(t["sva"], 64, [[32 * 520, 128], [520, 32], [65, 8], [1, 1]]),
                      1.0), "pool", 1, label="p4ones")

    def p4_t(self, tt):
        em, t = self.em, self.t
        cfg = self._p4_cfg()
        em.join(tt, "p3")
        em.w(tt, "c4", "p4kv")
        em.w(tt, "pool", "p4ones")
        em.w(tt, "wo4", ("p4wo", ND - 1))
        PS = t["PS"]
        em.w(tt, f"q{0}", ("p4q", 0))
        self._attn_unit_t_pass1(tt, self._p4_unit(0, 0), cfg)
        for qt in range(NT_OWN):
            for h in range(NH_A):
                uid = qt * NH_A + h
                nqt, nh = divmod(uid + 1, NH_A)
                if nqt < NT_OWN:
                    if nh == 0:
                        em.w(tt, f"q{nqt % 2}", ("p4q", nqt))
                    self._attn_unit_t_pass1(tt, self._p4_unit(nqt, nh), cfg)
                self._attn_unit_t_main(tt, self._p4_unit(qt, h), cfg)
            em.w(tt, "dve", ("A4oev", qt * NH_A + NH_A - 1))
            em.wopt(tt, "dve", ("A4ngr", qt * NH_A + NH_A - 1))
            for rnd in range(4):
                if rnd > 0:
                    em.w(tt, "dve", ("p4oftev", qt, rnd - 1))
                elif qt > 0:
                    em.w(tt, "dve", ("p4oftev", qt - 1, 3))
                for k in range(4):
                    inst = tt.transpose(
                        t["PSB"][:, k * 128:(k + 1) * 128],
                        t["otm"][:, (rnd * 4 + k) * 128:(rnd * 4 + k + 1) * 128],
                        t["sident"][:, :])
                em.i(inst, "pe", 1, label=("p4oft", qt, rnd))
            em.w(tt, "dve", ("p4oftev", qt, 3))
            for og in range(4):
                em.wopt(tt, "dve", ("p4aoev", qt - 1, og))
                em.wopt(tt, "dve", ("A4bm", qt * NH_A + NH_A - 1, og))
                em.wopt(tt, "dve", ("A4bm", qt * NH_A + NH_A - 1, og + 4))
                em.wopt(tt, "dve", ("A4bm", (qt + 1) * NH_A, og))
                em.wopt(tt, "dve", ("A4bm", (qt + 1) * NH_A, og + 4))
                inst = tt.matmul(PS[:, og * 512:(og + 1) * 512], t["sones_f"][:, :],
                                 t["bo_row"][:, og * 512:(og + 1) * 512],
                                 start=True, stop=False)
                for dc in range(ND):
                    inst = tt.matmul(PS[:, og * 512:(og + 1) * 512],
                                     t["ofmt"][:, dc * 128:(dc + 1) * 128],
                                     t["swo"][:, dc * D + og * 512: dc * D + (og + 1) * 512],
                                     start=False, stop=(dc == ND - 1))
                em.i(inst, "pe", 1, label=("p4wom", qt, og))

    def p4_v(self, v):
        em, t = self.em, self.t
        cfg = self._p4_cfg()
        em.join(v, "p3")
        self._attn_unit_v(v, self._p4_unit(0, 0), cfg)
        for qt in range(NT_OWN):
            for h in range(NH_A):
                uid = qt * NH_A + h
                nqt, nh = divmod(uid + 1, NH_A)
                if nqt < NT_OWN:
                    self._attn_unit_v(v, self._p4_unit(nqt, nh), cfg)
                self._attn_unit_v_ngr(v, self._p4_unit(qt, h), cfg)
                self._attn_unit_v_post(v, self._p4_unit(qt, h), cfg)
            for rnd in range(4):
                em.w(v, "pe", ("p4oft", qt, rnd))
                em.i(v.tensor_copy(t["ofmt"][:, rnd * 512:(rnd + 1) * 512],
                                   t["PSB"][:, :]),
                     "dve", 1, label=("p4oftev", qt, rnd))
            em.wopt(v, "ao4", ("p4aosp", qt - 1))
            for og in range(4):
                em.w(v, "pe", ("p4wom", qt, og))
                em.i(v.tensor_copy(t["aostage"][qt % 2][:, og * 512:(og + 1) * 512],
                                   t["PS"][:, og * 512:(og + 1) * 512]),
                     "dve", 1, label=("p4aoev", qt, og))

    def p4_s(self, s):
        em = self.em
        cfg = self._p4_cfg()
        em.join(s, "p3")
        for qt in range(NT_OWN):
            for h in range(NH_A):
                self._attn_unit_s(s, self._p4_unit(qt, h), cfg)

    def alloc_p5(self, st):
        nc, t = self.nc, self.t
        e = st.enter_context
        t["PSB"] = e(nc.psum_tensor("PSB5", [128, 512], BF16))
        t["skf"] = [e(nc.sbuf_tensor(f"skf{i}", [128, SEQ], BF16)) for i in range(2)]
        t["svf"] = [e(nc.sbuf_tensor(f"svf{i}", [128, 32 * 129], BF16)) for i in range(2)]
        t["sqf"] = [e(nc.sbuf_tensor(f"sqf{i}", [128, OWN], BF16)) for i in range(2)]
        t["ptbuf"] = [e(nc.sbuf_tensor(f"p5ptbuf{i}", [128, SEQ], BF16)) for i in range(2)]
        t["mask_t"] = [e(nc.sbuf_tensor(f"mask{i}", [128, SEQ], BF16)) for i in range(2)]
        t["avtm"] = e(nc.sbuf_tensor("avtm", [128, 128], BF16))
        t["fst"] = [e(nc.sbuf_tensor(f"fst{i}", [128, 16], F32)) for i in range(2)]
        t["ngb5"] = e(nc.sbuf_tensor("ngb5", [128, 2], BF16))
        t["ngrow5"] = e(nc.sbuf_tensor("ngrow5", [1, 1024], BF16))

    def _p5_cfg(self):
        em = self.em

        def oev_wait(v, uid):
            em.wopt(v, "pe", ("p5ot", uid - 1))

        def nt_wait(tt, uid):
            em.wopt(tt, "dve", ("p5otev", uid - 1))

        return dict(nm="F5", hd=HD_F, avw=HD_F + 1, scale=SC_F, mask=True,
                    pre_sa=None, oev_wait=oev_wait,
                    mkld_sem=lambda uid: f"mk{uid % 2}", nt_wait=nt_wait)

    def _p5_unit(self, h, qt):
        t = self.t
        uid = h * NT_OWN + qt
        par = uid % 2
        st = t["fst"][par]
        sb = dict(m=st[:, 8:9], il=st[:, 9:10], mad=st[:, 0:8])
        skf = t["skf"][h % 2]
        svf = t["svf"][h % 2]
        return dict(
            uid=uid, par=par,
            lhsT=t["sqf"][h % 2][:, qt * 128:(qt + 1) * 128],
            qrhs=t["sqf"][h % 2][:, qt * 128:(qt + 1) * 128],
            krhs=lambda ks: skf[:, ks * 512:(ks + 1) * 512],
            kt128=lambda kt: skf[:, kt * 128:(kt + 1) * 128],
            vrhs=lambda kt: svf[:, kt * 129:(kt + 1) * 129],
            ngb=t["ngb5"][:, par:par + 1],
            ngrow4x=t["ngrow5"][0:1, par * 512:(par + 1) * 512],
            ptbuf=t["ptbuf"][par],
            odst=t["avtm"][:, :],
            sb=sb, mask=t["mask_t"][par])

    def p5_y(self, y):
        em, t, io = self.em, self.t, self.io
        em.join(y, "p4")
        for h in range(NH_F):
            if h >= 2:
                em.w(y, "pe", ("F5s1", (h - 1) * NT_OWN, 0))
            em.i(y.dma_start(t["skf"][h % 2][:, :], io["kf_sp"][h, :, :]),
                 f"kv{h % 2}", 16)
            em.i(y.dma_start(
                t["svf"][h % 2][:, :].rearrange("p (c k) -> p c k", c=32)[:, :, 0:128],
                io["vf_sp"][:, h * 128:(h + 1) * 128]
                .rearrange("(c p) k -> p c k", p=128)),
                 f"kv{h % 2}", 16)
            em.i(y.dma_start(t["sqf"][h % 2][:, :], io["qf_sp"][h, :, :]),
                 f"kv{h % 2}", 16, label=("p5kv", h))
            for qt in range(NT_OWN):
                uid = h * NT_OWN + qt
                em.wopt(y, "dve", ("F5pa", uid - 2, 1))
                em.i(y.dma_start(t["mask_t"][uid % 2][:, :], io["mask_sp"][qt, :, :]),
                     f"mk{uid % 2}", 16, label=("F5mk", uid))
        if self.dev:
            em.w(y, "dve", ("p5otev", NH_F * NT_OWN - 1))
            em.i(y.dma_start(io["ofm_tap"], t["OFM"][:, :]), "tap", 16)

    def p5_g(self, p):
        em, t = self.em, self.t
        em.join(p, "p4")
        for i in range(2):
            em.i(p.memset(t["svf"][i][:, :].rearrange("p (c k) -> p c k", c=32)[:, :, 128:129],
                          1.0), "pool", 1, label=("p5ones", i))

    def p5_t(self, tt):
        em, t = self.em, self.t
        cfg = self._p5_cfg()
        em.join(tt, "p4")
        em.w(tt, "pool", ("p5ones", 1))
        em.w(tt, "kv0", ("p5kv", 0))
        self._attn_unit_t_pass1(tt, self._p5_unit(0, 0), cfg)
        for h in range(NH_F):
            for qt in range(NT_OWN):
                uid = h * NT_OWN + qt
                u = self._p5_unit(h, qt)
                nh, nqt = divmod(uid + 1, NT_OWN)
                if nh < NH_F:
                    if nqt == 0:
                        em.w(tt, f"kv{nh % 2}", ("p5kv", nh))
                    self._attn_unit_t_pass1(tt, self._p5_unit(nh, nqt), cfg)
                self._attn_unit_t_main(tt, u, cfg)
                em.w(tt, "dve", ("F5oev", uid))
                em.wopt(tt, "dve", ("p5otev", uid - 1))
                em.i(tt.transpose(t["PSB"][:, 256:384], t["avtm"][:, :],
                                  t["sident"][:, :]),
                     "pe", 1, label=("p5ot", uid))

    def p5_v(self, v):
        em, t = self.em, self.t
        cfg = self._p5_cfg()
        em.join(v, "p4")
        ofm = t["OFM"]
        self._attn_unit_v(v, self._p5_unit(0, 0), cfg)
        for h in range(NH_F):
            for qt in range(NT_OWN):
                uid = h * NT_OWN + qt
                u = self._p5_unit(h, qt)
                nh, nqt = divmod(uid + 1, NT_OWN)
                if nh < NH_F:
                    self._attn_unit_v(v, self._p5_unit(nh, nqt), cfg)
                self._attn_unit_v_ngr(v, u, cfg)
                self._attn_unit_v_post(v, u, cfg)
                em.w(v, "pe", ("p5ot", uid))
                em.i(v.tensor_copy(ofm[:, h * OWN + qt * 128: h * OWN + (qt + 1) * 128],
                                   t["PSB"][:, 256:384]),
                     "dve", 1, label=("p5otev", uid))

    def p5_s(self, s):
        em = self.em
        cfg = self._p5_cfg()
        em.join(s, "p4")
        for h in range(NH_F):
            for qt in range(NT_OWN):
                self._attn_unit_s(s, self._p5_unit(h, qt), cfg)

    def alloc_p6(self, st):
        nc, t = self.nc, self.t
        e = st.enter_context
        t["PTB"] = e(nc.psum_tensor("PTB6", [128, 1024], BF16))
        t["swo6"] = e(nc.sbuf_tensor("swo6", [128, ND * D], BF16))
        t["sxf6"] = [e(nc.sbuf_tensor(f"sxf6{i}", [128, D], F32)) for i in range(2)]
        t["xn6"] = [e(nc.sbuf_tensor(f"xn6{i}", [128, D], F32)) for i in range(2)]
        t["x26"] = [e(nc.sbuf_tensor(f"x26{i}", [128, D], F32)) for i in range(2)]
        t["tt6"] = e(nc.sbuf_tensor("tt6", [128, D], BF16))
        t["Lb"] = e(nc.sbuf_tensor("Lb", [128, 16], F32))
        t["curb"] = e(nc.sbuf_tensor("curb", [128, 16], F32))
        t["maskb"] = e(nc.sbuf_tensor("maskb", [128, 16], F32))
        t["Eb"] = e(nc.sbuf_tensor("Eb", [128, 16], F32))
        t["mjs"] = e(nc.sbuf_tensor("mjs", [128, 8], F32))
        t["g6"] = e(nc.sbuf_tensor("g6", [128, 8], F32))
        t["junk"] = e(nc.sbuf_tensor("junk6", [128, 2048], F32))

    def p6_g(self, g):
        em, t, io = self.em, self.t, self.io
        em.join(g, "p5")
        for dc in range(ND):
            em.i(g.dma_start(t["swo6"][:, dc * D:(dc + 1) * D],
                             io["wo"][dc * 128:(dc + 1) * 128, :]),
                 "wo6", 16, label=("p6wo", dc))

    def p6_y(self, y):
        em, t, io = self.em, self.t, self.io
        em.join(y, "p5")
        for qt in range(NT_OWN):
            if qt >= 2:
                em.w(y, "act", ("p6sq", qt - 2))
            em.i(y.dma_start(t["sxf6"][qt % 2][:, :], io["x_own"][qt * 128:(qt + 1) * 128, :]),
                 f"x{qt % 2}", 16, label=("p6x", qt))
            if qt >= 1:
                em.w(y, "dve", ("p6x2", qt - 1, 3))
                em.i(y.dma_start(io["x2_sp"][(qt - 1) * 128:qt * 128, :],
                                 t["x26"][(qt - 1) % 2][:, :]),
                     f"x2s{(qt - 1) % 2}", 16, label=("p6x2sp", qt - 1))
        em.w(y, "dve", ("p6x2", NT_OWN - 1, 3))
        em.i(y.dma_start(io["x2_sp"][(NT_OWN - 1) * 128:NT_OWN * 128, :],
                         t["x26"][(NT_OWN - 1) % 2][:, :]),
             f"x2s{(NT_OWN - 1) % 2}", 16, label=("p6x2sp", NT_OWN - 1))
        if self.dev:
            em.w(y, "dve", ("p6cw", NT_OWN - 1))
            em.i(y.dma_start(io["tfm_tap"], t["OFM"][:, :]), "tap", 16)
            em.i(y.dma_start(io["cw_tap"], t["s_cw"][:, :]), "tap", 16)

    def p6_t(self, tt):
        em, t = self.em, self.t
        em.join(tt, "p5")
        PS = t["PS"]
        buf = t["OFM"]
        em.w(tt, "wo6", ("p6wo", ND - 1))
        for qt in range(NT_OWN):
            for og in range(4):
                em.wopt(tt, "dve", ("p6x2", qt - 1, og))
                for dc in range(ND):
                    inst = tt.matmul(PS[:, og * 512:(og + 1) * 512],
                                     buf[:, dc * OWN + qt * 128: dc * OWN + (qt + 1) * 128],
                                     t["swo6"][:, dc * D + og * 512: dc * D + (og + 1) * 512],
                                     start=(dc == 0), stop=(dc == ND - 1))
                em.i(inst, "pe", 1, label=("p6wom", qt, og))
            em.w(tt, "act", ("p6tt", qt))
            for half in range(2):
                if half == 0:
                    em.wopt(tt, "dve", ("p6ttev", qt - 1, 1))
                else:
                    em.w(tt, "dve", ("p6ttev", qt, 0))
                for k in range(8):
                    inst = tt.transpose(
                        t["PTB"][:, k * 128:(k + 1) * 128],
                        t["tt6"][:, (half * 8 + k) * 128:(half * 8 + k + 1) * 128],
                        t["sident"][:, :])
                em.i(inst, "pe", 1, label=("p6ttr", qt, half))
            em.w(tt, "dve", ("p6ttev", qt, 1))
            em.wopt(tt, "dve", ("p6gev", qt - 1))
            for dc in range(ND):
                inst = tt.matmul(PS[:, 2048:2064],
                                 buf[:, dc * OWN + qt * 128: dc * OWN + (qt + 1) * 128],
                                 t["sgate"][:, dc * 16:(dc + 1) * 16],
                                 start=(dc == 0), stop=(dc == ND - 1))
            em.i(inst, "pe", 1, label=("p6gate", qt))

    def p6_s(self, s):
        em, t = self.em, self.t
        em.join(s, "p5")
        for qt in range(NT_OWN):
            em.w(s, f"x{qt % 2}", ("p6x", qt))
            em.wopt(s, "dve", ("p6x2", qt - 2, 3))
            em.i(s.activation(t["xn6"][qt % 2][:, :], t["sxf6"][qt % 2][:, :], AF.Identity,
                              bias=t["bias2_s"][:, qt:qt + 1], scale=t["rstd_s"][:, qt:qt + 1]),
                 "act", 1, label=("p6sq", qt))
            em.w(s, "dve", ("p6x2", qt, 3))
            em.selfsem(s, s.activation(t["junk"][:, :], t["x26"][qt % 2][:, :], AF.Square,
                                       bias=0.0, scale=1.0,
                                       accum_out=t["g6"][:, 0:1]), "ssf")
            em.selfsem(s, s.activation(t["g6"][:, 1:2], t["g6"][:, 0:1], AF.Ln,
                                       bias=EPS_RMS, scale=1.0 / D), "ssf")
            em.selfsem(s, s.activation(t["g6"][:, 2:3], t["g6"][:, 1:2], AF.Exp,
                                       bias=0.0, scale=-0.5), "ssf")
            em.wopt(s, "pe", ("p6ttr", qt - 1, 1))
            em.i(s.activation(t["tt6"][:, :], t["x26"][qt % 2][:, :], AF.Identity,
                              bias=0.0, scale=t["g6"][:, 2:3]),
                 "act", 1, label=("p6tt", qt))
            em.w(s, "dve", ("p6negm1", qt))
            em.i(s.activation(t["Eb"][:, :], t["Lb"][:, :], AF.Exp,
                              bias=t["g6"][:, 4 + qt % 2: 5 + qt % 2], scale=1.0),
                 "act", 1, label=("p6exp", qt))

    def p6_v(self, v):
        em, t = self.em, self.t
        em.join(v, "p5")
        PS = t["PS"]
        tfm = t["OFM"]
        for qt in range(NT_OWN):
            em.wopt(v, "act", ("p6tt", qt - 2))
            em.wopt(v, f"x2s{qt % 2}", ("p6x2sp", qt - 2))
            for og in range(4):
                em.w(v, "pe", ("p6wom", qt, og))
                em.w(v, "act", ("p6sq", qt))
                em.i(v.scalar_tensor_tensor(
                    t["x26"][qt % 2][:, og * 512:(og + 1) * 512],
                    PS[:, og * 512:(og + 1) * 512], 1.0,
                    t["xn6"][qt % 2][:, og * 512:(og + 1) * 512], ALU.mult, ALU.add),
                    "dve", 1, label=("p6x2", qt, og))
            for half in range(2):
                em.w(v, "pe", ("p6ttr", qt, half))
                for k in range(8):
                    dc = half * 8 + k
                    inst = v.tensor_scalar_mul(
                        tfm[:, dc * OWN + qt * 128: dc * OWN + (qt + 1) * 128],
                        t["PTB"][:, k * 128:(k + 1) * 128],
                        t["ln2_col"][:, dc:dc + 1])
                em.i(inst, "dve", 1, label=("p6ttev", qt, half))
            em.w(v, "pe", ("p6gate", qt))
            em.selfsem(v, v.tensor_copy(t["Lb"][:, :], PS[:, 2048:2064]), "vsf")
            em.selfsem(v, v.tensor_copy(t["curb"][:, :], t["Lb"][:, :]), "vsf")
            em.mark("dve", ("p6gev", qt))
            for j in range(6):
                em.selfsem(v, v.reduce_max(t["mjs"][:, j:j + 1], t["curb"][:, :], axis=AX.X),
                           "vsf")
                if j < 5:
                    em.selfsem(v, v.tensor_single_scalar(t["maskb"][:, :], t["curb"][:, :],
                                                         t["mjs"][:, j:j + 1], ALU.is_ge),
                               "vsf")
                    em.selfsem(v, v.scalar_tensor_tensor(t["curb"][:, :], t["maskb"][:, :],
                                                         -1e30, t["curb"][:, :],
                                                         ALU.mult, ALU.add), "vsf")
            em.selfsem(v, v.tensor_single_scalar(t["maskb"][:, :], t["Lb"][:, :],
                                                 t["mjs"][:, 5:6], ALU.is_ge), "vsf")
            em.wopt(v, "act", ("p6exp", qt - 2))
            em.i(v.tensor_scalar_mul(t["g6"][:, 4 + qt % 2: 5 + qt % 2],
                                     t["mjs"][:, 0:1], -1.0),
                 "dve", 1, label=("p6negm1", qt))
            em.w(v, "act", ("p6exp", qt))
            em.selfsem(v, v.tensor_mul(t["Eb"][:, :], t["Eb"][:, :], t["maskb"][:, :]), "vsf")
            em.selfsem(v, v.reduce_sum(t["g6"][:, 6:7], t["Eb"][:, :], axis=AX.X), "vsf")
            em.selfsem(v, v.reciprocal(t["g6"][:, 7:8], t["g6"][:, 6:7]), "vsf")
            em.i(v.tensor_scalar_mul(t["s_cw"][:, qt * 16:(qt + 1) * 16], t["Eb"][:, :],
                                     t["g6"][:, 7:8]),
                 "dve", 1, label=("p6cw", qt))

    def alloc_p7(self, st):
        nc, t = self.nc, self.t
        e = st.enter_context
        t["PS1"] = e(nc.psum_tensor("PS1", [128, 512], F32))
        t["hh"] = e(nc.sbuf_tensor("hh", [128, SUBI * 1024], BF16))
        t["hsil"] = e(nc.sbuf_tensor("hsil", [128, 1024], BF16))
        t["seg"] = [e(nc.sbuf_tensor(f"seg{i}", [128, ND * 128], BF16)) for i in range(2)]
        t["seu"] = [e(nc.sbuf_tensor(f"seu{i}", [128, ND * 128], BF16)) for i in range(2)]
        t["sed"] = e(nc.sbuf_tensor("sed", [128, SUBI * D], BF16))
        t["acc"] = e(nc.sbuf_tensor("acc", [128, 8 * D], F32))
        t["fxs"] = e(nc.sbuf_tensor("fxs", [128, D], F32))
        self.p7gu = []
        self.p7dn = []
        gix = 0
        dix = 0
        for half in range(2):
            for e_ in range(NEXP + 1):
                ni = NI_E if e_ < NEXP else NI_S
                subs = []
                s0 = 0
                while s0 < ni:
                    subs.append((s0, min(SUBI, ni - s0)))
                    s0 += SUBI
                for si, (s0, slen) in enumerate(subs):
                    for k in range(slen):
                        self.p7gu.append((half, e_, si, s0 + k, gix))
                        gix += 1
                    self.p7dn.append((half, e_, si, s0, slen, dix))
                    dix += 1
        self.p7subs = {}
        for half, e_, si, s0, slen, dix_ in self.p7dn:
            self.p7subs.setdefault((half, e_), []).append((si, s0, slen, dix_))

    def _wsl(self, name, e_, oc):
        io = self.io
        if e_ < NEXP:
            w = io["egT"] if name == "g" else io["euT"]
            return w[e_, :, oc * 128:(oc + 1) * 128].rearrange("(c p) o -> p c o", p=128)
        w = io["sgT"] if name == "g" else io["suT"]
        return w[:, oc * 128:(oc + 1) * 128].rearrange("(c p) o -> p c o", p=128)

    def _wdl(self, e_, ic):
        io = self.io
        if e_ < NEXP:
            return io["edT"][e_, ic * 128:(ic + 1) * 128, :]
        return io["sdT"][ic * 128:(ic + 1) * 128, :]

    def p7_g(self, g):
        em, t, io = self.em, self.t, self.io
        em.join(g, "p6")
        dn_by_gix = {}
        for half, e_, si, s0, slen, dix in self.p7dn:
            dn_by_gix[(half, e_, si)] = (s0, slen, dix)
        for half in range(2):
            for half2, e_, si, oc, gix in self.p7gu:
                if half2 != half:
                    continue
                em.wopt(g, "pe", ("p7um", gix - 2))
                em.i(g.dma_start(t["seg"][gix % 2][:, :].rearrange("p (c o) -> p c o", c=ND),
                                 self._wsl("g", e_, oc)), f"gw{gix % 2}", 16)
                em.i(g.dma_start(t["seu"][gix % 2][:, :].rearrange("p (c o) -> p c o", c=ND),
                                 self._wsl("u", e_, oc)), f"gw{gix % 2}", 16,
                     label=("p7wu", gix))
                s0, slen, dix = dn_by_gix[(half, e_, si)]
                if oc == s0 + slen - 1:
                    em.wopt(g, "pe", ("p7dnl", dix - 1))
                    for k in range(slen):
                        em.i(g.dma_start(t["sed"][:, k * D:(k + 1) * D],
                                         self._wdl(e_, s0 + k)),
                             "sed", 16, label=("p7wd", dix, k))
            for ttl in range(8):
                r = half * 8 + ttl
                em.w(g, "fx7", ("p7f1", half, ttl))
                em.i(g.dma_start(t["fxs"][:, :], io["ao_sp"][r * 128:(r + 1) * 128, :],
                                 accum_op=ALU.add), "ac7", 16, label=("p7ac1", half, ttl))
                em.w(g, "ac7", ("p7ac1", half, ttl))
                em.i(g.dma_start(t["fxs"][:, :], io["x2_sp"][r * 128:(r + 1) * 128, :],
                                 accum_op=ALU.add), "ac7", 16, label=("p7ac", half, ttl))

    def p7_t(self, tt):
        em, t = self.em, self.t
        em.join(tt, "p6")
        PS, PS1 = t["PS"], t["PS1"]
        tfm = t["OFM"]
        pg = [PS[:, 0:512], PS[:, 512:1024]]
        pu = [PS[:, 1024:1536], PS[:, 1536:2048]]
        pdown = [PS[:, 2048:2560], PS[:, 2560:3072], PS[:, 3072:3584], PS1[:, :]]

        gu_iter = iter(self.p7gu)
        for half, e_, subs_key in [(h, e, None) for h in range(2) for e in range(NEXP + 1)]:
            subs = self.p7subs[(half, e_)]
            for si, s0, slen, dix in subs:
                for k in range(slen):
                    half2, e2, si2, oc, gix = next(gu_iter)
                    assert (half2, e2, si2, oc) == (half, e_, si, s0 + k)
                    em.w(tt, f"gw{gix % 2}", ("p7wu", gix))
                    em.wopt(tt, "act", ("p7sil", gix - 1))
                    for gr in range(2):
                        for dc in range(ND):
                            col = dc * OWN + half * 1024 + gr * 512
                            inst = tt.matmul(pg[gr],
                                             t["seg"][gix % 2][:, dc * 128:(dc + 1) * 128],
                                             tfm[:, col:col + 512],
                                             start=(dc == 0), stop=(dc == ND - 1))
                    em.i(inst, "pe", 1, label=("p7gm", gix))
                    em.wopt(tt, "dve", ("p7hh", gix - 1))
                    for gr in range(2):
                        for dc in range(ND):
                            col = dc * OWN + half * 1024 + gr * 512
                            inst = tt.matmul(pu[gr],
                                             t["seu"][gix % 2][:, dc * 128:(dc + 1) * 128],
                                             tfm[:, col:col + 512],
                                             start=(dc == 0), stop=(dc == ND - 1))
                    em.i(inst, "pe", 1, label=("p7um", gix))
                em.w(tt, "sed", ("p7wd", dix, slen - 1))
                for ttl in range(8):
                    if ttl > 0:
                        em.w(tt, "dve", ("p7ev", dix, ttl - 1))
                    else:
                        em.wopt(tt, "dve", ("p7ev", dix - 1, 7))
                    for k in range(slen):
                        em.w(tt, "dve", ("p7hh", self._p7gix(half, e_, si, s0 + k)))
                        for dg in range(4):
                            inst = tt.matmul(
                                pdown[dg],
                                t["hh"][:, k * 1024 + ttl * 128: k * 1024 + ttl * 128 + 128],
                                t["sed"][:, k * D + dg * 512: k * D + (dg + 1) * 512],
                                start=(k == 0), stop=(k == slen - 1))
                    em.i(inst, "pe", 1, label=("p7dn", dix, ttl))
                em.mark("pe", ("p7dnl", dix))

    def _p7gix(self, half, e_, si, oc):
        if not hasattr(self, "_p7gixmap"):
            self._p7gixmap = {}
            for h2, e2, si2, oc2, gix in self.p7gu:
                self._p7gixmap[(h2, e2, si2, oc2)] = gix
        return self._p7gixmap[(half, e_, si, oc)]

    def p7_s(self, s):
        em, t = self.em, self.t
        em.join(s, "p6")
        for half, e_, si, oc, gix in self.p7gu:
            em.w(s, "pe", ("p7gm", gix))
            em.wopt(s, "dve", ("p7hh", gix - 1))
            em.i(s.activation(t["hsil"][:, :], t["PS"][:, 0:1024], AF.Silu,
                              bias=0.0, scale=1.0),
                 "act", 1, label=("p7sil", gix))

    def p7_v(self, v):
        em, t = self.em, self.t
        em.join(v, "p6")
        PS, PS1 = t["PS"], t["PS1"]
        pdown = [PS[:, 2048:2560], PS[:, 2560:3072], PS[:, 3072:3584], PS1[:, :]]
        gu_iter = iter(self.p7gu)
        first_writer = {}
        for half in range(2):
            for e_ in range(NEXP + 1):
                subs = self.p7subs[(half, e_)]
                for si, s0, slen, dix in subs:
                    for k in range(slen):
                        half2, e2, si2, oc, gix = next(gu_iter)
                        em.w(v, "act", ("p7sil", gix))
                        em.w(v, "pe", ("p7um", gix))
                        em.i(v.tensor_mul(t["hh"][:, k * 1024:(k + 1) * 1024],
                                          t["hsil"][:, :], PS[:, 1024:2048]),
                             "dve", 1, label=("p7hh", gix))
                    for ttl in range(8):
                        em.w(v, "pe", ("p7dn", dix, ttl))
                        em.wopt(v, "dve", ("p7ev", dix - 1, ttl))
                        cwcol = 256 if e_ == NEXP else (half * 8 + ttl) * 16 + e_
                        fw = first_writer.get((half, ttl))
                        for dg in range(4):
                            if fw is None:
                                inst = v.tensor_scalar_mul(
                                    t["acc"][:, ttl * D + dg * 512: ttl * D + (dg + 1) * 512],
                                    pdown[dg], t["s_cw"][:, cwcol:cwcol + 1])
                            else:
                                inst = v.scalar_tensor_tensor(
                                    t["acc"][:, ttl * D + dg * 512: ttl * D + (dg + 1) * 512],
                                    pdown[dg], t["s_cw"][:, cwcol:cwcol + 1],
                                    t["acc"][:, ttl * D + dg * 512: ttl * D + (dg + 1) * 512],
                                    ALU.mult, ALU.add)
                        first_writer[(half, ttl)] = True
                        em.i(inst, "dve", 1, label=("p7ev", dix, ttl))
            for ttl in range(8):
                last_dix = self.p7subs[(half, NEXP)][-1][3]
                em.w(v, "ac7", ("p7ac", half, ttl))
                em.w(v, "dve", ("p7ev", last_dix, ttl))
                em.i(v.tensor_add(t["fxs"][:, :], t["fxs"][:, :],
                                  t["acc"][:, ttl * D:(ttl + 1) * D]),
                     "dve", 1, label=("p7out", half, ttl))
            first_writer = {}

    def p7_y(self, y):
        em, t, io = self.em, self.t, self.io
        em.join(y, "p6")
        for half in range(2):
            for ttl in range(8):
                r = half * 8 + ttl
                if r >= 1:
                    em.w(y, "st7", ("p7st", (r - 1) // 8, (r - 1) % 8))
                em.i(y.dma_start(t["fxs"][:, :], io["x_own"][r * 128:(r + 1) * 128, :]),
                     "fx7", 16, label=("p7f1", half, ttl))
                em.w(y, "dve", ("p7out", half, ttl))
                em.i(y.dma_start(io["out"][r * 128:(r + 1) * 128, :], t["fxs"][:, :]),
                     "st7", 16, label=("p7st", half, ttl))
        em.w(y, "st7", ("p7st", 1, 7))

    def build(self, nphases=99):
        self.declare_dram()
        nc, em = self.nc, self.em
        with ExitStack() as persist:
            self.alloc_persist(persist)
            em.attach(nc, persist)
            block = persist.enter_context(nc.Block())
            mocks = {k: MockEng() for k in "tvsgy"}

            def run_phase(name, fns, allocf=None, pre=None, post=None):
                with ExitStack() as st:
                    if allocf:
                        allocf(st)
                    if pre:
                        pre()
                    em.p1()
                    for key in ("t", "v", "s", "g", "y"):
                        if key in fns:
                            fns[key](mocks[key])
                    em.endphase(name)
                    em.p2()
                    for key, section in (("t", block.tensor), ("v", block.vector),
                                         ("s", block.scalar), ("g", block.gpsimd),
                                         ("y", block.sync)):
                        if key in fns:
                            section(fns[key])
                    em.endphase(name)
                if post:
                    post()

            plist = [
                ("p0", lambda: run_phase("p0", dict(v=self.p0_v, g=self.p0_g, y=self.p0_y))),
                ("p1", lambda: (self.big_alloc("xn_fm", (128, ND * SEQ), BF16),
                                run_phase("p1", dict(t=self.p1_t, v=self.p1_v, s=self.p1_s,
                                                     g=self.p1_g, y=self.p1_y), self.alloc_p1))),
                ("p2", lambda: run_phase("p2", dict(t=self.p2_t, v=self.p2_v, s=self.p2_s,
                                                    y=self.p2_y), self.alloc_p2)),
                ("p3", lambda: run_phase("p3", dict(t=self.p3_t, v=self.p3_v, g=self.p3_g,
                                                    y=self.p3_y), self.alloc_p3,
                                         post=lambda: self.big_free("xn_fm"))),
                ("p4", lambda: run_phase("p4", dict(t=self.p4_t, v=self.p4_v, s=self.p4_s,
                                                    g=self.p4_g, y=self.p4_y), self.alloc_p4)),
                ("p5", lambda: (self.big_alloc("OFM", (128, NH_F * OWN), BF16),
                                run_phase("p5", dict(t=self.p5_t, v=self.p5_v, s=self.p5_s,
                                                     g=self.p5_g, y=self.p5_y), self.alloc_p5))),
                ("p6", lambda: run_phase("p6", dict(t=self.p6_t, v=self.p6_v, s=self.p6_s,
                                                    g=self.p6_g, y=self.p6_y), self.alloc_p6)),
                ("p7", lambda: run_phase("p7", dict(t=self.p7_t, v=self.p7_v, s=self.p7_s,
                                                    g=self.p7_g, y=self.p7_y), self.alloc_p7,
                                         post=lambda: self.big_free("OFM"))),
            ]
            lastnm = None
            for i, (nm_, fn_) in enumerate(plist):
                if i < nphases:
                    fn_()
                    lastnm = nm_
            fin = lastnm

            def mkfin(engname):
                def f(eng):
                    em.join(eng, fin)
                return f
            run_phase("fin", {k: mkfin(k) for k in ("t", "v", "s", "g", "y")})
            for nm_ in list(reversed(list(self.guards.keys()))):
                self.big_free(nm_)
        return nc


def build_program(dev=False, nphases=99):
    return Builder(dev=dev).build(nphases)


_BF = ml_dtypes.bfloat16


def _prep_weights(inputs):
    w = {}
    w["Wqkv"] = np.ascontiguousarray(inputs["L0_attn_Wqkv"], np.float32)
    bqkv = np.asarray(inputs["L0_attn_bqkv"], np.float32)
    w["bqkv_col"] = np.ascontiguousarray(bqkv.reshape(24, 128).T)
    w["bv_row"] = np.ascontiguousarray(bqkv[2560:3072][None, :])
    w["Wo"] = np.ascontiguousarray(inputs["L0_attn_Wo"], np.float32)
    w["bo_row"] = np.ascontiguousarray(np.asarray(inputs["L0_attn_bo"], np.float32)[None, :])
    w["wq"] = np.ascontiguousarray(inputs["L0_ffn_wq"], np.float32)
    w["wk"] = np.ascontiguousarray(inputs["L0_ffn_wk"], np.float32)
    w["wv"] = np.ascontiguousarray(inputs["L0_ffn_wv"], np.float32)
    w["wo"] = np.ascontiguousarray(inputs["L0_ffn_wo"], np.float32)
    w["ln1_col"] = np.ascontiguousarray(
        np.asarray(inputs["L0_ffn_ln1_w"], np.float32).reshape(16, 128).T)
    w["ln2_col"] = np.ascontiguousarray(
        np.asarray(inputs["L0_ffn_ln2_w"], np.float32).reshape(16, 128).T)
    w["gate_T"] = np.ascontiguousarray(np.asarray(inputs["L0_ffn_gate_w"], np.float32).T)
    w["egT"] = np.ascontiguousarray(
        np.asarray(inputs["L0_ffn_eg_w"], np.float32).transpose(0, 2, 1))
    w["euT"] = np.ascontiguousarray(
        np.asarray(inputs["L0_ffn_eu_w"], np.float32).transpose(0, 2, 1))
    w["edT"] = np.ascontiguousarray(
        np.asarray(inputs["L0_ffn_ed_w"], np.float32).transpose(0, 2, 1))
    w["sgT"] = np.ascontiguousarray(np.asarray(inputs["L0_ffn_sg_w"], np.float32).T)
    w["suT"] = np.ascontiguousarray(np.asarray(inputs["L0_ffn_su_w"], np.float32).T)
    w["sdT"] = np.ascontiguousarray(np.asarray(inputs["L0_ffn_sd_w"], np.float32).T)
    w["consts_in"] = np.tile(np.array([[EPS_LN, EPS_RMS, 1.0 / D, -0.5, SC_A, SC_F,
                                        0.0, 0.0]], np.float32), (128, 1))
    w["ident_bf"] = np.eye(128, dtype=_BF)
    w["ones_bf"] = np.ones((1, 128), _BF)
    w["ones_f32"] = np.ones((1, 128), np.float32)
    return w


def kernel(**inputs):
    x = np.asarray(inputs["L0_attn_x"], np.float32)
    w = _prep_weights(inputs)
    nc = build_program(dev=False)
    in_maps = []
    for c in range(8):
        b = c // 2
        h = c % 2
        m = dict(w)
        m["x_own"] = np.ascontiguousarray(x[b, h * 2048:(h + 1) * 2048, :])
        m["x_oth"] = np.ascontiguousarray(x[b, (1 - h) * 2048:(2 - h) * 2048, :])
        m["qbase_col"] = np.full((128, 1), float(h * 2048), np.float32)
        m["othbase_col"] = np.full((128, 1), float(-h * 2048), np.float32)
        in_maps.append(m)
    res = run_bass_kernel_spmd(nc, in_maps, core_ids=list(range(8)))
    out = np.empty((4, 4096, 2048), np.float32)
    for c in range(8):
        b = c // 2
        h = c % 2
        out[b, h * 2048:(h + 1) * 2048, :] = res.results[c]["out"]
    return out


# version 08
# speedup vs baseline: 1.2229x; 1.0601x over previous
import numpy as np
import ml_dtypes
from contextlib import ExitStack

import concourse.bass as bass
import concourse.mybir as mybir
from concourse.bass_utils import run_bass_kernel_spmd

F32 = mybir.dt.float32
BF16 = mybir.dt.bfloat16
AF = mybir.ActivationFunctionType
ALU = mybir.AluOpType
AX = mybir.AxisListType

D = 2048
ND = 16
SEQ = 4096
OWN = 2048
NT_OWN = 16
NT_ALL = 32
HD_A = 64
NH_A = 32
SC_A = 0.125
HD_F = 128
NH_F = 16
SC_F = 1.0 / float(np.sqrt(128.0))
NEXP = 16
I_E = 1408
NI_E = 11
SI = 2816
NI_S = 22
EPS_LN = 1e-5
EPS_RMS = 1e-6
KTA = 24
KTB = 8
SUBI = 6
CONSTS = (EPS_LN, EPS_RMS, 1.0 / D, -0.5, SC_A, SC_F)

SEMS = ("pe", "dve", "act", "pool", "vsf", "ssf", "psf")


class _MI:
    def then_inc(self, *a, **k):
        return self


class MockEng:
    def __getattr__(self, name):
        return lambda *a, **k: _MI()


class Em:

    def __init__(self):
        self.lab = {}
        self.cnt = {n: 0 for n in SEMS}
        self.hs = {}
        self.mode = 1
        self._snap = None
        self._p1end = None
        self.nc = None
        self.stack = None

    def attach(self, nc, stack):
        self.nc = nc
        self.stack = stack
        for n in SEMS:
            self.hs[n] = stack.enter_context(nc.semaphore(n))

    def _s(self, name):
        if name not in self.cnt:
            self.cnt[name] = 0
        if self.mode == 2 and name not in self.hs:
            self.hs[name] = self.stack.enter_context(self.nc.semaphore(name))
        return name

    def p1(self):
        self.mode = 1
        self._snap = dict(self.cnt)

    def p2(self):
        self.mode = 2
        self._p1end = dict(self.cnt)
        self.cnt = dict(self._snap)

    def endphase(self, phase):
        if self.mode == 1:
            self.lab[("__endd", phase)] = dict(self.cnt)
        else:
            assert self.cnt == self._p1end, (phase, self.cnt, self._p1end)

    def _key(self, s, label):
        return (s,) + (label if isinstance(label, tuple) else (label,))

    def i(self, inst, s, k=1, label=None):
        self._s(s)
        self.cnt[s] += k
        if label is not None:
            key = self._key(s, label)
            if self.mode == 1:
                self.lab[key] = self.cnt[s]
            else:
                assert self.lab[key] == self.cnt[s], (key, self.lab[key], self.cnt[s])
        if self.mode == 2:
            inst.then_inc(self.hs[s], k)

    def mark(self, s, label):
        if self.mode == 1:
            self.lab[self._key(s, label)] = self.cnt[s]

    def w(self, eng, s, label):
        self._s(s)
        if self.mode == 2:
            eng.wait_ge(self.hs[s], self.lab[self._key(s, label)])

    def wopt(self, eng, s, label):
        self._s(s)
        if self.mode == 2:
            v = self.lab.get(self._key(s, label))
            if v is not None:
                eng.wait_ge(self.hs[s], v)

    def join(self, eng, prev_phase):
        if self.mode == 2:
            for s, v in sorted(self.lab[("__endd", prev_phase)].items()):
                if v > 0:
                    eng.wait_ge(self.hs[self._s(s)], v)

    def selfsem(self, eng, inst, s):
        self.cnt[s] += 1
        if self.mode == 2:
            inst.then_inc(self.hs[s], 1)
            eng.wait_ge(self.hs[s], self.cnt[s])


class Builder:
    def __init__(self, dev=False):
        self.dev = dev
        self.nc = bass.Bass("TRN2", target_bir_lowering=False)
        self.em = Em()
        self.io = {}
        self.t = {}
        self.guards = {}

    def declare_dram(self):
        nc, io = self.nc, self.io

        def inp(name, shape, dt=F32):
            io[name] = nc.dram_tensor(name, list(shape), dt, kind="ExternalInput").ap()

        inp("x_own", (OWN, D)); inp("x_oth", (OWN, D))
        inp("qbase_col", (128, 1)); inp("othbase_col", (128, 1))
        inp("Wqkv", (D, 3072)); inp("bqkv_col", (128, 24)); inp("bv_row", (1, 512))
        inp("Wo", (D, D)); inp("bo_row", (1, D))
        inp("wq", (D, D)); inp("wk", (D, D)); inp("wv", (D, D)); inp("wo", (D, D))
        inp("ln1_col", (128, 16)); inp("ln2_col", (128, 16))
        inp("gate_T", (D, 16))
        inp("egT", (NEXP, D, I_E)); inp("euT", (NEXP, D, I_E)); inp("edT", (NEXP, I_E, D))
        inp("sgT", (D, SI)); inp("suT", (D, SI)); inp("sdT", (SI, D))
        inp("consts_in", (128, 8))
        inp("ident_bf", (128, 128), BF16)
        inp("ones_bf", (1, 128), BF16)
        inp("ones_f32", (1, 128))

        io["out"] = nc.dram_tensor("out", [OWN, D], F32, kind="ExternalOutput").ap()

        spk = "ExternalOutput" if self.dev else "Internal"

        def spill(name, shape, dt=BF16, kind=None):
            io[name] = nc.dram_tensor(name, list(shape), dt, kind=kind or spk).ap()

        spill("qa_sp", (16, 128, OWN))
        spill("ka_sp", (4, 128, SEQ))
        spill("va_sp", (SEQ, 512))
        spill("qf_sp", (16, 128, OWN))
        spill("kf_sp", (16, 128, SEQ))
        spill("vf_sp", (SEQ, D))
        spill("ao_sp", (OWN, D), F32)
        spill("x2_sp", (OWN, D), F32)
        spill("r1_bounce", (32, 128), BF16, kind="Internal")
        spill("mask_sp", (16, 128, SEQ), BF16, kind="Internal")
        if self.dev:
            spill("h1_tap", (128, ND * SEQ), BF16)
            spill("tfm_tap", (128, ND * OWN), BF16)
            spill("cw_tap", (128, 272), F32)
            spill("ofm_tap", (128, NH_F * OWN), BF16)

    def alloc_persist(self, stack):
        nc, t = self.nc, self.t
        e = stack.enter_context
        t["PS"] = e(nc.psum_tensor("PS", [128, 3584], F32))
        t["negmean_s"] = e(nc.sbuf_tensor("negmean_s", [128, 32], F32))
        t["rstd_s"] = e(nc.sbuf_tensor("rstd_s", [128, 32], F32))
        t["vsum_s"] = e(nc.sbuf_tensor("vsum_s", [128, 32], F32))
        t["bias2_s"] = e(nc.sbuf_tensor("bias2_s", [128, 32], F32))
        t["r1_s"] = e(nc.sbuf_tensor("r1_s", [128, 32], F32))
        t["sb_qbase"] = e(nc.sbuf_tensor("sb_qbase", [128, 1], F32))
        t["sb_othbase"] = e(nc.sbuf_tensor("sb_othbase", [128, 1], F32))
        t["bqkv_col"] = e(nc.sbuf_tensor("sb_bqkv_col", [128, 24], F32))
        t["bv_row"] = e(nc.sbuf_tensor("sb_bv_row", [1, 512], F32))
        t["bo_row"] = e(nc.sbuf_tensor("sb_bo_row", [1, D], F32))
        t["ln1_col"] = e(nc.sbuf_tensor("sb_ln1_col", [128, 16], F32))
        t["ln2_col"] = e(nc.sbuf_tensor("sb_ln2_col", [128, 16], F32))
        t["sgate"] = e(nc.sbuf_tensor("sgate", [128, 256], BF16))
        t["s_cw"] = e(nc.sbuf_tensor("s_cw", [128, 272], F32))
        t["sident"] = e(nc.sbuf_tensor("sident", [128, 128], BF16))
        t["sones_bf"] = e(nc.sbuf_tensor("sones_bf", [1, 128], BF16))
        t["sones_f"] = e(nc.sbuf_tensor("sones_f", [1, 128], F32))
        t["stat8"] = e(nc.sbuf_tensor("stat8", [128, 16], F32))
        t["sconsts"] = e(nc.sbuf_tensor("sconsts", [128, 8], F32))
        for i, val in enumerate(CONSTS):
            nc.const_aps.register(F32, val, t["sconsts"][:, i:i + 1])

    def big_alloc(self, name, shape, dt):
        g = self.nc.sbuf_tensor(name, list(shape), dt)
        h = g.__enter__()
        self.guards[name] = g
        self.t[name] = h
        return h

    def big_free(self, name):
        self.guards.pop(name).__exit__(None, None, None)
        self.t.pop(name, None)

    def p0_y(self, y):
        em, t, io = self.em, self.t, self.io
        for nm, dst in (("ident_bf", "sident"), ("ones_bf", "sones_bf"),
                        ("ones_f32", "sones_f"), ("bqkv_col", "bqkv_col"),
                        ("bv_row", "bv_row"), ("bo_row", "bo_row"),
                        ("ln1_col", "ln1_col"), ("ln2_col", "ln2_col"),
                        ("qbase_col", "sb_qbase"), ("othbase_col", "sb_othbase"),
                        ("consts_in", "sconsts")):
            em.i(y.dma_start(t[dst][:, :], io[nm]), "c0", 16)

    def p0_g(self, g):
        em, t, io = self.em, self.t, self.io
        em.i(g.dma_start(t["sgate"][:, :].rearrange("p (c e) -> p c e", c=16),
                         io["gate_T"].rearrange("(c p) e -> p c e", p=128)),
             "cg0", 16, label="p0gate")

    def p0_v(self, v):
        em, t = self.em, self.t
        em.i(v.memset(t["s_cw"][:, 256:272], 1.0), "dve", 1, label="p0cw1")

    def alloc_p1(self, st):
        nc, t = self.nc, self.t
        e = st.enter_context
        t["PTB"] = e(nc.psum_tensor("PTB1", [128, 1024], BF16))
        t["sxf"] = [e(nc.sbuf_tensor(f"sxf{i}", [128, D], F32)) for i in range(2)]
        t["sxn"] = [e(nc.sbuf_tensor(f"sxn{i}", [128, D], BF16)) for i in range(2)]
        t["swb"] = [e(nc.sbuf_tensor(f"swb{i}", [128, ND * 128], BF16)) for i in range(2)]
        t["swv"] = e(nc.sbuf_tensor("swv1", [128, ND * 512], BF16))
        t["stage"] = [e(nc.sbuf_tensor(f"stage{i}", [128, D], BF16)) for i in range(2)]
        t["vstage"] = [e(nc.sbuf_tensor(f"vstage{i}", [128, 512], BF16)) for i in range(2)]
        t["rs_t"] = e(nc.sbuf_tensor("rs_t", [128, 2], F32))
        t["iota128"] = e(nc.sbuf_tensor("iota128", [128, 128], F32))
        t["thr0"] = e(nc.sbuf_tensor("thr0", [128, 32], F32))
        t["thrq"] = [e(nc.sbuf_tensor(f"thrq{i}", [128, 32], F32)) for i in range(2)]
        t["obq"] = e(nc.sbuf_tensor("obq", [128, 1], F32))

        seq = []
        for oc in range(20):
            for rnd in range(1 if oc < 16 else 2):
                seq.append((oc, rnd))
        self.p1seq = seq

    def _xtile_src(self, r):
        io = self.io
        if r < 16:
            return io["x_own"][r * 128:(r + 1) * 128, :]
        return io["x_oth"][(r - 16) * 128:(r - 16 + 1) * 128, :]

    def p1_y(self, y):
        em, t, io = self.em, self.t, self.io
        em.join(y, "p0")
        for r in range(NT_ALL):
            if r >= 2:
                em.w(y, "act", ("p1xn", r - 2))
            em.i(y.dma_start(t["sxf"][r % 2][:, :], self._xtile_src(r)),
                 f"x{r % 2}", 16, label=("p1x", r))
        for si, (oc, rnd) in enumerate(self.p1seq):
            em.w(y, "dve", ("p1pjev", si))
            st = t["stage"][si % 2]
            if oc < 16:
                for h, srcband in ((2 * oc, 0), (2 * oc + 1, 1)):
                    m, rem = divmod(h, 8)
                    band = rem // 4
                    rr = rem % 4
                    c_t = 4 * m + rr
                    em.i(y.dma_start(io["qa_sp"][c_t, band * 64:(band + 1) * 64, :],
                                     st[srcband * 64:(srcband + 1) * 64, :]),
                         f"sp{si % 2}", 16,
                         label=("p1sp", si) if h % 2 == 1 else None)
            else:
                em.i(y.dma_start(io["ka_sp"][oc - 16, :, rnd * 2048:(rnd + 1) * 2048],
                                 st[:, :]), f"sp{si % 2}", 16, label=("p1sp", si))
        for r in range(NT_ALL):
            em.w(y, "dve", ("p1vev", r))
            em.i(y.dma_start(io["va_sp"][r * 128:(r + 1) * 128, :], t["vstage"][r % 2][:, :]),
                 f"vs{r % 2}", 16, label=("p1vsp", r))

    def p1_s(self, s):
        em, t = self.em, self.t
        em.join(s, "p0")
        for r in range(NT_ALL):
            em.w(s, "dve", ("p1rs", r))
            inst = s.mul(t["negmean_s"][:, r:r + 1], t["rs_t"][:, r % 2:r % 2 + 1], -1.0 / D)
            em.i(inst, "act", 1, label=("p1nm", r))
            em.w(s, "act", ("p1nm", r))
            em.selfsem(s, s.activation(t["stage"][0][:, :], t["sxf"][r % 2][:, :],
                                       AF.Square,
                                       bias=t["negmean_s"][:, r:r + 1], scale=1.0,
                                       accum_out=t["vsum_s"][:, r:r + 1]), "ssf")
            em.selfsem(s, s.activation(t["stat8"][:, 0:1], t["vsum_s"][:, r:r + 1], AF.Ln,
                                       bias=EPS_LN, scale=1.0 / D), "ssf")
            em.selfsem(s, s.activation(t["rstd_s"][:, r:r + 1], t["stat8"][:, 0:1], AF.Exp,
                                       bias=0.0, scale=-0.5), "ssf")
            em.selfsem(s, s.mul(t["bias2_s"][:, r:r + 1], t["negmean_s"][:, r:r + 1],
                                t["rstd_s"][:, r:r + 1]), "ssf")
            em.wopt(s, "pe", ("p1tB", r - 2))
            em.i(s.activation(t["sxn"][r % 2][:, :], t["sxf"][r % 2][:, :], AF.Identity,
                              bias=t["bias2_s"][:, r:r + 1], scale=t["rstd_s"][:, r:r + 1]),
                 "act", 1, label=("p1xn", r))

    def p1_v(self, v):
        em, t = self.em, self.t
        em.join(v, "p0")
        for qt in range(NT_OWN):
            bi = qt % 2
            em.w(v, "pool", ("p1thr", qt))
            em.wopt(v, f"mks{bi}", ("p1mks", qt - 2))
            mb = t["swv"][:, bi * SEQ:bi * SEQ + SEQ]
            in0 = bass.AP(t["iota128"], 0, [[128, 128], [0, 32], [1, 128]])
            in1 = bass.AP(t["thrq"][bi], 0, [[32, 128], [1, 32], [0, 128]])
            em.i(v.tensor_tensor(mb[:, :].rearrange("p (c q) -> p c q", c=32),
                                 in0, in1, ALU.is_ge),
                 "dve", 1, label=("p1mtt", qt))
        xn_fm = t["xn_fm"]
        for r in range(NT_ALL):
            em.w(v, f"x{r % 2}", ("p1x", r))
            em.wopt(v, "act", ("p1nm", r - 2))
            em.i(v.reduce_sum(t["rs_t"][:, r % 2:r % 2 + 1], t["sxf"][r % 2][:, :], axis=AX.X),
                 "dve", 1, label=("p1rs", r))
            em.w(v, "pe", ("p1tA", r))
            em.i(v.tensor_copy(
                xn_fm[:, :].rearrange("p (c k) -> p c k", c=ND)[:, 0:8, r * 128:(r + 1) * 128],
                t["PTB"][:, :].rearrange("p (a b) -> p a b", a=8)),
                "dve", 1, label=("p1eA", r))
            em.w(v, "pe", ("p1tB", r))
            em.i(v.tensor_copy(
                xn_fm[:, :].rearrange("p (c k) -> p c k", c=ND)[:, 8:16, r * 128:(r + 1) * 128],
                t["PTB"][:, :].rearrange("p (a b) -> p a b", a=8)),
                "dve", 1, label=("p1eB", r))
        PS = t["PS"]
        for si, (oc, rnd) in enumerate(self.p1seq):
            em.w(v, "pe", ("p1pj", si))
            em.wopt(v, f"sp{si % 2}", ("p1sp", si - 2))
            st = t["stage"][si % 2]
            for gi in range(4):
                inst = v.tensor_scalar_add(st[:, gi * 512:(gi + 1) * 512],
                                           PS[:, gi * 512:(gi + 1) * 512],
                                           t["bqkv_col"][:, oc:oc + 1])
            em.i(inst, "dve", 1, label=("p1pjev", si))
        for r in range(NT_ALL):
            em.w(v, "pe", ("p1v", r))
            em.wopt(v, f"vs{r % 2}", ("p1vsp", r - 2))
            em.i(v.tensor_copy(t["vstage"][r % 2][:, :], PS[:, (r % 4) * 512:(r % 4 + 1) * 512]),
                 "dve", 1, label=("p1vev", r))

    def p1_t(self, tt):
        em, t = self.em, self.t
        em.join(tt, "p0")
        for r in range(NT_ALL):
            em.w(tt, "act", ("p1xn", r))
            em.wopt(tt, "dve", ("p1eB", r - 1))
            for k in range(8):
                inst = tt.transpose(t["PTB"][:, k * 128:(k + 1) * 128],
                                    t["sxn"][r % 2][:, k * 128:(k + 1) * 128],
                                    t["sident"][:, :])
            em.i(inst, "pe", 1, label=("p1tA", r))
            em.w(tt, "dve", ("p1eA", r))
            for k in range(8):
                inst = tt.transpose(t["PTB"][:, k * 128:(k + 1) * 128],
                                    t["sxn"][r % 2][:, (8 + k) * 128:(9 + k) * 128],
                                    t["sident"][:, :])
            em.i(inst, "pe", 1, label=("p1tB", r))
        PS = t["PS"]
        xn_fm = t["xn_fm"]
        for si, (oc, rnd) in enumerate(self.p1seq):
            em.w(tt, f"w{oc % 2}", ("p1w", oc))
            if si == 0:
                em.w(tt, "dve", ("p1eB", NT_ALL - 1))
            em.wopt(tt, "dve", ("p1pjev", si - 1))
            for dc in range(ND):
                for gi in range(4):
                    col = dc * SEQ + rnd * 2048 + gi * 512
                    inst = tt.matmul(PS[:, gi * 512:(gi + 1) * 512],
                                     t["swb"][oc % 2][:, dc * 128:(dc + 1) * 128],
                                     xn_fm[:, col:col + 512],
                                     start=(dc == 0), stop=(dc == ND - 1))
            em.i(inst, "pe", 1, label=("p1pj", si))
            if rnd == (0 if oc < 16 else 1):
                em.mark("pe", ("p1pjoc", oc))
        em.w(tt, "wv", ("p1wv", ND - 1))
        em.w(tt, "dve", ("p1pjev", len(self.p1seq) - 1))
        for r in range(NT_ALL):
            em.wopt(tt, "dve", ("p1vev", r - 4))
            bank = r % 4
            inst = tt.matmul(PS[:, bank * 512:(bank + 1) * 512], t["sones_f"][:, :],
                             t["bv_row"][:, :], start=True, stop=False)
            for dc in range(ND):
                inst = tt.matmul(PS[:, bank * 512:(bank + 1) * 512],
                                 xn_fm[:, dc * SEQ + r * 128: dc * SEQ + (r + 1) * 128],
                                 t["swv"][:, dc * 512:(dc + 1) * 512],
                                 start=False, stop=(dc == ND - 1))
            em.i(inst, "pe", 1, label=("p1v", r))

    def p1_g(self, g):
        em, t, io = self.em, self.t, self.io
        em.join(g, "p0")
        em.selfsem(g, g.iota(t["iota128"][:, :], [[1, 128]], channel_multiplier=0,
                             allow_small_or_imprecise_dtypes=True), "psf")
        em.selfsem(g, g.iota(t["thr0"][:, :], [[128, 32]], channel_multiplier=1,
                             allow_small_or_imprecise_dtypes=True), "psf")
        em.selfsem(g, g.tensor_sub(t["obq"][:, :], t["sb_othbase"][:, :],
                                   t["sb_qbase"][:, :]), "psf")
        em.mark("pool", "p1iotas")
        for qt in range(NT_OWN):
            bi = qt % 2
            thrq = t["thrq"][bi]
            em.wopt(g, "dve", ("p1mtt", qt - 2))
            em.selfsem(g, g.tensor_scalar_add(thrq[:, 0:16], t["thr0"][:, 0:16],
                                              float(-128 * qt)), "psf")
            em.selfsem(g, g.tensor_scalar_add(thrq[:, 16:32], t["thr0"][:, 0:16],
                                              float(2048 - 128 * qt)), "psf")
            em.selfsem(g, g.tensor_scalar_add(thrq[:, 16:32], thrq[:, 16:32],
                                              t["obq"][:, :]), "psf")
            em.mark("pool", ("p1thr", qt))
            em.i(g.engine_nop() if False else g.tensor_copy(t["obq"][:, :], t["obq"][:, :]),
                 "pool", 1, label=("p1thr", qt))
            em.w(g, "dve", ("p1mtt", qt))
            mb = t["swv"][:, bi * SEQ:bi * SEQ + SEQ]
            em.i(g.dma_start(io["mask_sp"][qt, :, :], mb),
                 f"mks{qt % 2}", 16, label=("p1mks", qt))
        for oc in range(20):
            if oc >= 2:
                em.w(g, "pe", ("p1pjoc", oc - 2))
            em.i(g.dma_start(
                t["swb"][oc % 2][:, :].rearrange("p (c o) -> p c o", c=ND),
                io["Wqkv"][:, oc * 128:(oc + 1) * 128].rearrange("(c p) o -> p c o", p=128)),
                f"w{oc % 2}", 16, label=("p1w", oc))
        em.w(g, "mks0", ("p1mks", NT_OWN - 2))
        em.w(g, "mks1", ("p1mks", NT_OWN - 1))
        for dc in range(ND):
            em.i(g.dma_start(t["swv"][:, dc * 512:(dc + 1) * 512],
                             io["Wqkv"][dc * 128:(dc + 1) * 128, 2560:3072]),
                 "wv", 16, label=("p1wv", dc))

    def alloc_p2(self, st):
        nc, t = self.nc, self.t
        e = st.enter_context
        t["PTB"] = e(nc.psum_tensor("PTB2", [128, 1024], BF16))
        t["r1b"] = e(nc.sbuf_tensor("r1b", [128, 32], BF16))
        t["r1T"] = e(nc.sbuf_tensor("r1T", [32, 128], BF16))
        t["r1row"] = e(nc.sbuf_tensor("r1row", [1, SEQ], BF16))
        t["st2"] = e(nc.sbuf_tensor("st2", [128, 32], F32))

    def p2_v(self, v):
        em, t = self.em, self.t
        em.join(v, "p1")
        em.selfsem(v, v.tensor_mul(t["st2"][:, :], t["rstd_s"][:, :], t["rstd_s"][:, :]),
                   "vsf")
        em.i(v.tensor_mul(t["st2"][:, :], t["st2"][:, :], t["vsum_s"][:, :]),
             "dve", 1, label="p2t2")
        em.w(v, "act", "p2r1")
        em.i(v.tensor_copy(t["r1b"][:, :], t["r1_s"][:, :]), "dve", 1, label="p2r1b")
        em.w(v, "pe", "p2tr")
        em.i(v.tensor_copy(t["r1T"][0:32, :], t["PTB"][0:32, 0:128]),
             "dve", 1, label="p2trev")
        xn_fm = t["xn_fm"]
        for dc in range(ND):
            inst = v.tensor_scalar_mul(xn_fm[:, dc * SEQ:(dc + 1) * SEQ],
                                       xn_fm[:, dc * SEQ:(dc + 1) * SEQ],
                                       t["ln1_col"][:, dc:dc + 1])
            em.i(inst, "dve", 1, label=("p2ln1", dc))
        em.w(v, "dve", ("p2ln1", ND - 1))
        for r in range(NT_ALL):
            em.w(v, "pe", ("p2bc", r))
            for dc in range(ND):
                inst = v.tensor_mul(xn_fm[:, dc * SEQ + r * 128: dc * SEQ + (r + 1) * 128],
                                    xn_fm[:, dc * SEQ + r * 128: dc * SEQ + (r + 1) * 128],
                                    t["PS"][:, (r % 2) * 512:(r % 2) * 512 + 128])
            em.i(inst, "dve", 1, label=("p2h1", r))

    def p2_s(self, s):
        em, t = self.em, self.t
        em.join(s, "p1")
        em.w(s, "dve", "p2t2")
        em.selfsem(s, s.activation(t["st2"][:, :], t["st2"][:, :], AF.Ln,
                                   bias=EPS_RMS, scale=1.0 / D), "ssf")
        em.i(s.activation(t["r1_s"][:, :], t["st2"][:, :], AF.Exp, bias=0.0, scale=-0.5),
             "act", 1, label="p2r1")

    def p2_t(self, tt):
        em, t = self.em, self.t
        em.join(tt, "p1")
        em.w(tt, "dve", "p2r1b")
        em.i(tt.transpose(t["PTB"][0:32, 0:128], t["r1b"][:, :], t["sident"][:, :]),
             "pe", 1, label="p2tr")
        for r in range(NT_ALL):
            em.w(tt, "bn", "p2r1row")
            em.w(tt, "dve", ("p2ln1", ND - 1))
            em.wopt(tt, "dve", ("p2h1", r - 2))
            em.i(tt.matmul(t["PS"][:, (r % 2) * 512:(r % 2) * 512 + 128],
                           t["sones_bf"][:, :], t["r1row"][0:1, r * 128:(r + 1) * 128],
                           start=True, stop=True),
                 "pe", 1, label=("p2bc", r))

    def p2_y(self, y):
        em, t, io = self.em, self.t, self.io
        em.join(y, "p1")
        em.w(y, "dve", "p2trev")
        em.i(y.dma_start(io["r1_bounce"], t["r1T"][0:32, :]), "bn", 16, label="p2bsp")
        em.w(y, "bn", "p2bsp")
        em.i(y.dma_start(t["r1row"][0:1, :],
                         io["r1_bounce"].rearrange("a b -> (a b)")),
             "bn", 16, label="p2r1row")

    def alloc_p3(self, st):
        nc, t = self.nc, self.t
        e = st.enter_context
        t["swb"] = [e(nc.sbuf_tensor(f"p3swb{i}", [128, ND * 128], BF16)) for i in range(2)]
        t["swv"] = e(nc.sbuf_tensor("p3swv", [128, ND * 512], BF16))
        t["stage"] = [e(nc.sbuf_tensor(f"p3stage{i}", [128, D], BF16)) for i in range(2)]
        t["vstage"] = [e(nc.sbuf_tensor(f"p3vstage{i}", [128, 512], BF16)) for i in range(2)]
        seq = []
        for k in range(32):
            for rnd in range(1 if k < 16 else 2):
                seq.append((k, rnd))
        self.p3seq = seq

    def p3_g(self, g):
        em, t, io = self.em, self.t, self.io
        em.join(g, "p2")
        for k in range(32):
            wname = "wq" if k < 16 else "wk"
            oc = k % 16
            if k >= 2:
                em.w(g, "pe", ("p3pjoc", k - 2))
            em.i(g.dma_start(
                t["swb"][k % 2][:, :].rearrange("p (c o) -> p c o", c=ND),
                io[wname][:, oc * 128:(oc + 1) * 128].rearrange("(c p) o -> p c o", p=128)),
                f"w{k % 2}", 16, label=("p3w", k))
        for og in range(4):
            em.w(g, "pe", ("p3vwfree", og))
            for dc in range(ND):
                em.i(g.dma_start(t["swv"][:, dc * 512:(dc + 1) * 512],
                                 io["wv"][dc * 128:(dc + 1) * 128, og * 512:(og + 1) * 512]),
                     "wv", 16, label=("p3wv", og, dc))

    def p3_t(self, tt):
        em, t = self.em, self.t
        em.join(tt, "p2")
        PS = t["PS"]
        xn_fm = t["xn_fm"]
        for si, (k, rnd) in enumerate(self.p3seq):
            em.w(tt, f"w{k % 2}", ("p3w", k))
            em.wopt(tt, "dve", ("p3pjev", si - 1))
            for dc in range(ND):
                for gi in range(4):
                    col = dc * SEQ + rnd * 2048 + gi * 512
                    inst = tt.matmul(PS[:, gi * 512:(gi + 1) * 512],
                                     t["swb"][k % 2][:, dc * 128:(dc + 1) * 128],
                                     xn_fm[:, col:col + 512],
                                     start=(dc == 0), stop=(dc == ND - 1))
            em.i(inst, "pe", 1, label=("p3pj", si))
            if rnd == (0 if k < 16 else 1):
                em.mark("pe", ("p3pjoc", k))
        em.w(tt, "dve", ("p3pjev", len(self.p3seq) - 1))
        for og in range(4):
            em.mark("pe", ("p3vwfree", og))
            em.w(tt, "wv", ("p3wv", og, ND - 1))
            for r in range(NT_ALL):
                gr = og * NT_ALL + r
                em.wopt(tt, "dve", ("p3vev", gr - 4))
                bank = r % 4
                for dc in range(ND):
                    inst = tt.matmul(PS[:, bank * 512:(bank + 1) * 512],
                                     xn_fm[:, dc * SEQ + r * 128: dc * SEQ + (r + 1) * 128],
                                     t["swv"][:, dc * 512:(dc + 1) * 512],
                                     start=(dc == 0), stop=(dc == ND - 1))
                em.i(inst, "pe", 1, label=("p3v", gr))

    def p3_v(self, v):
        em, t = self.em, self.t
        em.join(v, "p2")
        PS = t["PS"]
        for si, (k, rnd) in enumerate(self.p3seq):
            em.w(v, "pe", ("p3pj", si))
            em.wopt(v, f"sp{si % 2}", ("p3sp", si - 2))
            st = t["stage"][si % 2]
            for gi in range(4):
                inst = v.tensor_copy(st[:, gi * 512:(gi + 1) * 512],
                                     PS[:, gi * 512:(gi + 1) * 512])
            em.i(inst, "dve", 1, label=("p3pjev", si))
        for gr in range(4 * NT_ALL):
            em.w(v, "pe", ("p3v", gr))
            em.wopt(v, f"vs{gr % 2}", ("p3vsp", gr - 2))
            em.i(v.tensor_copy(t["vstage"][gr % 2][:, :],
                               PS[:, (gr % 4) * 512:(gr % 4 + 1) * 512]),
                 "dve", 1, label=("p3vev", gr))

    def p3_y(self, y):
        em, t, io = self.em, self.t, self.io
        em.join(y, "p2")
        for si, (k, rnd) in enumerate(self.p3seq):
            em.w(y, "dve", ("p3pjev", si))
            st = t["stage"][si % 2]
            if k < 16:
                em.i(y.dma_start(io["qf_sp"][k, :, :], st[:, :]), f"sp{si % 2}", 16,
                     label=("p3sp", si))
            else:
                em.i(y.dma_start(io["kf_sp"][k - 16, :, rnd * 2048:(rnd + 1) * 2048],
                                 st[:, :]), f"sp{si % 2}", 16, label=("p3sp", si))
        for gr in range(4 * NT_ALL):
            og, r = divmod(gr, NT_ALL)
            em.w(y, "dve", ("p3vev", gr))
            em.i(y.dma_start(io["vf_sp"][r * 128:(r + 1) * 128, og * 512:(og + 1) * 512],
                             t["vstage"][gr % 2][:, :]), f"vs{gr % 2}", 16,
                 label=("p3vsp", gr))
        if self.dev:
            em.w(y, "dve", ("p2h1", NT_ALL - 1))
            em.i(y.dma_start(io["h1_tap"], t["xn_fm"][:, :]), "tap", 16)


    def _attn_unit_t_pass1(self, tt, u, cfg):
        em, t = self.em, self.t
        PS = t["PS"]
        nm = cfg["nm"]
        uid = u["uid"]
        if cfg.get("pre_sa") is not None:
            cfg["pre_sa"](tt, uid)
        for ks in range(8):
            b = ks % 4
            prev = (nm + "bm", uid, ks - 4) if ks >= 4 else (nm + "bm", uid - 1, ks + 4)
            em.wopt(tt, "dve", prev)
            inst = tt.matmul(PS[:, b * 512:(b + 1) * 512], u["lhsT"], u["krhs"](ks),
                             start=True, stop=True)
            em.i(inst, "pe", 1, label=(nm + "s1", uid, ks))

    def _attn_unit_t_main(self, tt, u, cfg):
        self._attn_unit_t_nt(tt, u, cfg)
        self._attn_unit_t_pass2(tt, u, cfg)
        self._attn_unit_t_av(tt, u, cfg)

    def _attn_unit_t_nt(self, tt, u, cfg):
        em, t = self.em, self.t
        PSB = t["PSB"]
        nm = cfg["nm"]
        uid = u["uid"]
        par = uid % 2
        em.w(tt, "dve", (nm + "ng", uid))
        em.wopt(tt, "dve", (nm + "ngr", uid - 1))
        if cfg.get("nt_wait") is not None:
            cfg["nt_wait"](tt, uid)
        em.i(tt.transpose(PSB[0:1, par * 128:(par + 1) * 128], u["ngb"],
                          t["sident"][:, :]),
             "pe", 1, label=(nm + "nt", uid))

    def _attn_unit_t_pass2(self, tt, u, cfg):
        em, t = self.em, self.t
        PS = t["PS"]
        nm = cfg["nm"]
        uid = u["uid"]
        for g in range(8):
            b2 = 2048 + (g % 2) * 512
            em.wopt(tt, "act", (nm + "ea", uid, g - 2))
            if g == 0:
                em.w(tt, "dve", (nm + "ngr", uid))
            tt.matmul(PS[:, b2:b2 + 512], t["sones_bf"][:, :],
                      u["ngrow4x"], start=True, stop=True)
            for k4 in range(4):
                kt = 4 * g + k4
                dst = PS[:, b2 + k4 * 128: b2 + (k4 + 1) * 128]
                inst = tt.matmul(dst, u["kt128"](kt), u["qrhs"], start=False,
                                 stop=False, skip_group_check=True)
            em.i(inst, "pe", 1, label=(nm + "s2", uid, g))

    def _attn_unit_t_av(self, tt, u, cfg):
        em, t = self.em, self.t
        PS = t["PS"]
        nm = cfg["nm"]
        avw = cfg["avw"]
        uid = u["uid"]
        avps = PS[:, 3072:3072 + avw]
        for g in range(8):
            if cfg["mask"]:
                em.w(tt, "dve", (nm + "pa", uid, g // 4))
            else:
                em.w(tt, "act", (nm + "ea", uid, g))
            if g == 0:
                em.wopt(tt, "dve", (nm + "oev", uid - 1))
            for k4 in range(4):
                kt = 4 * g + k4
                inst = tt.matmul(avps, u["ptbuf"][:, kt * 128:(kt + 1) * 128],
                                 u["vrhs"](kt), start=(kt == 0), stop=(kt == 31))
            if g == 7:
                em.i(inst, "pe", 1, label=(nm + "av", uid))

    def _attn_unit_v(self, v, u, cfg):
        em, t = self.em, self.t
        PS = t["PS"]
        PSB = t["PSB"]
        nm = cfg["nm"]
        hd = cfg["hd"]
        avw = cfg["avw"]
        uid = u["uid"]
        par = uid % 2
        sb = u["sb"]
        for ks in range(8):
            em.w(v, "pe", (nm + "s1", uid, ks))
            inst = v.reduce_max(sb["mad"][:, ks:ks + 1],
                                PS[:, (ks % 4) * 512:(ks % 4 + 1) * 512], axis=AX.X)
            em.i(inst, "dve", 1, label=(nm + "bm", uid, ks))
        em.w(v, "dve", (nm + "bm", uid, 7))
        em.selfsem(v, v.reduce_max(sb["m"], sb["mad"][:, 0:8], axis=AX.X), "vsf")
        em.i(v.tensor_scalar_mul(u["ngb"], sb["m"], -1.0), "dve", 1,
             label=(nm + "ng", uid))

    def _attn_unit_v_ngr(self, v, u, cfg):
        em, t = self.em, self.t
        PSB = t["PSB"]
        nm = cfg["nm"]
        uid = u["uid"]
        par = uid % 2
        em.w(v, "pe", (nm + "nt", uid))
        dst = u["ngrow4x"]
        src_ap = bass.AP(PSB, par * 128, [[512, 1], [0, 4], [1, 128]])
        em.i(v.tensor_copy(dst[0:1, :].rearrange("p (r q) -> p r q", r=4), src_ap),
             "dve", 1, label=(nm + "ngr", uid))

    def _attn_unit_v_post(self, v, u, cfg):
        em, t = self.em, self.t
        PS = t["PS"]
        nm = cfg["nm"]
        hd = cfg["hd"]
        avw = cfg["avw"]
        uid = u["uid"]
        sb = u["sb"]
        if cfg["mask"]:
            for hh in range(2):
                em.w(v, "act", (nm + "ea", uid, 4 * hh + 3))
                em.w(v, cfg["mkld_sem"](uid), (nm + "mk", uid))
                em.i(v.tensor_mul(u["ptbuf"][:, hh * 2048:(hh + 1) * 2048],
                                  u["ptbuf"][:, hh * 2048:(hh + 1) * 2048],
                                  u["mask"][:, hh * 2048:(hh + 1) * 2048]),
                     "dve", 1, label=(nm + "pa", uid, hh))
        em.w(v, "pe", (nm + "av", uid))
        em.selfsem(v, v.reciprocal(sb["il"], PS[:, 3072 + hd:3072 + hd + 1]), "vsf")
        if cfg.get("oev_wait") is not None:
            cfg["oev_wait"](v, uid)
        em.i(v.tensor_scalar_mul(u["odst"], PS[:, 3072:3072 + hd], sb["il"]),
             "dve", 1, label=(nm + "oev", uid))

    def _attn_unit_s(self, s, u, cfg):
        em, t = self.em, self.t
        PS = t["PS"]
        nm = cfg["nm"]
        sc = cfg["scale"]
        uid = u["uid"]
        for g in range(8):
            em.w(s, "pe", (nm + "s2", uid, g))
            if g == 0:
                em.wopt(s, "pe", (nm + "av", uid - 2))
            b2 = 2048 + (g % 2) * 512
            em.i(s.activation(u["ptbuf"][:, g * 512:(g + 1) * 512],
                              PS[:, b2:b2 + 512], AF.Exp, bias=0.0, scale=sc),
                 "act", 1, label=(nm + "ea", uid, g))

    def alloc_p4(self, st):
        nc, t = self.nc, self.t
        e = st.enter_context
        t["PSB"] = e(nc.psum_tensor("PSB4", [128, 512], BF16))
        t["ska"] = e(nc.sbuf_tensor("ska", [128, 4 * SEQ], BF16))
        t["sva"] = e(nc.sbuf_tensor("sva", [128, 32 * 520], BF16))
        t["swo"] = e(nc.sbuf_tensor("swo", [128, ND * D], BF16))
        t["sq"] = [e(nc.sbuf_tensor(f"sq{i}", [128, ND * 128], BF16)) for i in range(2)]
        t["ptbuf"] = [e(nc.sbuf_tensor(f"ptbuf{i}", [128, SEQ], BF16)) for i in range(2)]
        t["otm"] = e(nc.sbuf_tensor("otm", [128, D], BF16))
        t["ofmt"] = e(nc.sbuf_tensor("ofmt", [128, ND * 128], BF16))
        t["aostage"] = [e(nc.sbuf_tensor("aostage0", [128, D], F32))] * 2
        t["ast"] = [e(nc.sbuf_tensor(f"ast{i}", [128, 16], F32)) for i in range(2)]
        t["ngb4"] = e(nc.sbuf_tensor("ngb4", [128, 2], BF16))
        t["ngrow4"] = e(nc.sbuf_tensor("ngrow4", [1, 1024], BF16))

    def _p4_cfg(self):
        em = self.em

        def pre_sa(tt, uid):
            if uid % NH_A == 1:
                for og in range(4):
                    em.wopt(tt, "dve", ("p4aoev", uid // NH_A - 1, og))

        def oev_wait(v, uid):
            em.wopt(v, "pe", ("p4oft", uid // NH_A - 1, 3))

        def nt_wait(tt, uid):
            if uid % NH_A == 0:
                em.wopt(tt, "dve", ("p4oftev", uid // NH_A - 2, 3))
            elif uid % NH_A == 1:
                em.wopt(tt, "dve", ("p4oftev", uid // NH_A - 1, 3))

        return dict(nm="A4", hd=HD_A, avw=HD_A + 1, scale=SC_A, mask=False,
                    pre_sa=pre_sa, oev_wait=oev_wait, mkld_sem=None, nt_wait=nt_wait)

    def _p4_unit(self, qt, h):
        t = self.t
        uid = qt * NH_A + h
        par = uid % 2
        b = (h // 4) % 2
        c = 4 * (h // 8) + (h % 8) % 4
        jc = (h // 4) // 2
        j = h // 4
        sq = t["sq"][qt % 2]
        ska, sva = t["ska"], t["sva"]
        st = t["ast"][par]
        sb = dict(m=st[:, 8:9], il=st[:, 9:10], mad=st[:, 0:8])
        return dict(
            uid=uid, par=par,
            lhsT=sq[b * 64:(b + 1) * 64, c * 128:(c + 1) * 128],
            qrhs=sq[b * 64:(b + 1) * 64, c * 128:(c + 1) * 128],
            krhs=lambda ks: ska[b * 64:(b + 1) * 64,
                                jc * SEQ + ks * 512: jc * SEQ + (ks + 1) * 512],
            kt128=lambda kt: ska[b * 64:(b + 1) * 64,
                                 jc * SEQ + kt * 128: jc * SEQ + (kt + 1) * 128],
            vrhs=lambda kt: sva[:, kt * 520 + j * 65: kt * 520 + (j + 1) * 65],
            ngb=t["ngb4"][:, par:par + 1],
            ngrow4x=t["ngrow4"][0:1, par * 512:(par + 1) * 512],
            ptbuf=t["ptbuf"][par],
            odst=t["otm"][:, h * 64:(h + 1) * 64],
            sb=sb, mask=None)

    def p4_y(self, y):
        em, t, io = self.em, self.t, self.io
        em.join(y, "p3")
        em.i(y.dma_start(t["ska"][:, :].rearrange("p (c k) -> p c k", c=4),
                         io["ka_sp"].rearrange("c p k -> p c k")), "c4", 16)
        for j in range(8):
            em.i(y.dma_start(
                bass.AP(t["sva"], j * 65, [[32 * 520, 128], [520, 32], [1, 64]]),
                io["va_sp"].rearrange("(c p) (j i) -> p c j i", p=128, j=8)[:, :, j, :]),
                "c4", 16)
        em.mark("c4", "p4kv")
        for qt in range(NT_OWN):
            if qt >= 2:
                em.w(y, "pe", ("A4s1", (qt - 1) * NH_A, 0))
            em.i(y.dma_start(t["sq"][qt % 2][:, :].rearrange("p (c q) -> p c q", c=ND),
                             io["qa_sp"][:, :, qt * 128:(qt + 1) * 128]
                             .rearrange("c p q -> p c q")),
                 f"q{qt % 2}", 16, label=("p4q", qt))
            if qt >= 1:
                for og in range(4):
                    em.w(y, "dve", ("p4aoev", qt - 1, og))
                em.i(y.dma_start(io["ao_sp"][(qt - 1) * 128:qt * 128, :],
                                 t["aostage"][(qt - 1) % 2][:, :]), "ao4", 16,
                     label=("p4aosp", qt - 1))
        for og in range(4):
            em.w(y, "dve", ("p4aoev", NT_OWN - 1, og))
        em.i(y.dma_start(io["ao_sp"][(NT_OWN - 1) * 128:NT_OWN * 128, :],
                         t["aostage"][(NT_OWN - 1) % 2][:, :]), "ao4", 16,
             label=("p4aosp", NT_OWN - 1))

    def p4_g(self, g):
        em, t, io = self.em, self.t, self.io
        em.join(g, "p3")
        for dc in range(ND):
            em.i(g.dma_start(t["swo"][:, dc * D:(dc + 1) * D],
                             io["Wo"][dc * 128:(dc + 1) * 128, :]),
                 "wo4", 16, label=("p4wo", dc))
        em.w(g, "c4", "p4kv")
        em.i(g.memset(bass.AP(t["sva"], 64, [[32 * 520, 128], [520, 32], [65, 8], [1, 1]]),
                      1.0), "pool", 1, label="p4ones")

    def _p4_wo(self, tt, qt, ngr_uid):
        em, t = self.em, self.t
        PS = t["PS"]
        em.w(tt, "dve", ("A4oev", qt * NH_A + NH_A - 1))
        em.w(tt, "dve", ("A4ngr", ngr_uid))
        for rnd in range(4):
            if rnd > 0:
                em.w(tt, "dve", ("p4oftev", qt, rnd - 1))
            elif qt > 0:
                em.w(tt, "dve", ("p4oftev", qt - 1, 3))
            for k in range(4):
                inst = tt.transpose(
                    t["PSB"][:, k * 128:(k + 1) * 128],
                    t["otm"][:, (rnd * 4 + k) * 128:(rnd * 4 + k + 1) * 128],
                    t["sident"][:, :])
            em.i(inst, "pe", 1, label=("p4oft", qt, rnd))
        em.w(tt, "dve", ("p4oftev", qt, 3))
        for og in range(4):
            em.wopt(tt, "dve", ("p4aoev", qt - 1, og))
            em.wopt(tt, "dve", ("A4bm", qt * NH_A + NH_A - 1, og))
            em.wopt(tt, "dve", ("A4bm", qt * NH_A + NH_A - 1, og + 4))
            em.wopt(tt, "dve", ("A4bm", (qt + 1) * NH_A, og))
            em.wopt(tt, "dve", ("A4bm", (qt + 1) * NH_A, og + 4))
            inst = tt.matmul(PS[:, og * 512:(og + 1) * 512], t["sones_f"][:, :],
                             t["bo_row"][:, og * 512:(og + 1) * 512],
                             start=True, stop=False)
            for dc in range(ND):
                inst = tt.matmul(PS[:, og * 512:(og + 1) * 512],
                                 t["ofmt"][:, dc * 128:(dc + 1) * 128],
                                 t["swo"][:, dc * D + og * 512: dc * D + (og + 1) * 512],
                                 start=False, stop=(dc == ND - 1))
            em.i(inst, "pe", 1, label=("p4wom", qt, og))

    def p4_t(self, tt):
        em, t = self.em, self.t
        cfg = self._p4_cfg()
        em.join(tt, "p3")
        em.w(tt, "c4", "p4kv")
        em.w(tt, "pool", "p4ones")
        em.w(tt, "wo4", ("p4wo", ND - 1))
        em.w(tt, f"q{0}", ("p4q", 0))
        self._attn_unit_t_pass1(tt, self._p4_unit(0, 0), cfg)
        for qt in range(NT_OWN):
            for h in range(NH_A):
                uid = qt * NH_A + h
                u = self._p4_unit(qt, h)
                nqt, nh = divmod(uid + 1, NH_A)
                if h == 0 and qt > 0:
                    self._attn_unit_t_nt(tt, u, cfg)
                    self._attn_unit_t_av(tt, self._p4_unit(qt - 1, NH_A - 1), cfg)
                    self._p4_wo(tt, qt - 1, uid)
                    self._attn_unit_t_pass1(tt, self._p4_unit(nqt, nh), cfg)
                    self._attn_unit_t_pass2(tt, u, cfg)
                else:
                    if nqt < NT_OWN:
                        if nh == 0:
                            em.w(tt, f"q{nqt % 2}", ("p4q", nqt))
                        self._attn_unit_t_pass1(tt, self._p4_unit(nqt, nh), cfg)
                    self._attn_unit_t_nt(tt, u, cfg)
                    if uid > 0:
                        pq, ph = divmod(uid - 1, NH_A)
                        self._attn_unit_t_av(tt, self._p4_unit(pq, ph), cfg)
                    self._attn_unit_t_pass2(tt, u, cfg)
        self._attn_unit_t_av(tt, self._p4_unit(NT_OWN - 1, NH_A - 1), cfg)
        self._p4_wo(tt, NT_OWN - 1, NT_OWN * NH_A - 1)

    def _p4_wo_v(self, v, qt):
        em, t = self.em, self.t
        for rnd in range(4):
            em.w(v, "pe", ("p4oft", qt, rnd))
            em.i(v.tensor_copy(t["ofmt"][:, rnd * 512:(rnd + 1) * 512],
                               t["PSB"][:, :]),
                 "dve", 1, label=("p4oftev", qt, rnd))
        em.wopt(v, "ao4", ("p4aosp", qt - 1))
        for og in range(4):
            em.w(v, "pe", ("p4wom", qt, og))
            em.i(v.tensor_copy(t["aostage"][qt % 2][:, og * 512:(og + 1) * 512],
                               t["PS"][:, og * 512:(og + 1) * 512]),
                 "dve", 1, label=("p4aoev", qt, og))

    def p4_v(self, v):
        em, t = self.em, self.t
        cfg = self._p4_cfg()
        em.join(v, "p3")
        self._attn_unit_v(v, self._p4_unit(0, 0), cfg)
        for qt in range(NT_OWN):
            for h in range(NH_A):
                uid = qt * NH_A + h
                u = self._p4_unit(qt, h)
                nqt, nh = divmod(uid + 1, NH_A)
                if h == 0 and qt > 0:
                    self._attn_unit_v_ngr(v, u, cfg)
                    self._attn_unit_v_post(v, self._p4_unit(qt - 1, NH_A - 1), cfg)
                    self._p4_wo_v(v, qt - 1)
                    self._attn_unit_v(v, self._p4_unit(nqt, nh), cfg)
                else:
                    if nqt < NT_OWN:
                        self._attn_unit_v(v, self._p4_unit(nqt, nh), cfg)
                    self._attn_unit_v_ngr(v, u, cfg)
                    if uid > 0:
                        pq, ph = divmod(uid - 1, NH_A)
                        self._attn_unit_v_post(v, self._p4_unit(pq, ph), cfg)
        self._attn_unit_v_post(v, self._p4_unit(NT_OWN - 1, NH_A - 1), cfg)
        self._p4_wo_v(v, NT_OWN - 1)

    def p4_s(self, s):
        em = self.em
        cfg = self._p4_cfg()
        em.join(s, "p3")
        for qt in range(NT_OWN):
            for h in range(NH_A):
                self._attn_unit_s(s, self._p4_unit(qt, h), cfg)

    def alloc_p5(self, st):
        nc, t = self.nc, self.t
        e = st.enter_context
        t["PSB"] = e(nc.psum_tensor("PSB5", [128, 512], BF16))
        t["skf"] = [e(nc.sbuf_tensor(f"skf{i}", [128, SEQ], BF16)) for i in range(2)]
        t["svf"] = [e(nc.sbuf_tensor(f"svf{i}", [128, 32 * 129], BF16)) for i in range(2)]
        t["sqf"] = [e(nc.sbuf_tensor(f"sqf{i}", [128, OWN], BF16)) for i in range(2)]
        t["ptbuf"] = [e(nc.sbuf_tensor(f"p5ptbuf{i}", [128, SEQ], BF16)) for i in range(2)]
        t["mask_t"] = [e(nc.sbuf_tensor(f"mask{i}", [128, SEQ], BF16)) for i in range(2)]
        t["avtm"] = e(nc.sbuf_tensor("avtm", [128, 128], BF16))
        t["fst"] = [e(nc.sbuf_tensor(f"fst{i}", [128, 16], F32)) for i in range(2)]
        t["ngb5"] = e(nc.sbuf_tensor("ngb5", [128, 2], BF16))
        t["ngrow5"] = e(nc.sbuf_tensor("ngrow5", [1, 1024], BF16))

    def _p5_cfg(self):
        em = self.em

        def oev_wait(v, uid):
            em.wopt(v, "pe", ("p5ot", uid - 1))

        def nt_wait(tt, uid):
            em.wopt(tt, "dve", ("p5otev", uid - 1))

        return dict(nm="F5", hd=HD_F, avw=HD_F + 1, scale=SC_F, mask=True,
                    pre_sa=None, oev_wait=oev_wait,
                    mkld_sem=lambda uid: f"mk{uid % 2}", nt_wait=nt_wait)

    def _p5_unit(self, h, qt):
        t = self.t
        uid = h * NT_OWN + qt
        par = uid % 2
        st = t["fst"][par]
        sb = dict(m=st[:, 8:9], il=st[:, 9:10], mad=st[:, 0:8])
        skf = t["skf"][h % 2]
        svf = t["svf"][h % 2]
        return dict(
            uid=uid, par=par,
            lhsT=t["sqf"][h % 2][:, qt * 128:(qt + 1) * 128],
            qrhs=t["sqf"][h % 2][:, qt * 128:(qt + 1) * 128],
            krhs=lambda ks: skf[:, ks * 512:(ks + 1) * 512],
            kt128=lambda kt: skf[:, kt * 128:(kt + 1) * 128],
            vrhs=lambda kt: svf[:, kt * 129:(kt + 1) * 129],
            ngb=t["ngb5"][:, par:par + 1],
            ngrow4x=t["ngrow5"][0:1, par * 512:(par + 1) * 512],
            ptbuf=t["ptbuf"][par],
            odst=t["avtm"][:, :],
            sb=sb, mask=t["mask_t"][par])

    def p5_y(self, y):
        em, t, io = self.em, self.t, self.io
        em.join(y, "p4")
        for h in range(NH_F):
            if h >= 2:
                em.w(y, "pe", ("F5s1", (h - 1) * NT_OWN, 0))
            em.i(y.dma_start(t["skf"][h % 2][:, :], io["kf_sp"][h, :, :]),
                 f"kv{h % 2}", 16)
            em.i(y.dma_start(
                t["svf"][h % 2][:, :].rearrange("p (c k) -> p c k", c=32)[:, :, 0:128],
                io["vf_sp"][:, h * 128:(h + 1) * 128]
                .rearrange("(c p) k -> p c k", p=128)),
                 f"kv{h % 2}", 16)
            em.i(y.dma_start(t["sqf"][h % 2][:, :], io["qf_sp"][h, :, :]),
                 f"kv{h % 2}", 16, label=("p5kv", h))
            for qt in range(NT_OWN):
                uid = h * NT_OWN + qt
                em.wopt(y, "dve", ("F5pa", uid - 2, 1))
                em.i(y.dma_start(t["mask_t"][uid % 2][:, :], io["mask_sp"][qt, :, :]),
                     f"mk{uid % 2}", 16, label=("F5mk", uid))
        if self.dev:
            em.w(y, "dve", ("p5otev", NH_F * NT_OWN - 1))
            em.i(y.dma_start(io["ofm_tap"], t["OFM"][:, :]), "tap", 16)

    def p5_g(self, p):
        em, t = self.em, self.t
        em.join(p, "p4")
        for i in range(2):
            em.i(p.memset(t["svf"][i][:, :].rearrange("p (c k) -> p c k", c=32)[:, :, 128:129],
                          1.0), "pool", 1, label=("p5ones", i))

    def p5_t(self, tt):
        em, t = self.em, self.t
        cfg = self._p5_cfg()
        em.join(tt, "p4")
        em.w(tt, "pool", ("p5ones", 1))
        em.w(tt, "kv0", ("p5kv", 0))
        self._attn_unit_t_pass1(tt, self._p5_unit(0, 0), cfg)
        for h in range(NH_F):
            for qt in range(NT_OWN):
                uid = h * NT_OWN + qt
                u = self._p5_unit(h, qt)
                nh, nqt = divmod(uid + 1, NT_OWN)
                if nh < NH_F:
                    if nqt == 0:
                        em.w(tt, f"kv{nh % 2}", ("p5kv", nh))
                    self._attn_unit_t_pass1(tt, self._p5_unit(nh, nqt), cfg)
                self._attn_unit_t_main(tt, u, cfg)
                em.w(tt, "dve", ("F5oev", uid))
                em.wopt(tt, "dve", ("p5otev", uid - 1))
                em.i(tt.transpose(t["PSB"][:, 256:384], t["avtm"][:, :],
                                  t["sident"][:, :]),
                     "pe", 1, label=("p5ot", uid))

    def p5_v(self, v):
        em, t = self.em, self.t
        cfg = self._p5_cfg()
        em.join(v, "p4")
        ofm = t["OFM"]
        self._attn_unit_v(v, self._p5_unit(0, 0), cfg)
        for h in range(NH_F):
            for qt in range(NT_OWN):
                uid = h * NT_OWN + qt
                u = self._p5_unit(h, qt)
                nh, nqt = divmod(uid + 1, NT_OWN)
                if nh < NH_F:
                    self._attn_unit_v(v, self._p5_unit(nh, nqt), cfg)
                self._attn_unit_v_ngr(v, u, cfg)
                self._attn_unit_v_post(v, u, cfg)
                em.w(v, "pe", ("p5ot", uid))
                em.i(v.tensor_copy(ofm[:, h * OWN + qt * 128: h * OWN + (qt + 1) * 128],
                                   t["PSB"][:, 256:384]),
                     "dve", 1, label=("p5otev", uid))

    def p5_s(self, s):
        em = self.em
        cfg = self._p5_cfg()
        em.join(s, "p4")
        for h in range(NH_F):
            for qt in range(NT_OWN):
                self._attn_unit_s(s, self._p5_unit(h, qt), cfg)

    def alloc_p6(self, st):
        nc, t = self.nc, self.t
        e = st.enter_context
        t["PTB"] = e(nc.psum_tensor("PTB6", [128, 1024], BF16))
        t["swo6"] = e(nc.sbuf_tensor("swo6", [128, ND * D], BF16))
        t["sxf6"] = [e(nc.sbuf_tensor(f"sxf6{i}", [128, D], F32)) for i in range(2)]
        t["xn6"] = [e(nc.sbuf_tensor(f"xn6{i}", [128, D], F32)) for i in range(2)]
        t["x26"] = [e(nc.sbuf_tensor(f"x26{i}", [128, D], F32)) for i in range(2)]
        t["tt6"] = e(nc.sbuf_tensor("tt6", [128, D], BF16))
        t["Lb"] = e(nc.sbuf_tensor("Lb", [128, 16], F32))
        t["curb"] = e(nc.sbuf_tensor("curb", [128, 16], F32))
        t["maskb"] = e(nc.sbuf_tensor("maskb", [128, 16], F32))
        t["Eb"] = e(nc.sbuf_tensor("Eb", [128, 16], F32))
        t["mjs"] = e(nc.sbuf_tensor("mjs", [128, 8], F32))
        t["g6"] = e(nc.sbuf_tensor("g6", [128, 8], F32))
        t["junk"] = e(nc.sbuf_tensor("junk6", [128, 2048], F32))

    def p6_g(self, g):
        em, t, io = self.em, self.t, self.io
        em.join(g, "p5")
        for dc in range(ND):
            em.i(g.dma_start(t["swo6"][:, dc * D:(dc + 1) * D],
                             io["wo"][dc * 128:(dc + 1) * 128, :]),
                 "wo6", 16, label=("p6wo", dc))

    def p6_y(self, y):
        em, t, io = self.em, self.t, self.io
        em.join(y, "p5")
        for qt in range(NT_OWN):
            if qt >= 2:
                em.w(y, "act", ("p6sq", qt - 2))
            em.i(y.dma_start(t["sxf6"][qt % 2][:, :], io["x_own"][qt * 128:(qt + 1) * 128, :]),
                 f"x{qt % 2}", 16, label=("p6x", qt))
            if qt >= 1:
                em.w(y, "dve", ("p6x2", qt - 1, 3))
                em.i(y.dma_start(io["x2_sp"][(qt - 1) * 128:qt * 128, :],
                                 t["x26"][(qt - 1) % 2][:, :]),
                     f"x2s{(qt - 1) % 2}", 16, label=("p6x2sp", qt - 1))
        em.w(y, "dve", ("p6x2", NT_OWN - 1, 3))
        em.i(y.dma_start(io["x2_sp"][(NT_OWN - 1) * 128:NT_OWN * 128, :],
                         t["x26"][(NT_OWN - 1) % 2][:, :]),
             f"x2s{(NT_OWN - 1) % 2}", 16, label=("p6x2sp", NT_OWN - 1))
        if self.dev:
            em.w(y, "dve", ("p6cw", NT_OWN - 1))
            em.i(y.dma_start(io["tfm_tap"], t["OFM"][:, :]), "tap", 16)
            em.i(y.dma_start(io["cw_tap"], t["s_cw"][:, :]), "tap", 16)

    def p6_t(self, tt):
        em, t = self.em, self.t
        em.join(tt, "p5")
        PS = t["PS"]
        buf = t["OFM"]
        em.w(tt, "wo6", ("p6wo", ND - 1))
        for qt in range(NT_OWN):
            for og in range(4):
                em.wopt(tt, "dve", ("p6x2", qt - 1, og))
                for dc in range(ND):
                    inst = tt.matmul(PS[:, og * 512:(og + 1) * 512],
                                     buf[:, dc * OWN + qt * 128: dc * OWN + (qt + 1) * 128],
                                     t["swo6"][:, dc * D + og * 512: dc * D + (og + 1) * 512],
                                     start=(dc == 0), stop=(dc == ND - 1))
                em.i(inst, "pe", 1, label=("p6wom", qt, og))
            em.w(tt, "act", ("p6tt", qt))
            for half in range(2):
                if half == 0:
                    em.wopt(tt, "dve", ("p6ttev", qt - 1, 1))
                else:
                    em.w(tt, "dve", ("p6ttev", qt, 0))
                for k in range(8):
                    inst = tt.transpose(
                        t["PTB"][:, k * 128:(k + 1) * 128],
                        t["tt6"][:, (half * 8 + k) * 128:(half * 8 + k + 1) * 128],
                        t["sident"][:, :])
                em.i(inst, "pe", 1, label=("p6ttr", qt, half))
            em.w(tt, "dve", ("p6ttev", qt, 1))
            em.wopt(tt, "dve", ("p6gev", qt - 1))
            for dc in range(ND):
                inst = tt.matmul(PS[:, 2048:2064],
                                 buf[:, dc * OWN + qt * 128: dc * OWN + (qt + 1) * 128],
                                 t["sgate"][:, dc * 16:(dc + 1) * 16],
                                 start=(dc == 0), stop=(dc == ND - 1))
            em.i(inst, "pe", 1, label=("p6gate", qt))

    def p6_s(self, s):
        em, t = self.em, self.t
        em.join(s, "p5")
        for qt in range(NT_OWN):
            em.w(s, f"x{qt % 2}", ("p6x", qt))
            em.wopt(s, "dve", ("p6x2", qt - 2, 3))
            em.i(s.activation(t["xn6"][qt % 2][:, :], t["sxf6"][qt % 2][:, :], AF.Identity,
                              bias=t["bias2_s"][:, qt:qt + 1], scale=t["rstd_s"][:, qt:qt + 1]),
                 "act", 1, label=("p6sq", qt))
            em.w(s, "dve", ("p6x2", qt, 3))
            em.selfsem(s, s.activation(t["junk"][:, :], t["x26"][qt % 2][:, :], AF.Square,
                                       bias=0.0, scale=1.0,
                                       accum_out=t["g6"][:, 0:1]), "ssf")
            em.selfsem(s, s.activation(t["g6"][:, 1:2], t["g6"][:, 0:1], AF.Ln,
                                       bias=EPS_RMS, scale=1.0 / D), "ssf")
            em.selfsem(s, s.activation(t["g6"][:, 2:3], t["g6"][:, 1:2], AF.Exp,
                                       bias=0.0, scale=-0.5), "ssf")
            em.wopt(s, "pe", ("p6ttr", qt - 1, 1))
            em.i(s.activation(t["tt6"][:, :], t["x26"][qt % 2][:, :], AF.Identity,
                              bias=0.0, scale=t["g6"][:, 2:3]),
                 "act", 1, label=("p6tt", qt))
            em.w(s, "dve", ("p6negm1", qt))
            em.i(s.activation(t["Eb"][:, :], t["Lb"][:, :], AF.Exp,
                              bias=t["g6"][:, 4 + qt % 2: 5 + qt % 2], scale=1.0),
                 "act", 1, label=("p6exp", qt))

    def p6_v(self, v):
        em, t = self.em, self.t
        em.join(v, "p5")
        PS = t["PS"]
        tfm = t["OFM"]
        for qt in range(NT_OWN):
            em.wopt(v, "act", ("p6tt", qt - 2))
            em.wopt(v, f"x2s{qt % 2}", ("p6x2sp", qt - 2))
            for og in range(4):
                em.w(v, "pe", ("p6wom", qt, og))
                em.w(v, "act", ("p6sq", qt))
                em.i(v.scalar_tensor_tensor(
                    t["x26"][qt % 2][:, og * 512:(og + 1) * 512],
                    PS[:, og * 512:(og + 1) * 512], 1.0,
                    t["xn6"][qt % 2][:, og * 512:(og + 1) * 512], ALU.mult, ALU.add),
                    "dve", 1, label=("p6x2", qt, og))
            for half in range(2):
                em.w(v, "pe", ("p6ttr", qt, half))
                for k in range(8):
                    dc = half * 8 + k
                    inst = v.tensor_scalar_mul(
                        tfm[:, dc * OWN + qt * 128: dc * OWN + (qt + 1) * 128],
                        t["PTB"][:, k * 128:(k + 1) * 128],
                        t["ln2_col"][:, dc:dc + 1])
                em.i(inst, "dve", 1, label=("p6ttev", qt, half))
            em.w(v, "pe", ("p6gate", qt))
            em.selfsem(v, v.tensor_copy(t["Lb"][:, :], PS[:, 2048:2064]), "vsf")
            em.selfsem(v, v.tensor_copy(t["curb"][:, :], t["Lb"][:, :]), "vsf")
            em.mark("dve", ("p6gev", qt))
            for j in range(6):
                em.selfsem(v, v.reduce_max(t["mjs"][:, j:j + 1], t["curb"][:, :], axis=AX.X),
                           "vsf")
                if j < 5:
                    em.selfsem(v, v.tensor_single_scalar(t["maskb"][:, :], t["curb"][:, :],
                                                         t["mjs"][:, j:j + 1], ALU.is_ge),
                               "vsf")
                    em.selfsem(v, v.scalar_tensor_tensor(t["curb"][:, :], t["maskb"][:, :],
                                                         -1e30, t["curb"][:, :],
                                                         ALU.mult, ALU.add), "vsf")
            em.selfsem(v, v.tensor_single_scalar(t["maskb"][:, :], t["Lb"][:, :],
                                                 t["mjs"][:, 5:6], ALU.is_ge), "vsf")
            em.wopt(v, "act", ("p6exp", qt - 2))
            em.i(v.tensor_scalar_mul(t["g6"][:, 4 + qt % 2: 5 + qt % 2],
                                     t["mjs"][:, 0:1], -1.0),
                 "dve", 1, label=("p6negm1", qt))
            em.w(v, "act", ("p6exp", qt))
            em.selfsem(v, v.tensor_mul(t["Eb"][:, :], t["Eb"][:, :], t["maskb"][:, :]), "vsf")
            em.selfsem(v, v.reduce_sum(t["g6"][:, 6:7], t["Eb"][:, :], axis=AX.X), "vsf")
            em.selfsem(v, v.reciprocal(t["g6"][:, 7:8], t["g6"][:, 6:7]), "vsf")
            em.i(v.tensor_scalar_mul(t["s_cw"][:, qt * 16:(qt + 1) * 16], t["Eb"][:, :],
                                     t["g6"][:, 7:8]),
                 "dve", 1, label=("p6cw", qt))

    def alloc_p7(self, st):
        nc, t = self.nc, self.t
        e = st.enter_context
        t["PS1"] = e(nc.psum_tensor("PS1", [128, 512], F32))
        t["hh"] = e(nc.sbuf_tensor("hh", [128, SUBI * 1024], BF16))
        t["hsil"] = e(nc.sbuf_tensor("hsil", [128, 1024], BF16))
        t["seg"] = [e(nc.sbuf_tensor(f"seg{i}", [128, ND * 128], BF16)) for i in range(2)]
        t["seu"] = [e(nc.sbuf_tensor(f"seu{i}", [128, ND * 128], BF16)) for i in range(2)]
        t["sed"] = e(nc.sbuf_tensor("sed", [128, SUBI * D], BF16))
        t["acc"] = e(nc.sbuf_tensor("acc", [128, 8 * D], F32))
        t["fxs"] = e(nc.sbuf_tensor("fxs", [128, D], F32))
        self.p7gu = []
        self.p7dn = []
        gix = 0
        dix = 0
        for half in range(2):
            for e_ in range(NEXP + 1):
                ni = NI_E if e_ < NEXP else NI_S
                subs = []
                s0 = 0
                while s0 < ni:
                    subs.append((s0, min(SUBI, ni - s0)))
                    s0 += SUBI
                for si, (s0, slen) in enumerate(subs):
                    for k in range(slen):
                        self.p7gu.append((half, e_, si, s0 + k, gix))
                        gix += 1
                    self.p7dn.append((half, e_, si, s0, slen, dix))
                    dix += 1
        self.p7subs = {}
        for half, e_, si, s0, slen, dix_ in self.p7dn:
            self.p7subs.setdefault((half, e_), []).append((si, s0, slen, dix_))

    def _wsl(self, name, e_, oc):
        io = self.io
        if e_ < NEXP:
            w = io["egT"] if name == "g" else io["euT"]
            return w[e_, :, oc * 128:(oc + 1) * 128].rearrange("(c p) o -> p c o", p=128)
        w = io["sgT"] if name == "g" else io["suT"]
        return w[:, oc * 128:(oc + 1) * 128].rearrange("(c p) o -> p c o", p=128)

    def _wdl(self, e_, ic):
        io = self.io
        if e_ < NEXP:
            return io["edT"][e_, ic * 128:(ic + 1) * 128, :]
        return io["sdT"][ic * 128:(ic + 1) * 128, :]

    def p7_g(self, g):
        em, t, io = self.em, self.t, self.io
        em.join(g, "p6")
        dn_by_gix = {}
        for half, e_, si, s0, slen, dix in self.p7dn:
            dn_by_gix[(half, e_, si)] = (s0, slen, dix)
        for half in range(2):
            for half2, e_, si, oc, gix in self.p7gu:
                if half2 != half:
                    continue
                em.wopt(g, "pe", ("p7um", gix - 2))
                em.i(g.dma_start(t["seg"][gix % 2][:, :].rearrange("p (c o) -> p c o", c=ND),
                                 self._wsl("g", e_, oc)), f"gw{gix % 2}", 16)
                em.i(g.dma_start(t["seu"][gix % 2][:, :].rearrange("p (c o) -> p c o", c=ND),
                                 self._wsl("u", e_, oc)), f"gw{gix % 2}", 16,
                     label=("p7wu", gix))
                s0, slen, dix = dn_by_gix[(half, e_, si)]
                if oc == s0 + slen - 1:
                    em.wopt(g, "pe", ("p7dnl", dix - 1))
                    for k in range(slen):
                        em.i(g.dma_start(t["sed"][:, k * D:(k + 1) * D],
                                         self._wdl(e_, s0 + k)),
                             "sed", 16, label=("p7wd", dix, k))
            for ttl in range(8):
                r = half * 8 + ttl
                em.w(g, "fx7", ("p7f1", half, ttl))
                em.i(g.dma_start(t["fxs"][:, :], io["ao_sp"][r * 128:(r + 1) * 128, :],
                                 accum_op=ALU.add), "ac7", 16, label=("p7ac1", half, ttl))
                em.w(g, "ac7", ("p7ac1", half, ttl))
                em.i(g.dma_start(t["fxs"][:, :], io["x2_sp"][r * 128:(r + 1) * 128, :],
                                 accum_op=ALU.add), "ac7", 16, label=("p7ac", half, ttl))

    def p7_t(self, tt):
        em, t = self.em, self.t
        em.join(tt, "p6")
        PS, PS1 = t["PS"], t["PS1"]
        tfm = t["OFM"]
        pg = [PS[:, 0:512], PS[:, 512:1024]]
        pu = [PS[:, 1024:1536], PS[:, 1536:2048]]
        pdown = [PS[:, 2048:2560], PS[:, 2560:3072], PS[:, 3072:3584], PS1[:, :]]

        gu_iter = iter(self.p7gu)
        for half, e_, subs_key in [(h, e, None) for h in range(2) for e in range(NEXP + 1)]:
            subs = self.p7subs[(half, e_)]
            for si, s0, slen, dix in subs:
                for k in range(slen):
                    half2, e2, si2, oc, gix = next(gu_iter)
                    assert (half2, e2, si2, oc) == (half, e_, si, s0 + k)
                    em.w(tt, f"gw{gix % 2}", ("p7wu", gix))
                    em.wopt(tt, "act", ("p7sil", gix - 1))
                    for gr in range(2):
                        for dc in range(ND):
                            col = dc * OWN + half * 1024 + gr * 512
                            inst = tt.matmul(pg[gr],
                                             t["seg"][gix % 2][:, dc * 128:(dc + 1) * 128],
                                             tfm[:, col:col + 512],
                                             start=(dc == 0), stop=(dc == ND - 1))
                    em.i(inst, "pe", 1, label=("p7gm", gix))
                    em.wopt(tt, "dve", ("p7hh", gix - 1))
                    for gr in range(2):
                        for dc in range(ND):
                            col = dc * OWN + half * 1024 + gr * 512
                            inst = tt.matmul(pu[gr],
                                             t["seu"][gix % 2][:, dc * 128:(dc + 1) * 128],
                                             tfm[:, col:col + 512],
                                             start=(dc == 0), stop=(dc == ND - 1))
                    em.i(inst, "pe", 1, label=("p7um", gix))
                em.w(tt, "sed", ("p7wd", dix, slen - 1))
                for ttl in range(8):
                    if ttl > 0:
                        em.w(tt, "dve", ("p7ev", dix, ttl - 1))
                    else:
                        em.wopt(tt, "dve", ("p7ev", dix - 1, 7))
                    for k in range(slen):
                        em.w(tt, "dve", ("p7hh", self._p7gix(half, e_, si, s0 + k)))
                        for dg in range(4):
                            inst = tt.matmul(
                                pdown[dg],
                                t["hh"][:, k * 1024 + ttl * 128: k * 1024 + ttl * 128 + 128],
                                t["sed"][:, k * D + dg * 512: k * D + (dg + 1) * 512],
                                start=(k == 0), stop=(k == slen - 1))
                    em.i(inst, "pe", 1, label=("p7dn", dix, ttl))
                em.mark("pe", ("p7dnl", dix))

    def _p7gix(self, half, e_, si, oc):
        if not hasattr(self, "_p7gixmap"):
            self._p7gixmap = {}
            for h2, e2, si2, oc2, gix in self.p7gu:
                self._p7gixmap[(h2, e2, si2, oc2)] = gix
        return self._p7gixmap[(half, e_, si, oc)]

    def p7_s(self, s):
        em, t = self.em, self.t
        em.join(s, "p6")
        for half, e_, si, oc, gix in self.p7gu:
            em.w(s, "pe", ("p7gm", gix))
            em.wopt(s, "dve", ("p7hh", gix - 1))
            em.i(s.activation(t["hsil"][:, :], t["PS"][:, 0:1024], AF.Silu,
                              bias=0.0, scale=1.0),
                 "act", 1, label=("p7sil", gix))

    def p7_v(self, v):
        em, t = self.em, self.t
        em.join(v, "p6")
        PS, PS1 = t["PS"], t["PS1"]
        pdown = [PS[:, 2048:2560], PS[:, 2560:3072], PS[:, 3072:3584], PS1[:, :]]
        gu_iter = iter(self.p7gu)
        first_writer = {}
        for half in range(2):
            for e_ in range(NEXP + 1):
                subs = self.p7subs[(half, e_)]
                for si, s0, slen, dix in subs:
                    for k in range(slen):
                        half2, e2, si2, oc, gix = next(gu_iter)
                        em.w(v, "act", ("p7sil", gix))
                        em.w(v, "pe", ("p7um", gix))
                        em.i(v.tensor_mul(t["hh"][:, k * 1024:(k + 1) * 1024],
                                          t["hsil"][:, :], PS[:, 1024:2048]),
                             "dve", 1, label=("p7hh", gix))
                    for ttl in range(8):
                        em.w(v, "pe", ("p7dn", dix, ttl))
                        em.wopt(v, "dve", ("p7ev", dix - 1, ttl))
                        cwcol = 256 if e_ == NEXP else (half * 8 + ttl) * 16 + e_
                        fw = first_writer.get((half, ttl))
                        for dg in range(4):
                            if fw is None:
                                inst = v.tensor_scalar_mul(
                                    t["acc"][:, ttl * D + dg * 512: ttl * D + (dg + 1) * 512],
                                    pdown[dg], t["s_cw"][:, cwcol:cwcol + 1])
                            else:
                                inst = v.scalar_tensor_tensor(
                                    t["acc"][:, ttl * D + dg * 512: ttl * D + (dg + 1) * 512],
                                    pdown[dg], t["s_cw"][:, cwcol:cwcol + 1],
                                    t["acc"][:, ttl * D + dg * 512: ttl * D + (dg + 1) * 512],
                                    ALU.mult, ALU.add)
                        first_writer[(half, ttl)] = True
                        em.i(inst, "dve", 1, label=("p7ev", dix, ttl))
            for ttl in range(8):
                last_dix = self.p7subs[(half, NEXP)][-1][3]
                em.w(v, "ac7", ("p7ac", half, ttl))
                em.w(v, "dve", ("p7ev", last_dix, ttl))
                em.i(v.tensor_add(t["fxs"][:, :], t["fxs"][:, :],
                                  t["acc"][:, ttl * D:(ttl + 1) * D]),
                     "dve", 1, label=("p7out", half, ttl))
            first_writer = {}

    def p7_y(self, y):
        em, t, io = self.em, self.t, self.io
        em.join(y, "p6")
        for half in range(2):
            for ttl in range(8):
                r = half * 8 + ttl
                if r >= 1:
                    em.w(y, "st7", ("p7st", (r - 1) // 8, (r - 1) % 8))
                em.i(y.dma_start(t["fxs"][:, :], io["x_own"][r * 128:(r + 1) * 128, :]),
                     "fx7", 16, label=("p7f1", half, ttl))
                em.w(y, "dve", ("p7out", half, ttl))
                em.i(y.dma_start(io["out"][r * 128:(r + 1) * 128, :], t["fxs"][:, :]),
                     "st7", 16, label=("p7st", half, ttl))
        em.w(y, "st7", ("p7st", 1, 7))

    def build(self, nphases=99):
        self.declare_dram()
        nc, em = self.nc, self.em
        with ExitStack() as persist:
            self.alloc_persist(persist)
            em.attach(nc, persist)
            block = persist.enter_context(nc.Block())
            mocks = {k: MockEng() for k in "tvsgy"}

            def run_phase(name, fns, allocf=None, pre=None, post=None):
                with ExitStack() as st:
                    if allocf:
                        allocf(st)
                    if pre:
                        pre()
                    em.p1()
                    for key in ("t", "v", "s", "g", "y"):
                        if key in fns:
                            fns[key](mocks[key])
                    em.endphase(name)
                    em.p2()
                    for key, section in (("t", block.tensor), ("v", block.vector),
                                         ("s", block.scalar), ("g", block.gpsimd),
                                         ("y", block.sync)):
                        if key in fns:
                            section(fns[key])
                    em.endphase(name)
                if post:
                    post()

            plist = [
                ("p0", lambda: run_phase("p0", dict(v=self.p0_v, g=self.p0_g, y=self.p0_y))),
                ("p1", lambda: (self.big_alloc("xn_fm", (128, ND * SEQ), BF16),
                                run_phase("p1", dict(t=self.p1_t, v=self.p1_v, s=self.p1_s,
                                                     g=self.p1_g, y=self.p1_y), self.alloc_p1))),
                ("p2", lambda: run_phase("p2", dict(t=self.p2_t, v=self.p2_v, s=self.p2_s,
                                                    y=self.p2_y), self.alloc_p2)),
                ("p3", lambda: run_phase("p3", dict(t=self.p3_t, v=self.p3_v, g=self.p3_g,
                                                    y=self.p3_y), self.alloc_p3,
                                         post=lambda: self.big_free("xn_fm"))),
                ("p4", lambda: run_phase("p4", dict(t=self.p4_t, v=self.p4_v, s=self.p4_s,
                                                    g=self.p4_g, y=self.p4_y), self.alloc_p4)),
                ("p5", lambda: (self.big_alloc("OFM", (128, NH_F * OWN), BF16),
                                run_phase("p5", dict(t=self.p5_t, v=self.p5_v, s=self.p5_s,
                                                     g=self.p5_g, y=self.p5_y), self.alloc_p5))),
                ("p6", lambda: run_phase("p6", dict(t=self.p6_t, v=self.p6_v, s=self.p6_s,
                                                    g=self.p6_g, y=self.p6_y), self.alloc_p6)),
                ("p7", lambda: run_phase("p7", dict(t=self.p7_t, v=self.p7_v, s=self.p7_s,
                                                    g=self.p7_g, y=self.p7_y), self.alloc_p7,
                                         post=lambda: self.big_free("OFM"))),
            ]
            lastnm = None
            for i, (nm_, fn_) in enumerate(plist):
                if i < nphases:
                    fn_()
                    lastnm = nm_
            fin = lastnm

            def mkfin(engname):
                def f(eng):
                    em.join(eng, fin)
                return f
            run_phase("fin", {k: mkfin(k) for k in ("t", "v", "s", "g", "y")})
            for nm_ in list(reversed(list(self.guards.keys()))):
                self.big_free(nm_)
        return nc


def build_program(dev=False, nphases=99):
    return Builder(dev=dev).build(nphases)


_BF = ml_dtypes.bfloat16


def _prep_weights(inputs):
    w = {}
    w["Wqkv"] = np.ascontiguousarray(inputs["L0_attn_Wqkv"], np.float32)
    bqkv = np.asarray(inputs["L0_attn_bqkv"], np.float32)
    w["bqkv_col"] = np.ascontiguousarray(bqkv.reshape(24, 128).T)
    w["bv_row"] = np.ascontiguousarray(bqkv[2560:3072][None, :])
    w["Wo"] = np.ascontiguousarray(inputs["L0_attn_Wo"], np.float32)
    w["bo_row"] = np.ascontiguousarray(np.asarray(inputs["L0_attn_bo"], np.float32)[None, :])
    w["wq"] = np.ascontiguousarray(inputs["L0_ffn_wq"], np.float32)
    w["wk"] = np.ascontiguousarray(inputs["L0_ffn_wk"], np.float32)
    w["wv"] = np.ascontiguousarray(inputs["L0_ffn_wv"], np.float32)
    w["wo"] = np.ascontiguousarray(inputs["L0_ffn_wo"], np.float32)
    w["ln1_col"] = np.ascontiguousarray(
        np.asarray(inputs["L0_ffn_ln1_w"], np.float32).reshape(16, 128).T)
    w["ln2_col"] = np.ascontiguousarray(
        np.asarray(inputs["L0_ffn_ln2_w"], np.float32).reshape(16, 128).T)
    w["gate_T"] = np.ascontiguousarray(np.asarray(inputs["L0_ffn_gate_w"], np.float32).T)
    w["egT"] = np.ascontiguousarray(
        np.asarray(inputs["L0_ffn_eg_w"], np.float32).transpose(0, 2, 1))
    w["euT"] = np.ascontiguousarray(
        np.asarray(inputs["L0_ffn_eu_w"], np.float32).transpose(0, 2, 1))
    w["edT"] = np.ascontiguousarray(
        np.asarray(inputs["L0_ffn_ed_w"], np.float32).transpose(0, 2, 1))
    w["sgT"] = np.ascontiguousarray(np.asarray(inputs["L0_ffn_sg_w"], np.float32).T)
    w["suT"] = np.ascontiguousarray(np.asarray(inputs["L0_ffn_su_w"], np.float32).T)
    w["sdT"] = np.ascontiguousarray(np.asarray(inputs["L0_ffn_sd_w"], np.float32).T)
    w["consts_in"] = np.tile(np.array([[EPS_LN, EPS_RMS, 1.0 / D, -0.5, SC_A, SC_F,
                                        0.0, 0.0]], np.float32), (128, 1))
    w["ident_bf"] = np.eye(128, dtype=_BF)
    w["ones_bf"] = np.ones((1, 128), _BF)
    w["ones_f32"] = np.ones((1, 128), np.float32)
    return w


def kernel(**inputs):
    x = np.asarray(inputs["L0_attn_x"], np.float32)
    w = _prep_weights(inputs)
    nc = build_program(dev=False)
    in_maps = []
    for c in range(8):
        b = c // 2
        h = c % 2
        m = dict(w)
        m["x_own"] = np.ascontiguousarray(x[b, h * 2048:(h + 1) * 2048, :])
        m["x_oth"] = np.ascontiguousarray(x[b, (1 - h) * 2048:(2 - h) * 2048, :])
        m["qbase_col"] = np.full((128, 1), float(h * 2048), np.float32)
        m["othbase_col"] = np.full((128, 1), float(-h * 2048), np.float32)
        in_maps.append(m)
    res = run_bass_kernel_spmd(nc, in_maps, core_ids=list(range(8)))
    out = np.empty((4, 4096, 2048), np.float32)
    for c in range(8):
        b = c // 2
        h = c % 2
        out[b, h * 2048:(h + 1) * 2048, :] = res.results[c]["out"]
    return out


# version 09 (final)
# speedup vs baseline: 1.2551x; 1.0263x over previous
import numpy as np
import ml_dtypes
from contextlib import ExitStack

import concourse.bass as bass
import concourse.mybir as mybir
from concourse.bass_utils import run_bass_kernel_spmd

F32 = mybir.dt.float32
BF16 = mybir.dt.bfloat16
AF = mybir.ActivationFunctionType
ALU = mybir.AluOpType
AX = mybir.AxisListType

D = 2048
ND = 16
SEQ = 4096
OWN = 2048
NT_OWN = 16
NT_ALL = 32
HD_A = 64
NH_A = 32
SC_A = 0.125
HD_F = 128
NH_F = 16
SC_F = 1.0 / float(np.sqrt(128.0))
NEXP = 16
I_E = 1408
NI_E = 11
SI = 2816
NI_S = 22
EPS_LN = 1e-5
EPS_RMS = 1e-6
KTA = 24
KTB = 8
SUBI = 6
CONSTS = (EPS_LN, EPS_RMS, 1.0 / D, -0.5, SC_A, SC_F)

SEMS = ("pe", "dve", "act", "pool", "vsf", "ssf", "psf")


class _MI:
    def then_inc(self, *a, **k):
        return self


class MockEng:
    def __getattr__(self, name):
        return lambda *a, **k: _MI()


class Em:

    def __init__(self):
        self.lab = {}
        self.cnt = {n: 0 for n in SEMS}
        self.hs = {}
        self.mode = 1
        self._snap = None
        self._p1end = None
        self.nc = None
        self.stack = None

    def attach(self, nc, stack):
        self.nc = nc
        self.stack = stack
        for n in SEMS:
            self.hs[n] = stack.enter_context(nc.semaphore(n))

    def _s(self, name):
        if name not in self.cnt:
            self.cnt[name] = 0
        if self.mode == 2 and name not in self.hs:
            self.hs[name] = self.stack.enter_context(self.nc.semaphore(name))
        return name

    def p1(self):
        self.mode = 1
        self._snap = dict(self.cnt)

    def p2(self):
        self.mode = 2
        self._p1end = dict(self.cnt)
        self.cnt = dict(self._snap)

    def endphase(self, phase):
        if self.mode == 1:
            self.lab[("__endd", phase)] = dict(self.cnt)
        else:
            assert self.cnt == self._p1end, (phase, self.cnt, self._p1end)

    def _key(self, s, label):
        return (s,) + (label if isinstance(label, tuple) else (label,))

    def i(self, inst, s, k=1, label=None):
        self._s(s)
        self.cnt[s] += k
        if label is not None:
            key = self._key(s, label)
            if self.mode == 1:
                self.lab[key] = self.cnt[s]
            else:
                assert self.lab[key] == self.cnt[s], (key, self.lab[key], self.cnt[s])
        if self.mode == 2:
            inst.then_inc(self.hs[s], k)

    def mark(self, s, label):
        if self.mode == 1:
            self.lab[self._key(s, label)] = self.cnt[s]

    def w(self, eng, s, label):
        self._s(s)
        if self.mode == 2:
            eng.wait_ge(self.hs[s], self.lab[self._key(s, label)])

    def wopt(self, eng, s, label):
        self._s(s)
        if self.mode == 2:
            v = self.lab.get(self._key(s, label))
            if v is not None:
                eng.wait_ge(self.hs[s], v)

    def join(self, eng, prev_phase):
        if self.mode == 2:
            for s, v in sorted(self.lab[("__endd", prev_phase)].items()):
                if v > 0:
                    eng.wait_ge(self.hs[self._s(s)], v)

    def selfsem(self, eng, inst, s):
        self.cnt[s] += 1
        if self.mode == 2:
            inst.then_inc(self.hs[s], 1)
            eng.wait_ge(self.hs[s], self.cnt[s])


class Builder:
    def __init__(self, dev=False):
        self.dev = dev
        self.nc = bass.Bass("TRN2", target_bir_lowering=False)
        self.em = Em()
        self.io = {}
        self.t = {}
        self.guards = {}

    def declare_dram(self):
        nc, io = self.nc, self.io

        def inp(name, shape, dt=F32):
            io[name] = nc.dram_tensor(name, list(shape), dt, kind="ExternalInput").ap()

        inp("x_own", (OWN, D)); inp("x_oth", (OWN, D))
        inp("qbase_col", (128, 1)); inp("othbase_col", (128, 1))
        inp("Wqkv", (D, 3072)); inp("bqkv_col", (128, 24)); inp("bv_row", (1, 512))
        inp("Wo", (D, D)); inp("bo_row", (1, D))
        inp("wq", (D, D)); inp("wk", (D, D)); inp("wv", (D, D)); inp("wo", (D, D))
        inp("ln1_col", (128, 16)); inp("ln2_col", (128, 16))
        inp("gate_T", (D, 16))
        inp("egT", (NEXP, D, I_E)); inp("euT", (NEXP, D, I_E)); inp("edT", (NEXP, I_E, D))
        inp("sgT", (D, SI)); inp("suT", (D, SI)); inp("sdT", (SI, D))
        inp("consts_in", (128, 8))
        inp("ident_bf", (128, 128), BF16)
        inp("ones_bf", (1, 128), BF16)
        inp("ones_f32", (1, 128))

        io["out"] = nc.dram_tensor("out", [OWN, D], F32, kind="ExternalOutput").ap()

        spk = "ExternalOutput" if self.dev else "Internal"

        def spill(name, shape, dt=BF16, kind=None):
            io[name] = nc.dram_tensor(name, list(shape), dt, kind=kind or spk).ap()

        spill("qa_sp", (16, 128, OWN))
        spill("ka_sp", (4, 128, SEQ))
        spill("va_sp", (SEQ, 512))
        spill("qf_sp", (16, 128, OWN))
        spill("kf_sp", (16, 128, SEQ))
        spill("vf_sp", (SEQ, D))
        spill("ao_sp", (OWN, D), F32)
        spill("x2_sp", (OWN, D), F32)
        spill("r1_bounce", (32, 128), BF16, kind="Internal")
        spill("mask_sp", (16, 128, SEQ), BF16, kind="Internal")
        if self.dev:
            spill("h1_tap", (128, ND * SEQ), BF16)
            spill("tfm_tap", (128, ND * OWN), BF16)
            spill("cw_tap", (128, 272), F32)
            spill("ofm_tap", (128, NH_F * OWN), BF16)

    def alloc_persist(self, stack):
        nc, t = self.nc, self.t
        e = stack.enter_context
        t["PS"] = e(nc.psum_tensor("PS", [128, 3584], F32))
        t["negmean_s"] = e(nc.sbuf_tensor("negmean_s", [128, 32], F32))
        t["rstd_s"] = e(nc.sbuf_tensor("rstd_s", [128, 32], F32))
        t["vsum_s"] = e(nc.sbuf_tensor("vsum_s", [128, 32], F32))
        t["bias2_s"] = e(nc.sbuf_tensor("bias2_s", [128, 32], F32))
        t["r1_s"] = e(nc.sbuf_tensor("r1_s", [128, 32], F32))
        t["sb_qbase"] = e(nc.sbuf_tensor("sb_qbase", [128, 1], F32))
        t["sb_othbase"] = e(nc.sbuf_tensor("sb_othbase", [128, 1], F32))
        t["bqkv_col"] = e(nc.sbuf_tensor("sb_bqkv_col", [128, 24], F32))
        t["bv_row"] = e(nc.sbuf_tensor("sb_bv_row", [1, 512], F32))
        t["bo_row"] = e(nc.sbuf_tensor("sb_bo_row", [1, D], F32))
        t["ln1_col"] = e(nc.sbuf_tensor("sb_ln1_col", [128, 16], F32))
        t["ln2_col"] = e(nc.sbuf_tensor("sb_ln2_col", [128, 16], F32))
        t["sgate"] = e(nc.sbuf_tensor("sgate", [128, 256], BF16))
        t["s_cw"] = e(nc.sbuf_tensor("s_cw", [128, 272], F32))
        t["sident"] = e(nc.sbuf_tensor("sident", [128, 128], BF16))
        t["sones_bf"] = e(nc.sbuf_tensor("sones_bf", [1, 128], BF16))
        t["sones_f"] = e(nc.sbuf_tensor("sones_f", [1, 128], F32))
        t["stat8"] = e(nc.sbuf_tensor("stat8", [128, 16], F32))
        t["sconsts"] = e(nc.sbuf_tensor("sconsts", [128, 8], F32))
        for i, val in enumerate(CONSTS):
            nc.const_aps.register(F32, val, t["sconsts"][:, i:i + 1])

    def big_alloc(self, name, shape, dt):
        g = self.nc.sbuf_tensor(name, list(shape), dt)
        h = g.__enter__()
        self.guards[name] = g
        self.t[name] = h
        return h

    def big_free(self, name):
        self.guards.pop(name).__exit__(None, None, None)
        self.t.pop(name, None)

    def p0_y(self, y):
        em, t, io = self.em, self.t, self.io
        for nm, dst in (("ident_bf", "sident"), ("ones_bf", "sones_bf"),
                        ("ones_f32", "sones_f"), ("bqkv_col", "bqkv_col"),
                        ("bv_row", "bv_row"), ("bo_row", "bo_row"),
                        ("ln1_col", "ln1_col"), ("ln2_col", "ln2_col"),
                        ("qbase_col", "sb_qbase"), ("othbase_col", "sb_othbase"),
                        ("consts_in", "sconsts")):
            em.i(y.dma_start(t[dst][:, :], io[nm]), "c0", 16)

    def p0_g(self, g):
        em, t, io = self.em, self.t, self.io
        em.i(g.dma_start(t["sgate"][:, :].rearrange("p (c e) -> p c e", c=16),
                         io["gate_T"].rearrange("(c p) e -> p c e", p=128)),
             "cg0", 16, label="p0gate")

    def p0_v(self, v):
        em, t = self.em, self.t
        em.i(v.memset(t["s_cw"][:, 256:272], 1.0), "dve", 1, label="p0cw1")

    def alloc_p1(self, st):
        nc, t = self.nc, self.t
        e = st.enter_context
        t["PTB"] = e(nc.psum_tensor("PTB1", [128, 1024], BF16))
        t["sxf"] = [e(nc.sbuf_tensor(f"sxf{i}", [128, D], F32)) for i in range(2)]
        t["sxn"] = [e(nc.sbuf_tensor(f"sxn{i}", [128, D], BF16)) for i in range(2)]
        t["swb"] = [e(nc.sbuf_tensor(f"swb{i}", [128, ND * 128], BF16)) for i in range(2)]
        t["swv"] = e(nc.sbuf_tensor("swv1", [128, ND * 512], BF16))
        t["stage"] = [e(nc.sbuf_tensor(f"stage{i}", [128, D], BF16)) for i in range(2)]
        t["vstage"] = [e(nc.sbuf_tensor(f"vstage{i}", [128, 512], BF16)) for i in range(2)]
        t["rs_t"] = e(nc.sbuf_tensor("rs_t", [128, 2], F32))
        t["iota128"] = e(nc.sbuf_tensor("iota128", [128, 128], F32))
        t["thr0"] = e(nc.sbuf_tensor("thr0", [128, 32], F32))
        t["thrq"] = [e(nc.sbuf_tensor(f"thrq{i}", [128, 32], F32)) for i in range(2)]
        t["obq"] = e(nc.sbuf_tensor("obq", [128, 1], F32))

        seq = []
        for oc in range(20):
            for rnd in range(1 if oc < 16 else 2):
                seq.append((oc, rnd))
        self.p1seq = seq

    def _xtile_src(self, r):
        io = self.io
        if r < 16:
            return io["x_own"][r * 128:(r + 1) * 128, :]
        return io["x_oth"][(r - 16) * 128:(r - 16 + 1) * 128, :]

    def p1_y(self, y):
        em, t, io = self.em, self.t, self.io
        em.join(y, "p0")
        for r in range(NT_ALL):
            if r >= 2:
                em.w(y, "act", ("p1xn", r - 2))
            em.i(y.dma_start(t["sxf"][r % 2][:, :], self._xtile_src(r)),
                 f"x{r % 2}", 16, label=("p1x", r))
        for si, (oc, rnd) in enumerate(self.p1seq):
            em.w(y, "dve", ("p1pjev", si))
            st = t["stage"][si % 2]
            if oc < 16:
                for h, srcband in ((2 * oc, 0), (2 * oc + 1, 1)):
                    m, rem = divmod(h, 8)
                    band = rem // 4
                    rr = rem % 4
                    c_t = 4 * m + rr
                    em.i(y.dma_start(io["qa_sp"][c_t, band * 64:(band + 1) * 64, :],
                                     st[srcband * 64:(srcband + 1) * 64, :]),
                         f"sp{si % 2}", 16,
                         label=("p1sp", si) if h % 2 == 1 else None)
            else:
                em.i(y.dma_start(io["ka_sp"][oc - 16, :, rnd * 2048:(rnd + 1) * 2048],
                                 st[:, :]), f"sp{si % 2}", 16, label=("p1sp", si))
        for r in range(NT_ALL):
            em.w(y, "dve", ("p1vev", r))
            em.i(y.dma_start(io["va_sp"][r * 128:(r + 1) * 128, :], t["vstage"][r % 2][:, :]),
                 f"vs{r % 2}", 16, label=("p1vsp", r))

    def p1_s(self, s):
        em, t = self.em, self.t
        em.join(s, "p0")
        for r in range(NT_ALL):
            em.w(s, "dve", ("p1rs", r))
            inst = s.mul(t["negmean_s"][:, r:r + 1], t["rs_t"][:, r % 2:r % 2 + 1], -1.0 / D)
            em.i(inst, "act", 1, label=("p1nm", r))
            em.w(s, "act", ("p1nm", r))
            em.selfsem(s, s.activation(t["stage"][0][:, :], t["sxf"][r % 2][:, :],
                                       AF.Square,
                                       bias=t["negmean_s"][:, r:r + 1], scale=1.0,
                                       accum_out=t["vsum_s"][:, r:r + 1]), "ssf")
            em.selfsem(s, s.activation(t["stat8"][:, 0:1], t["vsum_s"][:, r:r + 1], AF.Ln,
                                       bias=EPS_LN, scale=1.0 / D), "ssf")
            em.selfsem(s, s.activation(t["rstd_s"][:, r:r + 1], t["stat8"][:, 0:1], AF.Exp,
                                       bias=0.0, scale=-0.5), "ssf")
            em.selfsem(s, s.mul(t["bias2_s"][:, r:r + 1], t["negmean_s"][:, r:r + 1],
                                t["rstd_s"][:, r:r + 1]), "ssf")
            em.wopt(s, "pe", ("p1tB", r - 2))
            em.i(s.activation(t["sxn"][r % 2][:, :], t["sxf"][r % 2][:, :], AF.Identity,
                              bias=t["bias2_s"][:, r:r + 1], scale=t["rstd_s"][:, r:r + 1]),
                 "act", 1, label=("p1xn", r))

    def p1_v(self, v):
        em, t = self.em, self.t
        em.join(v, "p0")
        for qt in range(NT_OWN):
            bi = qt % 2
            em.w(v, "pool", ("p1thr", qt))
            em.wopt(v, f"mks{bi}", ("p1mks", qt - 2))
            mb = t["swv"][:, bi * SEQ:bi * SEQ + SEQ]
            in0 = bass.AP(t["iota128"], 0, [[128, 128], [0, 32], [1, 128]])
            in1 = bass.AP(t["thrq"][bi], 0, [[32, 128], [1, 32], [0, 128]])
            em.i(v.tensor_tensor(mb[:, :].rearrange("p (c q) -> p c q", c=32),
                                 in0, in1, ALU.is_ge),
                 "dve", 1, label=("p1mtt", qt))
        xn_fm = t["xn_fm"]
        for r in range(NT_ALL):
            em.w(v, f"x{r % 2}", ("p1x", r))
            em.wopt(v, "act", ("p1nm", r - 2))
            em.i(v.reduce_sum(t["rs_t"][:, r % 2:r % 2 + 1], t["sxf"][r % 2][:, :], axis=AX.X),
                 "dve", 1, label=("p1rs", r))
            em.w(v, "pe", ("p1tA", r))
            em.i(v.tensor_copy(
                xn_fm[:, :].rearrange("p (c k) -> p c k", c=ND)[:, 0:8, r * 128:(r + 1) * 128],
                t["PTB"][:, :].rearrange("p (a b) -> p a b", a=8)),
                "dve", 1, label=("p1eA", r))
            em.w(v, "pe", ("p1tB", r))
            em.i(v.tensor_copy(
                xn_fm[:, :].rearrange("p (c k) -> p c k", c=ND)[:, 8:16, r * 128:(r + 1) * 128],
                t["PTB"][:, :].rearrange("p (a b) -> p a b", a=8)),
                "dve", 1, label=("p1eB", r))
        PS = t["PS"]
        for si, (oc, rnd) in enumerate(self.p1seq):
            em.w(v, "pe", ("p1pj", si))
            em.wopt(v, f"sp{si % 2}", ("p1sp", si - 2))
            st = t["stage"][si % 2]
            for gi in range(4):
                inst = v.tensor_scalar_add(st[:, gi * 512:(gi + 1) * 512],
                                           PS[:, gi * 512:(gi + 1) * 512],
                                           t["bqkv_col"][:, oc:oc + 1])
            em.i(inst, "dve", 1, label=("p1pjev", si))
        for r in range(NT_ALL):
            em.w(v, "pe", ("p1v", r))
            em.wopt(v, f"vs{r % 2}", ("p1vsp", r - 2))
            em.i(v.tensor_copy(t["vstage"][r % 2][:, :], PS[:, (r % 4) * 512:(r % 4 + 1) * 512]),
                 "dve", 1, label=("p1vev", r))

    def p1_t(self, tt):
        em, t = self.em, self.t
        em.join(tt, "p0")
        for r in range(NT_ALL):
            em.w(tt, "act", ("p1xn", r))
            em.wopt(tt, "dve", ("p1eB", r - 1))
            for k in range(8):
                inst = tt.transpose(t["PTB"][:, k * 128:(k + 1) * 128],
                                    t["sxn"][r % 2][:, k * 128:(k + 1) * 128],
                                    t["sident"][:, :])
            em.i(inst, "pe", 1, label=("p1tA", r))
            em.w(tt, "dve", ("p1eA", r))
            for k in range(8):
                inst = tt.transpose(t["PTB"][:, k * 128:(k + 1) * 128],
                                    t["sxn"][r % 2][:, (8 + k) * 128:(9 + k) * 128],
                                    t["sident"][:, :])
            em.i(inst, "pe", 1, label=("p1tB", r))
        PS = t["PS"]
        xn_fm = t["xn_fm"]
        for si, (oc, rnd) in enumerate(self.p1seq):
            em.w(tt, f"w{oc % 2}", ("p1w", oc))
            if si == 0:
                em.w(tt, "dve", ("p1eB", NT_ALL - 1))
            em.wopt(tt, "dve", ("p1pjev", si - 1))
            for dc in range(ND):
                for gi in range(4):
                    col = dc * SEQ + rnd * 2048 + gi * 512
                    inst = tt.matmul(PS[:, gi * 512:(gi + 1) * 512],
                                     t["swb"][oc % 2][:, dc * 128:(dc + 1) * 128],
                                     xn_fm[:, col:col + 512],
                                     start=(dc == 0), stop=(dc == ND - 1))
            em.i(inst, "pe", 1, label=("p1pj", si))
            if rnd == (0 if oc < 16 else 1):
                em.mark("pe", ("p1pjoc", oc))
        em.w(tt, "wv", ("p1wv", ND - 1))
        em.w(tt, "dve", ("p1pjev", len(self.p1seq) - 1))
        for r in range(NT_ALL):
            em.wopt(tt, "dve", ("p1vev", r - 4))
            bank = r % 4
            inst = tt.matmul(PS[:, bank * 512:(bank + 1) * 512], t["sones_f"][:, :],
                             t["bv_row"][:, :], start=True, stop=False)
            for dc in range(ND):
                inst = tt.matmul(PS[:, bank * 512:(bank + 1) * 512],
                                 xn_fm[:, dc * SEQ + r * 128: dc * SEQ + (r + 1) * 128],
                                 t["swv"][:, dc * 512:(dc + 1) * 512],
                                 start=False, stop=(dc == ND - 1))
            em.i(inst, "pe", 1, label=("p1v", r))

    def p1_g(self, g):
        em, t, io = self.em, self.t, self.io
        em.join(g, "p0")
        em.selfsem(g, g.iota(t["iota128"][:, :], [[1, 128]], channel_multiplier=0,
                             allow_small_or_imprecise_dtypes=True), "psf")
        em.selfsem(g, g.iota(t["thr0"][:, :], [[128, 32]], channel_multiplier=1,
                             allow_small_or_imprecise_dtypes=True), "psf")
        em.selfsem(g, g.tensor_sub(t["obq"][:, :], t["sb_othbase"][:, :],
                                   t["sb_qbase"][:, :]), "psf")
        em.mark("pool", "p1iotas")
        for qt in range(NT_OWN):
            bi = qt % 2
            thrq = t["thrq"][bi]
            em.wopt(g, "dve", ("p1mtt", qt - 2))
            em.selfsem(g, g.tensor_scalar_add(thrq[:, 0:16], t["thr0"][:, 0:16],
                                              float(-128 * qt)), "psf")
            em.selfsem(g, g.tensor_scalar_add(thrq[:, 16:32], t["thr0"][:, 0:16],
                                              float(2048 - 128 * qt)), "psf")
            em.selfsem(g, g.tensor_scalar_add(thrq[:, 16:32], thrq[:, 16:32],
                                              t["obq"][:, :]), "psf")
            em.mark("pool", ("p1thr", qt))
            em.i(g.engine_nop() if False else g.tensor_copy(t["obq"][:, :], t["obq"][:, :]),
                 "pool", 1, label=("p1thr", qt))
            em.w(g, "dve", ("p1mtt", qt))
            mb = t["swv"][:, bi * SEQ:bi * SEQ + SEQ]
            em.i(g.dma_start(io["mask_sp"][qt, :, :], mb),
                 f"mks{qt % 2}", 16, label=("p1mks", qt))
        for oc in range(20):
            if oc >= 2:
                em.w(g, "pe", ("p1pjoc", oc - 2))
            em.i(g.dma_start(
                t["swb"][oc % 2][:, :].rearrange("p (c o) -> p c o", c=ND),
                io["Wqkv"][:, oc * 128:(oc + 1) * 128].rearrange("(c p) o -> p c o", p=128)),
                f"w{oc % 2}", 16, label=("p1w", oc))
        em.w(g, "mks0", ("p1mks", NT_OWN - 2))
        em.w(g, "mks1", ("p1mks", NT_OWN - 1))
        for dc in range(ND):
            em.i(g.dma_start(t["swv"][:, dc * 512:(dc + 1) * 512],
                             io["Wqkv"][dc * 128:(dc + 1) * 128, 2560:3072]),
                 "wv", 16, label=("p1wv", dc))

    def alloc_p2(self, st):
        nc, t = self.nc, self.t
        e = st.enter_context
        t["PTB"] = e(nc.psum_tensor("PTB2", [128, 1024], BF16))
        t["r1b"] = e(nc.sbuf_tensor("r1b", [128, 32], BF16))
        t["r1T"] = e(nc.sbuf_tensor("r1T", [32, 128], BF16))
        t["r1row"] = e(nc.sbuf_tensor("r1row", [1, SEQ], BF16))
        t["st2"] = e(nc.sbuf_tensor("st2", [128, 32], F32))

    def p2_v(self, v):
        em, t = self.em, self.t
        em.join(v, "p1")
        em.selfsem(v, v.tensor_mul(t["st2"][:, :], t["rstd_s"][:, :], t["rstd_s"][:, :]),
                   "vsf")
        em.i(v.tensor_mul(t["st2"][:, :], t["st2"][:, :], t["vsum_s"][:, :]),
             "dve", 1, label="p2t2")
        em.w(v, "act", "p2r1")
        em.i(v.tensor_copy(t["r1b"][:, :], t["r1_s"][:, :]), "dve", 1, label="p2r1b")
        em.w(v, "pe", "p2tr")
        em.i(v.tensor_copy(t["r1T"][0:32, :], t["PTB"][0:32, 0:128]),
             "dve", 1, label="p2trev")
        xn_fm = t["xn_fm"]
        for dc in range(ND):
            inst = v.tensor_scalar_mul(xn_fm[:, dc * SEQ:(dc + 1) * SEQ],
                                       xn_fm[:, dc * SEQ:(dc + 1) * SEQ],
                                       t["ln1_col"][:, dc:dc + 1])
            em.i(inst, "dve", 1, label=("p2ln1", dc))
        em.w(v, "dve", ("p2ln1", ND - 1))
        for r in range(NT_ALL):
            em.w(v, "pe", ("p2bc", r))
            for dc in range(ND):
                inst = v.tensor_mul(xn_fm[:, dc * SEQ + r * 128: dc * SEQ + (r + 1) * 128],
                                    xn_fm[:, dc * SEQ + r * 128: dc * SEQ + (r + 1) * 128],
                                    t["PS"][:, (r % 2) * 512:(r % 2) * 512 + 128])
            em.i(inst, "dve", 1, label=("p2h1", r))

    def p2_s(self, s):
        em, t = self.em, self.t
        em.join(s, "p1")
        em.w(s, "dve", "p2t2")
        em.selfsem(s, s.activation(t["st2"][:, :], t["st2"][:, :], AF.Ln,
                                   bias=EPS_RMS, scale=1.0 / D), "ssf")
        em.i(s.activation(t["r1_s"][:, :], t["st2"][:, :], AF.Exp, bias=0.0, scale=-0.5),
             "act", 1, label="p2r1")

    def p2_t(self, tt):
        em, t = self.em, self.t
        em.join(tt, "p1")
        em.w(tt, "dve", "p2r1b")
        em.i(tt.transpose(t["PTB"][0:32, 0:128], t["r1b"][:, :], t["sident"][:, :]),
             "pe", 1, label="p2tr")
        for r in range(NT_ALL):
            em.w(tt, "bn", "p2r1row")
            em.w(tt, "dve", ("p2ln1", ND - 1))
            em.wopt(tt, "dve", ("p2h1", r - 2))
            em.i(tt.matmul(t["PS"][:, (r % 2) * 512:(r % 2) * 512 + 128],
                           t["sones_bf"][:, :], t["r1row"][0:1, r * 128:(r + 1) * 128],
                           start=True, stop=True),
                 "pe", 1, label=("p2bc", r))

    def p2_y(self, y):
        em, t, io = self.em, self.t, self.io
        em.join(y, "p1")
        em.w(y, "dve", "p2trev")
        em.i(y.dma_start(io["r1_bounce"], t["r1T"][0:32, :]), "bn", 16, label="p2bsp")
        em.w(y, "bn", "p2bsp")
        em.i(y.dma_start(t["r1row"][0:1, :],
                         io["r1_bounce"].rearrange("a b -> (a b)")),
             "bn", 16, label="p2r1row")

    def alloc_p3(self, st):
        nc, t = self.nc, self.t
        e = st.enter_context
        t["swb"] = [e(nc.sbuf_tensor(f"p3swb{i}", [128, ND * 128], BF16)) for i in range(2)]
        t["swv"] = e(nc.sbuf_tensor("p3swv", [128, ND * 512], BF16))
        t["stage"] = [e(nc.sbuf_tensor(f"p3stage{i}", [128, D], BF16)) for i in range(2)]
        t["vstage"] = [e(nc.sbuf_tensor(f"p3vstage{i}", [128, 512], BF16)) for i in range(2)]
        seq = []
        for k in range(32):
            for rnd in range(1 if k < 16 else 2):
                seq.append((k, rnd))
        self.p3seq = seq

    def p3_g(self, g):
        em, t, io = self.em, self.t, self.io
        em.join(g, "p2")
        for k in range(32):
            wname = "wq" if k < 16 else "wk"
            oc = k % 16
            if k >= 2:
                em.w(g, "pe", ("p3pjoc", k - 2))
            em.i(g.dma_start(
                t["swb"][k % 2][:, :].rearrange("p (c o) -> p c o", c=ND),
                io[wname][:, oc * 128:(oc + 1) * 128].rearrange("(c p) o -> p c o", p=128)),
                f"w{k % 2}", 16, label=("p3w", k))
        for og in range(4):
            em.w(g, "pe", ("p3vwfree", og))
            for dc in range(ND):
                em.i(g.dma_start(t["swv"][:, dc * 512:(dc + 1) * 512],
                                 io["wv"][dc * 128:(dc + 1) * 128, og * 512:(og + 1) * 512]),
                     "wv", 16, label=("p3wv", og, dc))

    def p3_t(self, tt):
        em, t = self.em, self.t
        em.join(tt, "p2")
        PS = t["PS"]
        xn_fm = t["xn_fm"]
        for si, (k, rnd) in enumerate(self.p3seq):
            em.w(tt, f"w{k % 2}", ("p3w", k))
            em.wopt(tt, "dve", ("p3pjev", si - 1))
            for dc in range(ND):
                for gi in range(4):
                    col = dc * SEQ + rnd * 2048 + gi * 512
                    inst = tt.matmul(PS[:, gi * 512:(gi + 1) * 512],
                                     t["swb"][k % 2][:, dc * 128:(dc + 1) * 128],
                                     xn_fm[:, col:col + 512],
                                     start=(dc == 0), stop=(dc == ND - 1))
            em.i(inst, "pe", 1, label=("p3pj", si))
            if rnd == (0 if k < 16 else 1):
                em.mark("pe", ("p3pjoc", k))
        em.w(tt, "dve", ("p3pjev", len(self.p3seq) - 1))
        for og in range(4):
            em.mark("pe", ("p3vwfree", og))
            em.w(tt, "wv", ("p3wv", og, ND - 1))
            for r in range(NT_ALL):
                gr = og * NT_ALL + r
                em.wopt(tt, "dve", ("p3vev", gr - 4))
                bank = r % 4
                for dc in range(ND):
                    inst = tt.matmul(PS[:, bank * 512:(bank + 1) * 512],
                                     xn_fm[:, dc * SEQ + r * 128: dc * SEQ + (r + 1) * 128],
                                     t["swv"][:, dc * 512:(dc + 1) * 512],
                                     start=(dc == 0), stop=(dc == ND - 1))
                em.i(inst, "pe", 1, label=("p3v", gr))

    def p3_v(self, v):
        em, t = self.em, self.t
        em.join(v, "p2")
        PS = t["PS"]
        for si, (k, rnd) in enumerate(self.p3seq):
            em.w(v, "pe", ("p3pj", si))
            em.wopt(v, f"sp{si % 2}", ("p3sp", si - 2))
            st = t["stage"][si % 2]
            for gi in range(4):
                inst = v.tensor_copy(st[:, gi * 512:(gi + 1) * 512],
                                     PS[:, gi * 512:(gi + 1) * 512])
            em.i(inst, "dve", 1, label=("p3pjev", si))
        for gr in range(4 * NT_ALL):
            em.w(v, "pe", ("p3v", gr))
            em.wopt(v, f"vs{gr % 2}", ("p3vsp", gr - 2))
            em.i(v.tensor_copy(t["vstage"][gr % 2][:, :],
                               PS[:, (gr % 4) * 512:(gr % 4 + 1) * 512]),
                 "dve", 1, label=("p3vev", gr))

    def p3_y(self, y):
        em, t, io = self.em, self.t, self.io
        em.join(y, "p2")
        for si, (k, rnd) in enumerate(self.p3seq):
            em.w(y, "dve", ("p3pjev", si))
            st = t["stage"][si % 2]
            if k < 16:
                em.i(y.dma_start(io["qf_sp"][k, :, :], st[:, :]), f"sp{si % 2}", 16,
                     label=("p3sp", si))
            else:
                em.i(y.dma_start(io["kf_sp"][k - 16, :, rnd * 2048:(rnd + 1) * 2048],
                                 st[:, :]), f"sp{si % 2}", 16, label=("p3sp", si))
        for gr in range(4 * NT_ALL):
            og, r = divmod(gr, NT_ALL)
            em.w(y, "dve", ("p3vev", gr))
            em.i(y.dma_start(io["vf_sp"][r * 128:(r + 1) * 128, og * 512:(og + 1) * 512],
                             t["vstage"][gr % 2][:, :]), f"vs{gr % 2}", 16,
                 label=("p3vsp", gr))
        if self.dev:
            em.w(y, "dve", ("p2h1", NT_ALL - 1))
            em.i(y.dma_start(io["h1_tap"], t["xn_fm"][:, :]), "tap", 16)


    def _attn_unit_t_pass1(self, tt, u, cfg):
        em, t = self.em, self.t
        PS = t["PS"]
        nm = cfg["nm"]
        uid = u["uid"]
        if cfg.get("pre_sa") is not None:
            cfg["pre_sa"](tt, uid)
        for ks in range(8):
            b = ks % 4
            prev = (nm + "bm", uid, ks - 4) if ks >= 4 else (nm + "bm", uid - 1, ks + 4)
            em.wopt(tt, "dve", prev)
            inst = tt.matmul(PS[:, b * 512:(b + 1) * 512], u["lhsT"], u["krhs"](ks),
                             start=True, stop=True)
            em.i(inst, "pe", 1, label=(nm + "s1", uid, ks))

    def _attn_unit_t_main(self, tt, u, cfg):
        self._attn_unit_t_nt(tt, u, cfg)
        self._attn_unit_t_pass2(tt, u, cfg)
        self._attn_unit_t_av(tt, u, cfg)

    def _attn_unit_t_nt(self, tt, u, cfg):
        em, t = self.em, self.t
        PSB = t["PSB"]
        nm = cfg["nm"]
        uid = u["uid"]
        par = uid % 2
        em.w(tt, "dve", (nm + "ng", uid))
        em.wopt(tt, "dve", (nm + "ngr", uid - 1))
        if cfg.get("nt_wait") is not None:
            cfg["nt_wait"](tt, uid)
        em.i(tt.transpose(PSB[0:1, par * 128:(par + 1) * 128], u["ngb"],
                          t["sident"][:, :]),
             "pe", 1, label=(nm + "nt", uid))

    def _attn_unit_t_pass2(self, tt, u, cfg):
        em, t = self.em, self.t
        PS = t["PS"]
        nm = cfg["nm"]
        uid = u["uid"]
        for g in range(8):
            b2 = 2048 + (g % 2) * 512
            em.wopt(tt, "act", (nm + "ea", uid, g - 2))
            if g == 0:
                em.w(tt, "dve", (nm + "ngr", uid))
            tt.matmul(PS[:, b2:b2 + 512], t["sones_bf"][:, :],
                      u["ngrow4x"], start=True, stop=True)
            for k4 in range(4):
                kt = 4 * g + k4
                dst = PS[:, b2 + k4 * 128: b2 + (k4 + 1) * 128]
                inst = tt.matmul(dst, u["kt128"](kt), u["qrhs"], start=False,
                                 stop=False, skip_group_check=True)
            em.i(inst, "pe", 1, label=(nm + "s2", uid, g))

    def _attn_unit_t_av(self, tt, u, cfg):
        em, t = self.em, self.t
        PS = t["PS"]
        nm = cfg["nm"]
        avw = cfg["avw"]
        uid = u["uid"]
        avps = PS[:, 3072:3072 + avw]
        for g in range(8):
            if cfg["mask"]:
                em.w(tt, "dve", (nm + "pa", uid, g // 4))
            else:
                em.w(tt, "act", (nm + "ea", uid, g))
            if g == 0:
                em.wopt(tt, "dve", (nm + "oev", uid - 1))
            for k4 in range(4):
                kt = 4 * g + k4
                inst = tt.matmul(avps, u["ptbuf"][:, kt * 128:(kt + 1) * 128],
                                 u["vrhs"](kt), start=(kt == 0), stop=(kt == 31))
            if g == 7:
                em.i(inst, "pe", 1, label=(nm + "av", uid))

    def _attn_unit_v(self, v, u, cfg):
        em, t = self.em, self.t
        PS = t["PS"]
        PSB = t["PSB"]
        nm = cfg["nm"]
        hd = cfg["hd"]
        avw = cfg["avw"]
        uid = u["uid"]
        par = uid % 2
        sb = u["sb"]
        for ks in range(8):
            em.w(v, "pe", (nm + "s1", uid, ks))
            inst = v.reduce_max(sb["mad"][:, ks:ks + 1],
                                PS[:, (ks % 4) * 512:(ks % 4 + 1) * 512], axis=AX.X)
            em.i(inst, "dve", 1, label=(nm + "bm", uid, ks))
        em.w(v, "dve", (nm + "bm", uid, 7))
        em.selfsem(v, v.reduce_max(sb["m"], sb["mad"][:, 0:8], axis=AX.X), "vsf")
        em.i(v.tensor_scalar_mul(u["ngb"], sb["m"], -1.0), "dve", 1,
             label=(nm + "ng", uid))

    def _attn_unit_v_ngr(self, v, u, cfg):
        em, t = self.em, self.t
        PSB = t["PSB"]
        nm = cfg["nm"]
        uid = u["uid"]
        par = uid % 2
        em.w(v, "pe", (nm + "nt", uid))
        dst = u["ngrow4x"]
        src_ap = bass.AP(PSB, par * 128, [[512, 1], [0, 4], [1, 128]])
        em.i(v.tensor_copy(dst[0:1, :].rearrange("p (r q) -> p r q", r=4), src_ap),
             "dve", 1, label=(nm + "ngr", uid))

    def _attn_unit_v_post(self, v, u, cfg):
        if cfg["mask"]:
            self._attn_unit_v_pa(v, u, cfg)
        self._attn_unit_v_oev(v, u, cfg)

    def _attn_unit_v_pa(self, v, u, cfg):
        em = self.em
        nm = cfg["nm"]
        uid = u["uid"]
        for hh in range(2):
            em.w(v, "act", (nm + "ea", uid, 4 * hh + 3))
            em.w(v, cfg["mkld_sem"](uid), (nm + "mk", uid))
            em.i(v.tensor_mul(u["ptbuf"][:, hh * 2048:(hh + 1) * 2048],
                              u["ptbuf"][:, hh * 2048:(hh + 1) * 2048],
                              u["mask"][:, hh * 2048:(hh + 1) * 2048]),
                 "dve", 1, label=(nm + "pa", uid, hh))

    def _attn_unit_v_oev(self, v, u, cfg):
        em, t = self.em, self.t
        PS = t["PS"]
        nm = cfg["nm"]
        hd = cfg["hd"]
        uid = u["uid"]
        sb = u["sb"]
        em.w(v, "pe", (nm + "av", uid))
        em.selfsem(v, v.reciprocal(sb["il"], PS[:, 3072 + hd:3072 + hd + 1]), "vsf")
        if cfg.get("oev_wait") is not None:
            cfg["oev_wait"](v, uid)
        em.i(v.tensor_scalar_mul(u["odst"], PS[:, 3072:3072 + hd], sb["il"]),
             "dve", 1, label=(nm + "oev", uid))

    def _attn_unit_s(self, s, u, cfg):
        em, t = self.em, self.t
        PS = t["PS"]
        nm = cfg["nm"]
        sc = cfg["scale"]
        uid = u["uid"]
        for g in range(8):
            em.w(s, "pe", (nm + "s2", uid, g))
            if g == 0:
                em.wopt(s, "pe", (nm + "av", uid - 2))
            b2 = 2048 + (g % 2) * 512
            em.i(s.activation(u["ptbuf"][:, g * 512:(g + 1) * 512],
                              PS[:, b2:b2 + 512], AF.Exp, bias=0.0, scale=sc),
                 "act", 1, label=(nm + "ea", uid, g))

    def alloc_p4(self, st):
        nc, t = self.nc, self.t
        e = st.enter_context
        t["PSB"] = e(nc.psum_tensor("PSB4", [128, 512], BF16))
        t["ska"] = e(nc.sbuf_tensor("ska", [128, 4 * SEQ], BF16))
        t["sva"] = e(nc.sbuf_tensor("sva", [128, 32 * 520], BF16))
        t["swo"] = e(nc.sbuf_tensor("swo", [128, ND * D], BF16))
        t["sq"] = [e(nc.sbuf_tensor(f"sq{i}", [128, ND * 128], BF16)) for i in range(2)]
        t["ptbuf"] = [e(nc.sbuf_tensor(f"ptbuf{i}", [128, SEQ], BF16)) for i in range(2)]
        t["otm"] = e(nc.sbuf_tensor("otm", [128, D], BF16))
        t["ofmt"] = e(nc.sbuf_tensor("ofmt", [128, ND * 128], BF16))
        t["aostage"] = [e(nc.sbuf_tensor("aostage0", [128, D], F32))] * 2
        t["ast"] = [e(nc.sbuf_tensor(f"ast{i}", [128, 16], F32)) for i in range(2)]
        t["ngb4"] = e(nc.sbuf_tensor("ngb4", [128, 2], BF16))
        t["ngrow4"] = e(nc.sbuf_tensor("ngrow4", [1, 1024], BF16))

    def _p4_cfg(self):
        em = self.em

        def pre_sa(tt, uid):
            if uid % NH_A == 1:
                for og in range(4):
                    em.wopt(tt, "dve", ("p4aoev", uid // NH_A - 1, og))

        def oev_wait(v, uid):
            em.wopt(v, "pe", ("p4oft", uid // NH_A - 1, 3))

        def nt_wait(tt, uid):
            if uid % NH_A == 0:
                em.wopt(tt, "dve", ("p4oftev", uid // NH_A - 2, 3))
            elif uid % NH_A == 1:
                em.wopt(tt, "dve", ("p4oftev", uid // NH_A - 1, 3))

        return dict(nm="A4", hd=HD_A, avw=HD_A + 1, scale=SC_A, mask=False,
                    pre_sa=pre_sa, oev_wait=oev_wait, mkld_sem=None, nt_wait=nt_wait)

    def _p4_unit(self, qt, h):
        t = self.t
        uid = qt * NH_A + h
        par = uid % 2
        b = (h // 4) % 2
        c = 4 * (h // 8) + (h % 8) % 4
        jc = (h // 4) // 2
        j = h // 4
        sq = t["sq"][qt % 2]
        ska, sva = t["ska"], t["sva"]
        st = t["ast"][par]
        sb = dict(m=st[:, 8:9], il=st[:, 9:10], mad=st[:, 0:8])
        return dict(
            uid=uid, par=par,
            lhsT=sq[b * 64:(b + 1) * 64, c * 128:(c + 1) * 128],
            qrhs=sq[b * 64:(b + 1) * 64, c * 128:(c + 1) * 128],
            krhs=lambda ks: ska[b * 64:(b + 1) * 64,
                                jc * SEQ + ks * 512: jc * SEQ + (ks + 1) * 512],
            kt128=lambda kt: ska[b * 64:(b + 1) * 64,
                                 jc * SEQ + kt * 128: jc * SEQ + (kt + 1) * 128],
            vrhs=lambda kt: sva[:, kt * 520 + j * 65: kt * 520 + (j + 1) * 65],
            ngb=t["ngb4"][:, par:par + 1],
            ngrow4x=t["ngrow4"][0:1, par * 512:(par + 1) * 512],
            ptbuf=t["ptbuf"][par],
            odst=t["otm"][:, h * 64:(h + 1) * 64],
            sb=sb, mask=None)

    def p4_y(self, y):
        em, t, io = self.em, self.t, self.io
        em.join(y, "p3")
        em.i(y.dma_start(t["ska"][:, :].rearrange("p (c k) -> p c k", c=4),
                         io["ka_sp"].rearrange("c p k -> p c k")), "c4", 16)
        for j in range(8):
            em.i(y.dma_start(
                bass.AP(t["sva"], j * 65, [[32 * 520, 128], [520, 32], [1, 64]]),
                io["va_sp"].rearrange("(c p) (j i) -> p c j i", p=128, j=8)[:, :, j, :]),
                "c4", 16)
        em.mark("c4", "p4kv")
        for qt in range(NT_OWN):
            if qt >= 2:
                em.w(y, "pe", ("A4s1", (qt - 1) * NH_A, 0))
            em.i(y.dma_start(t["sq"][qt % 2][:, :].rearrange("p (c q) -> p c q", c=ND),
                             io["qa_sp"][:, :, qt * 128:(qt + 1) * 128]
                             .rearrange("c p q -> p c q")),
                 f"q{qt % 2}", 16, label=("p4q", qt))
            if qt >= 1:
                for og in range(4):
                    em.w(y, "dve", ("p4aoev", qt - 1, og))
                em.i(y.dma_start(io["ao_sp"][(qt - 1) * 128:qt * 128, :],
                                 t["aostage"][(qt - 1) % 2][:, :]), "ao4", 16,
                     label=("p4aosp", qt - 1))
        for og in range(4):
            em.w(y, "dve", ("p4aoev", NT_OWN - 1, og))
        em.i(y.dma_start(io["ao_sp"][(NT_OWN - 1) * 128:NT_OWN * 128, :],
                         t["aostage"][(NT_OWN - 1) % 2][:, :]), "ao4", 16,
             label=("p4aosp", NT_OWN - 1))

    def p4_g(self, g):
        em, t, io = self.em, self.t, self.io
        em.join(g, "p3")
        for dc in range(ND):
            em.i(g.dma_start(t["swo"][:, dc * D:(dc + 1) * D],
                             io["Wo"][dc * 128:(dc + 1) * 128, :]),
                 "wo4", 16, label=("p4wo", dc))
        em.w(g, "c4", "p4kv")
        em.i(g.memset(bass.AP(t["sva"], 64, [[32 * 520, 128], [520, 32], [65, 8], [1, 1]]),
                      1.0), "pool", 1, label="p4ones")

    def _p4_wo(self, tt, qt, ngr_uid):
        em, t = self.em, self.t
        PS = t["PS"]
        em.w(tt, "dve", ("A4oev", qt * NH_A + NH_A - 1))
        em.w(tt, "dve", ("A4ngr", ngr_uid))
        for rnd in range(4):
            if rnd > 0:
                em.w(tt, "dve", ("p4oftev", qt, rnd - 1))
            elif qt > 0:
                em.w(tt, "dve", ("p4oftev", qt - 1, 3))
            for k in range(4):
                inst = tt.transpose(
                    t["PSB"][:, k * 128:(k + 1) * 128],
                    t["otm"][:, (rnd * 4 + k) * 128:(rnd * 4 + k + 1) * 128],
                    t["sident"][:, :])
            em.i(inst, "pe", 1, label=("p4oft", qt, rnd))
        em.w(tt, "dve", ("p4oftev", qt, 3))
        for og in range(4):
            em.wopt(tt, "dve", ("p4aoev", qt - 1, og))
            em.wopt(tt, "dve", ("A4bm", qt * NH_A + NH_A - 1, og))
            em.wopt(tt, "dve", ("A4bm", qt * NH_A + NH_A - 1, og + 4))
            em.wopt(tt, "dve", ("A4bm", (qt + 1) * NH_A, og))
            em.wopt(tt, "dve", ("A4bm", (qt + 1) * NH_A, og + 4))
            inst = tt.matmul(PS[:, og * 512:(og + 1) * 512], t["sones_f"][:, :],
                             t["bo_row"][:, og * 512:(og + 1) * 512],
                             start=True, stop=False)
            for dc in range(ND):
                inst = tt.matmul(PS[:, og * 512:(og + 1) * 512],
                                 t["ofmt"][:, dc * 128:(dc + 1) * 128],
                                 t["swo"][:, dc * D + og * 512: dc * D + (og + 1) * 512],
                                 start=False, stop=(dc == ND - 1))
            em.i(inst, "pe", 1, label=("p4wom", qt, og))

    def p4_t(self, tt):
        em, t = self.em, self.t
        cfg = self._p4_cfg()
        em.join(tt, "p3")
        em.w(tt, "c4", "p4kv")
        em.w(tt, "pool", "p4ones")
        em.w(tt, "wo4", ("p4wo", ND - 1))
        em.w(tt, f"q{0}", ("p4q", 0))
        self._attn_unit_t_pass1(tt, self._p4_unit(0, 0), cfg)
        for qt in range(NT_OWN):
            for h in range(NH_A):
                uid = qt * NH_A + h
                u = self._p4_unit(qt, h)
                nqt, nh = divmod(uid + 1, NH_A)
                if h == 0 and qt > 0:
                    self._attn_unit_t_nt(tt, u, cfg)
                    self._attn_unit_t_av(tt, self._p4_unit(qt - 1, NH_A - 1), cfg)
                    self._p4_wo(tt, qt - 1, uid)
                    self._attn_unit_t_pass1(tt, self._p4_unit(nqt, nh), cfg)
                    self._attn_unit_t_pass2(tt, u, cfg)
                else:
                    if nqt < NT_OWN:
                        if nh == 0:
                            em.w(tt, f"q{nqt % 2}", ("p4q", nqt))
                        self._attn_unit_t_pass1(tt, self._p4_unit(nqt, nh), cfg)
                    self._attn_unit_t_nt(tt, u, cfg)
                    if uid > 0:
                        pq, ph = divmod(uid - 1, NH_A)
                        self._attn_unit_t_av(tt, self._p4_unit(pq, ph), cfg)
                    self._attn_unit_t_pass2(tt, u, cfg)
        self._attn_unit_t_av(tt, self._p4_unit(NT_OWN - 1, NH_A - 1), cfg)
        self._p4_wo(tt, NT_OWN - 1, NT_OWN * NH_A - 1)

    def _p4_wo_v(self, v, qt):
        em, t = self.em, self.t
        for rnd in range(4):
            em.w(v, "pe", ("p4oft", qt, rnd))
            em.i(v.tensor_copy(t["ofmt"][:, rnd * 512:(rnd + 1) * 512],
                               t["PSB"][:, :]),
                 "dve", 1, label=("p4oftev", qt, rnd))
        em.wopt(v, "ao4", ("p4aosp", qt - 1))
        for og in range(4):
            em.w(v, "pe", ("p4wom", qt, og))
            em.i(v.tensor_copy(t["aostage"][qt % 2][:, og * 512:(og + 1) * 512],
                               t["PS"][:, og * 512:(og + 1) * 512]),
                 "dve", 1, label=("p4aoev", qt, og))

    def p4_v(self, v):
        em, t = self.em, self.t
        cfg = self._p4_cfg()
        em.join(v, "p3")
        self._attn_unit_v(v, self._p4_unit(0, 0), cfg)
        for qt in range(NT_OWN):
            for h in range(NH_A):
                uid = qt * NH_A + h
                u = self._p4_unit(qt, h)
                nqt, nh = divmod(uid + 1, NH_A)
                if h == 0 and qt > 0:
                    self._attn_unit_v_ngr(v, u, cfg)
                    self._attn_unit_v_post(v, self._p4_unit(qt - 1, NH_A - 1), cfg)
                    self._p4_wo_v(v, qt - 1)
                    self._attn_unit_v(v, self._p4_unit(nqt, nh), cfg)
                else:
                    if nqt < NT_OWN:
                        self._attn_unit_v(v, self._p4_unit(nqt, nh), cfg)
                    self._attn_unit_v_ngr(v, u, cfg)
                    if uid > 0:
                        pq, ph = divmod(uid - 1, NH_A)
                        self._attn_unit_v_post(v, self._p4_unit(pq, ph), cfg)
        self._attn_unit_v_post(v, self._p4_unit(NT_OWN - 1, NH_A - 1), cfg)
        self._p4_wo_v(v, NT_OWN - 1)

    def p4_s(self, s):
        em = self.em
        cfg = self._p4_cfg()
        em.join(s, "p3")
        for qt in range(NT_OWN):
            for h in range(NH_A):
                self._attn_unit_s(s, self._p4_unit(qt, h), cfg)

    def alloc_p5(self, st):
        nc, t = self.nc, self.t
        e = st.enter_context
        t["PSB"] = e(nc.psum_tensor("PSB5", [128, 512], BF16))
        t["skf"] = [e(nc.sbuf_tensor(f"skf{i}", [128, SEQ], BF16)) for i in range(2)]
        t["svf"] = [e(nc.sbuf_tensor(f"svf{i}", [128, 32 * 129], BF16)) for i in range(2)]
        t["sqf"] = [e(nc.sbuf_tensor(f"sqf{i}", [128, OWN], BF16)) for i in range(2)]
        t["ptbuf"] = [e(nc.sbuf_tensor(f"p5ptbuf{i}", [128, SEQ], BF16)) for i in range(2)]
        t["mask_t"] = [e(nc.sbuf_tensor(f"mask{i}", [128, SEQ], BF16)) for i in range(2)]
        t["avtm"] = e(nc.sbuf_tensor("avtm", [128, 128], BF16))
        t["fst"] = [e(nc.sbuf_tensor(f"fst{i}", [128, 16], F32)) for i in range(2)]
        t["ngb5"] = e(nc.sbuf_tensor("ngb5", [128, 2], BF16))
        t["ngrow5"] = e(nc.sbuf_tensor("ngrow5", [1, 1024], BF16))

    def _p5_cfg(self):
        em = self.em

        def oev_wait(v, uid):
            em.wopt(v, "pe", ("p5ot", uid - 1))

        def nt_wait(tt, uid):
            em.wopt(tt, "dve", ("p5otev", uid - 2))

        return dict(nm="F5", hd=HD_F, avw=HD_F + 1, scale=SC_F, mask=True,
                    pre_sa=None, oev_wait=oev_wait,
                    mkld_sem=lambda uid: f"mk{uid % 2}", nt_wait=nt_wait)

    def _p5_unit(self, h, qt):
        t = self.t
        uid = h * NT_OWN + qt
        par = uid % 2
        st = t["fst"][par]
        sb = dict(m=st[:, 8:9], il=st[:, 9:10], mad=st[:, 0:8])
        skf = t["skf"][h % 2]
        svf = t["svf"][h % 2]
        return dict(
            uid=uid, par=par,
            lhsT=t["sqf"][h % 2][:, qt * 128:(qt + 1) * 128],
            qrhs=t["sqf"][h % 2][:, qt * 128:(qt + 1) * 128],
            krhs=lambda ks: skf[:, ks * 512:(ks + 1) * 512],
            kt128=lambda kt: skf[:, kt * 128:(kt + 1) * 128],
            vrhs=lambda kt: svf[:, kt * 129:(kt + 1) * 129],
            ngb=t["ngb5"][:, par:par + 1],
            ngrow4x=t["ngrow5"][0:1, par * 512:(par + 1) * 512],
            ptbuf=t["ptbuf"][par],
            odst=t["avtm"][:, :],
            sb=sb, mask=t["mask_t"][par])

    def p5_y(self, y):
        em, t, io = self.em, self.t, self.io
        em.join(y, "p4")
        for h in range(NH_F):
            if h >= 2:
                em.w(y, "pe", ("F5s1", (h - 1) * NT_OWN, 0))
            em.i(y.dma_start(t["skf"][h % 2][:, :], io["kf_sp"][h, :, :]),
                 f"kv{h % 2}", 16)
            em.i(y.dma_start(
                t["svf"][h % 2][:, :].rearrange("p (c k) -> p c k", c=32)[:, :, 0:128],
                io["vf_sp"][:, h * 128:(h + 1) * 128]
                .rearrange("(c p) k -> p c k", p=128)),
                 f"kv{h % 2}", 16)
            em.i(y.dma_start(t["sqf"][h % 2][:, :], io["qf_sp"][h, :, :]),
                 f"kv{h % 2}", 16, label=("p5kv", h))
            for qt in range(NT_OWN):
                uid = h * NT_OWN + qt
                em.wopt(y, "dve", ("F5pa", uid - 2, 1))
                em.i(y.dma_start(t["mask_t"][uid % 2][:, :], io["mask_sp"][qt, :, :]),
                     f"mk{uid % 2}", 16, label=("F5mk", uid))
        if self.dev:
            em.w(y, "dve", ("p5otev", NH_F * NT_OWN - 1))
            em.i(y.dma_start(io["ofm_tap"], t["OFM"][:, :]), "tap", 16)

    def p5_g(self, p):
        em, t = self.em, self.t
        em.join(p, "p4")
        for i in range(2):
            em.i(p.memset(t["svf"][i][:, :].rearrange("p (c k) -> p c k", c=32)[:, :, 128:129],
                          1.0), "pool", 1, label=("p5ones", i))

    def p5_t(self, tt):
        em, t = self.em, self.t
        cfg = self._p5_cfg()
        em.join(tt, "p4")
        em.w(tt, "pool", ("p5ones", 1))
        em.w(tt, "kv0", ("p5kv", 0))
        self._attn_unit_t_pass1(tt, self._p5_unit(0, 0), cfg)
        for h in range(NH_F):
            for qt in range(NT_OWN):
                uid = h * NT_OWN + qt
                u = self._p5_unit(h, qt)
                nh, nqt = divmod(uid + 1, NT_OWN)
                if nh < NH_F:
                    if nqt == 0:
                        em.w(tt, f"kv{nh % 2}", ("p5kv", nh))
                    self._attn_unit_t_pass1(tt, self._p5_unit(nh, nqt), cfg)
                self._attn_unit_t_nt(tt, u, cfg)
                if uid > 0:
                    self._attn_unit_t_av(tt, self._p5_unit(*divmod(uid - 1, NT_OWN)),
                                         cfg)
                self._attn_unit_t_pass2(tt, u, cfg)
                if uid > 0:
                    self._p5_ot(tt, uid - 1)
        self._attn_unit_t_av(tt, self._p5_unit(NH_F - 1, NT_OWN - 1), cfg)
        self._p5_ot(tt, NH_F * NT_OWN - 1)

    def _p5_ot(self, tt, uid):
        em, t = self.em, self.t
        em.w(tt, "dve", ("F5oev", uid))
        em.wopt(tt, "dve", ("p5otev", uid - 1))
        em.i(tt.transpose(t["PSB"][:, 256:384], t["avtm"][:, :],
                          t["sident"][:, :]),
             "pe", 1, label=("p5ot", uid))

    def p5_v(self, v):
        em, t = self.em, self.t
        cfg = self._p5_cfg()
        em.join(v, "p4")
        ofm = t["OFM"]
        self._attn_unit_v(v, self._p5_unit(0, 0), cfg)
        for h in range(NH_F):
            for qt in range(NT_OWN):
                uid = h * NT_OWN + qt
                u = self._p5_unit(h, qt)
                nh, nqt = divmod(uid + 1, NT_OWN)
                if nh < NH_F:
                    self._attn_unit_v(v, self._p5_unit(nh, nqt), cfg)
                self._attn_unit_v_ngr(v, u, cfg)
                if uid > 0:
                    pu = self._p5_unit(*divmod(uid - 1, NT_OWN))
                    self._attn_unit_v_oev(v, pu, cfg)
                    self._p5_otev(v, uid - 1)
                self._attn_unit_v_pa(v, u, cfg)
        last = NH_F * NT_OWN - 1
        self._attn_unit_v_oev(v, self._p5_unit(*divmod(last, NT_OWN)), cfg)
        self._p5_otev(v, last)

    def _p5_otev(self, v, uid):
        em, t = self.em, self.t
        h, qt = divmod(uid, NT_OWN)
        em.w(v, "pe", ("p5ot", uid))
        em.i(v.tensor_copy(t["OFM"][:, h * OWN + qt * 128: h * OWN + (qt + 1) * 128],
                           t["PSB"][:, 256:384]),
             "dve", 1, label=("p5otev", uid))

    def p5_s(self, s):
        em = self.em
        cfg = self._p5_cfg()
        em.join(s, "p4")
        for h in range(NH_F):
            for qt in range(NT_OWN):
                self._attn_unit_s(s, self._p5_unit(h, qt), cfg)

    def alloc_p6(self, st):
        nc, t = self.nc, self.t
        e = st.enter_context
        t["PTB"] = e(nc.psum_tensor("PTB6", [128, 1024], BF16))
        t["swo6"] = e(nc.sbuf_tensor("swo6", [128, ND * D], BF16))
        t["sxf6"] = [e(nc.sbuf_tensor(f"sxf6{i}", [128, D], F32)) for i in range(2)]
        t["xn6"] = [e(nc.sbuf_tensor(f"xn6{i}", [128, D], F32)) for i in range(2)]
        t["x26"] = [e(nc.sbuf_tensor(f"x26{i}", [128, D], F32)) for i in range(2)]
        t["tt6"] = e(nc.sbuf_tensor("tt6", [128, D], BF16))
        t["Lb"] = e(nc.sbuf_tensor("Lb", [128, 16], F32))
        t["curb"] = e(nc.sbuf_tensor("curb", [128, 16], F32))
        t["maskb"] = e(nc.sbuf_tensor("maskb", [128, 16], F32))
        t["Eb"] = e(nc.sbuf_tensor("Eb", [128, 16], F32))
        t["mjs"] = e(nc.sbuf_tensor("mjs", [128, 8], F32))
        t["g6"] = e(nc.sbuf_tensor("g6", [128, 8], F32))
        t["junk"] = e(nc.sbuf_tensor("junk6", [128, 2048], F32))

    def p6_g(self, g):
        em, t, io = self.em, self.t, self.io
        em.join(g, "p5")
        for dc in range(ND):
            em.i(g.dma_start(t["swo6"][:, dc * D:(dc + 1) * D],
                             io["wo"][dc * 128:(dc + 1) * 128, :]),
                 "wo6", 16, label=("p6wo", dc))

    def p6_y(self, y):
        em, t, io = self.em, self.t, self.io
        em.join(y, "p5")
        for qt in range(NT_OWN):
            if qt >= 2:
                em.w(y, "act", ("p6sq", qt - 2))
            em.i(y.dma_start(t["sxf6"][qt % 2][:, :], io["x_own"][qt * 128:(qt + 1) * 128, :]),
                 f"x{qt % 2}", 16, label=("p6x", qt))
            if qt >= 1:
                em.w(y, "dve", ("p6x2", qt - 1, 3))
                em.i(y.dma_start(io["x2_sp"][(qt - 1) * 128:qt * 128, :],
                                 t["x26"][(qt - 1) % 2][:, :]),
                     f"x2s{(qt - 1) % 2}", 16, label=("p6x2sp", qt - 1))
        em.w(y, "dve", ("p6x2", NT_OWN - 1, 3))
        em.i(y.dma_start(io["x2_sp"][(NT_OWN - 1) * 128:NT_OWN * 128, :],
                         t["x26"][(NT_OWN - 1) % 2][:, :]),
             f"x2s{(NT_OWN - 1) % 2}", 16, label=("p6x2sp", NT_OWN - 1))
        if self.dev:
            em.w(y, "dve", ("p6cw", NT_OWN - 1))
            em.i(y.dma_start(io["tfm_tap"], t["OFM"][:, :]), "tap", 16)
            em.i(y.dma_start(io["cw_tap"], t["s_cw"][:, :]), "tap", 16)

    def p6_t(self, tt):
        em, t = self.em, self.t
        em.join(tt, "p5")
        PS = t["PS"]
        buf = t["OFM"]
        em.w(tt, "wo6", ("p6wo", ND - 1))
        for qt in range(NT_OWN):
            for og in range(4):
                em.wopt(tt, "dve", ("p6x2", qt - 1, og))
                for dc in range(ND):
                    inst = tt.matmul(PS[:, og * 512:(og + 1) * 512],
                                     buf[:, dc * OWN + qt * 128: dc * OWN + (qt + 1) * 128],
                                     t["swo6"][:, dc * D + og * 512: dc * D + (og + 1) * 512],
                                     start=(dc == 0), stop=(dc == ND - 1))
                em.i(inst, "pe", 1, label=("p6wom", qt, og))
            em.w(tt, "act", ("p6tt", qt))
            for half in range(2):
                if half == 0:
                    em.wopt(tt, "dve", ("p6ttev", qt - 1, 1))
                else:
                    em.w(tt, "dve", ("p6ttev", qt, 0))
                for k in range(8):
                    inst = tt.transpose(
                        t["PTB"][:, k * 128:(k + 1) * 128],
                        t["tt6"][:, (half * 8 + k) * 128:(half * 8 + k + 1) * 128],
                        t["sident"][:, :])
                em.i(inst, "pe", 1, label=("p6ttr", qt, half))
            em.w(tt, "dve", ("p6ttev", qt, 1))
            em.wopt(tt, "dve", ("p6gev", qt - 1))
            for dc in range(ND):
                inst = tt.matmul(PS[:, 2048:2064],
                                 buf[:, dc * OWN + qt * 128: dc * OWN + (qt + 1) * 128],
                                 t["sgate"][:, dc * 16:(dc + 1) * 16],
                                 start=(dc == 0), stop=(dc == ND - 1))
            em.i(inst, "pe", 1, label=("p6gate", qt))

    def p6_s(self, s):
        em, t = self.em, self.t
        em.join(s, "p5")
        for qt in range(NT_OWN):
            em.w(s, f"x{qt % 2}", ("p6x", qt))
            em.wopt(s, "dve", ("p6x2", qt - 2, 3))
            em.i(s.activation(t["xn6"][qt % 2][:, :], t["sxf6"][qt % 2][:, :], AF.Identity,
                              bias=t["bias2_s"][:, qt:qt + 1], scale=t["rstd_s"][:, qt:qt + 1]),
                 "act", 1, label=("p6sq", qt))
            em.w(s, "dve", ("p6x2", qt, 3))
            em.selfsem(s, s.activation(t["junk"][:, :], t["x26"][qt % 2][:, :], AF.Square,
                                       bias=0.0, scale=1.0,
                                       accum_out=t["g6"][:, 0:1]), "ssf")
            em.selfsem(s, s.activation(t["g6"][:, 1:2], t["g6"][:, 0:1], AF.Ln,
                                       bias=EPS_RMS, scale=1.0 / D), "ssf")
            em.selfsem(s, s.activation(t["g6"][:, 2:3], t["g6"][:, 1:2], AF.Exp,
                                       bias=0.0, scale=-0.5), "ssf")
            em.wopt(s, "pe", ("p6ttr", qt - 1, 1))
            em.i(s.activation(t["tt6"][:, :], t["x26"][qt % 2][:, :], AF.Identity,
                              bias=0.0, scale=t["g6"][:, 2:3]),
                 "act", 1, label=("p6tt", qt))
            em.w(s, "dve", ("p6negm1", qt))
            em.i(s.activation(t["Eb"][:, :], t["Lb"][:, :], AF.Exp,
                              bias=t["g6"][:, 4 + qt % 2: 5 + qt % 2], scale=1.0),
                 "act", 1, label=("p6exp", qt))

    def p6_v(self, v):
        em, t = self.em, self.t
        em.join(v, "p5")
        PS = t["PS"]
        tfm = t["OFM"]
        for qt in range(NT_OWN):
            em.wopt(v, "act", ("p6tt", qt - 2))
            em.wopt(v, f"x2s{qt % 2}", ("p6x2sp", qt - 2))
            for og in range(4):
                em.w(v, "pe", ("p6wom", qt, og))
                em.w(v, "act", ("p6sq", qt))
                em.i(v.scalar_tensor_tensor(
                    t["x26"][qt % 2][:, og * 512:(og + 1) * 512],
                    PS[:, og * 512:(og + 1) * 512], 1.0,
                    t["xn6"][qt % 2][:, og * 512:(og + 1) * 512], ALU.mult, ALU.add),
                    "dve", 1, label=("p6x2", qt, og))
            for half in range(2):
                em.w(v, "pe", ("p6ttr", qt, half))
                for k in range(8):
                    dc = half * 8 + k
                    inst = v.tensor_scalar_mul(
                        tfm[:, dc * OWN + qt * 128: dc * OWN + (qt + 1) * 128],
                        t["PTB"][:, k * 128:(k + 1) * 128],
                        t["ln2_col"][:, dc:dc + 1])
                em.i(inst, "dve", 1, label=("p6ttev", qt, half))
            em.w(v, "pe", ("p6gate", qt))
            em.selfsem(v, v.tensor_copy(t["Lb"][:, :], PS[:, 2048:2064]), "vsf")
            em.selfsem(v, v.tensor_copy(t["curb"][:, :], t["Lb"][:, :]), "vsf")
            em.mark("dve", ("p6gev", qt))
            for j in range(6):
                em.selfsem(v, v.reduce_max(t["mjs"][:, j:j + 1], t["curb"][:, :], axis=AX.X),
                           "vsf")
                if j < 5:
                    em.selfsem(v, v.tensor_single_scalar(t["maskb"][:, :], t["curb"][:, :],
                                                         t["mjs"][:, j:j + 1], ALU.is_ge),
                               "vsf")
                    em.selfsem(v, v.scalar_tensor_tensor(t["curb"][:, :], t["maskb"][:, :],
                                                         -1e30, t["curb"][:, :],
                                                         ALU.mult, ALU.add), "vsf")
            em.selfsem(v, v.tensor_single_scalar(t["maskb"][:, :], t["Lb"][:, :],
                                                 t["mjs"][:, 5:6], ALU.is_ge), "vsf")
            em.wopt(v, "act", ("p6exp", qt - 2))
            em.i(v.tensor_scalar_mul(t["g6"][:, 4 + qt % 2: 5 + qt % 2],
                                     t["mjs"][:, 0:1], -1.0),
                 "dve", 1, label=("p6negm1", qt))
            em.w(v, "act", ("p6exp", qt))
            em.selfsem(v, v.tensor_mul(t["Eb"][:, :], t["Eb"][:, :], t["maskb"][:, :]), "vsf")
            em.selfsem(v, v.reduce_sum(t["g6"][:, 6:7], t["Eb"][:, :], axis=AX.X), "vsf")
            em.selfsem(v, v.reciprocal(t["g6"][:, 7:8], t["g6"][:, 6:7]), "vsf")
            em.i(v.tensor_scalar_mul(t["s_cw"][:, qt * 16:(qt + 1) * 16], t["Eb"][:, :],
                                     t["g6"][:, 7:8]),
                 "dve", 1, label=("p6cw", qt))

    def alloc_p7(self, st):
        nc, t = self.nc, self.t
        e = st.enter_context
        t["PS1"] = e(nc.psum_tensor("PS1", [128, 512], F32))
        t["hh"] = e(nc.sbuf_tensor("hh", [128, SUBI * 1024], BF16))
        t["hsil"] = e(nc.sbuf_tensor("hsil", [128, 1024], BF16))
        t["seg"] = [e(nc.sbuf_tensor(f"seg{i}", [128, ND * 128], BF16)) for i in range(2)]
        t["seu"] = [e(nc.sbuf_tensor(f"seu{i}", [128, ND * 128], BF16)) for i in range(2)]
        t["sed"] = e(nc.sbuf_tensor("sed", [128, SUBI * D], BF16))
        t["acc"] = e(nc.sbuf_tensor("acc", [128, 8 * D], F32))
        t["fxs"] = e(nc.sbuf_tensor("fxs", [128, D], F32))
        self.p7gu = []
        self.p7dn = []
        gix = 0
        dix = 0
        for half in range(2):
            for e_ in range(NEXP + 1):
                ni = NI_E if e_ < NEXP else NI_S
                subs = []
                s0 = 0
                while s0 < ni:
                    subs.append((s0, min(SUBI, ni - s0)))
                    s0 += SUBI
                for si, (s0, slen) in enumerate(subs):
                    for k in range(slen):
                        self.p7gu.append((half, e_, si, s0 + k, gix))
                        gix += 1
                    self.p7dn.append((half, e_, si, s0, slen, dix))
                    dix += 1
        self.p7subs = {}
        for half, e_, si, s0, slen, dix_ in self.p7dn:
            self.p7subs.setdefault((half, e_), []).append((si, s0, slen, dix_))

    def _wsl(self, name, e_, oc):
        io = self.io
        if e_ < NEXP:
            w = io["egT"] if name == "g" else io["euT"]
            return w[e_, :, oc * 128:(oc + 1) * 128].rearrange("(c p) o -> p c o", p=128)
        w = io["sgT"] if name == "g" else io["suT"]
        return w[:, oc * 128:(oc + 1) * 128].rearrange("(c p) o -> p c o", p=128)

    def _wdl(self, e_, ic):
        io = self.io
        if e_ < NEXP:
            return io["edT"][e_, ic * 128:(ic + 1) * 128, :]
        return io["sdT"][ic * 128:(ic + 1) * 128, :]

    def p7_g(self, g):
        em, t, io = self.em, self.t, self.io
        em.join(g, "p6")
        dn_by_gix = {}
        for half, e_, si, s0, slen, dix in self.p7dn:
            dn_by_gix[(half, e_, si)] = (s0, slen, dix)
        for half in range(2):
            for half2, e_, si, oc, gix in self.p7gu:
                if half2 != half:
                    continue
                em.wopt(g, "pe", ("p7um", gix - 2))
                em.i(g.dma_start(t["seg"][gix % 2][:, :].rearrange("p (c o) -> p c o", c=ND),
                                 self._wsl("g", e_, oc)), f"gw{gix % 2}", 16)
                em.i(g.dma_start(t["seu"][gix % 2][:, :].rearrange("p (c o) -> p c o", c=ND),
                                 self._wsl("u", e_, oc)), f"gw{gix % 2}", 16,
                     label=("p7wu", gix))
                s0, slen, dix = dn_by_gix[(half, e_, si)]
                if oc == s0 + slen - 1:
                    em.wopt(g, "pe", ("p7dnl", dix - 1))
                    for k in range(slen):
                        em.i(g.dma_start(t["sed"][:, k * D:(k + 1) * D],
                                         self._wdl(e_, s0 + k)),
                             "sed", 16, label=("p7wd", dix, k))
            for ttl in range(8):
                r = half * 8 + ttl
                em.w(g, "fx7", ("p7f1", half, ttl))
                em.i(g.dma_start(t["fxs"][:, :], io["ao_sp"][r * 128:(r + 1) * 128, :],
                                 accum_op=ALU.add), "ac7", 16, label=("p7ac1", half, ttl))
                em.w(g, "ac7", ("p7ac1", half, ttl))
                em.i(g.dma_start(t["fxs"][:, :], io["x2_sp"][r * 128:(r + 1) * 128, :],
                                 accum_op=ALU.add), "ac7", 16, label=("p7ac", half, ttl))

    def p7_t(self, tt):
        em, t = self.em, self.t
        em.join(tt, "p6")
        PS, PS1 = t["PS"], t["PS1"]
        tfm = t["OFM"]
        pg = [PS[:, 0:512], PS[:, 512:1024]]
        pu = [PS[:, 1024:1536], PS[:, 1536:2048]]
        pdown = [PS[:, 2048:2560], PS[:, 2560:3072], PS[:, 3072:3584], PS1[:, :]]

        gu_iter = iter(self.p7gu)
        for half, e_, subs_key in [(h, e, None) for h in range(2) for e in range(NEXP + 1)]:
            subs = self.p7subs[(half, e_)]
            for si, s0, slen, dix in subs:
                for k in range(slen):
                    half2, e2, si2, oc, gix = next(gu_iter)
                    assert (half2, e2, si2, oc) == (half, e_, si, s0 + k)
                    em.w(tt, f"gw{gix % 2}", ("p7wu", gix))
                    em.wopt(tt, "act", ("p7sil", gix - 1))
                    for gr in range(2):
                        for dc in range(ND):
                            col = dc * OWN + half * 1024 + gr * 512
                            inst = tt.matmul(pg[gr],
                                             t["seg"][gix % 2][:, dc * 128:(dc + 1) * 128],
                                             tfm[:, col:col + 512],
                                             start=(dc == 0), stop=(dc == ND - 1))
                    em.i(inst, "pe", 1, label=("p7gm", gix))
                    em.wopt(tt, "dve", ("p7hh", gix - 1))
                    for gr in range(2):
                        for dc in range(ND):
                            col = dc * OWN + half * 1024 + gr * 512
                            inst = tt.matmul(pu[gr],
                                             t["seu"][gix % 2][:, dc * 128:(dc + 1) * 128],
                                             tfm[:, col:col + 512],
                                             start=(dc == 0), stop=(dc == ND - 1))
                    em.i(inst, "pe", 1, label=("p7um", gix))
                em.w(tt, "sed", ("p7wd", dix, slen - 1))
                for ttl in range(8):
                    if ttl > 0:
                        em.w(tt, "dve", ("p7ev", dix, ttl - 1))
                    else:
                        em.wopt(tt, "dve", ("p7ev", dix - 1, 7))
                    for k in range(slen):
                        em.w(tt, "dve", ("p7hh", self._p7gix(half, e_, si, s0 + k)))
                        for dg in range(4):
                            inst = tt.matmul(
                                pdown[dg],
                                t["hh"][:, k * 1024 + ttl * 128: k * 1024 + ttl * 128 + 128],
                                t["sed"][:, k * D + dg * 512: k * D + (dg + 1) * 512],
                                start=(k == 0), stop=(k == slen - 1))
                    em.i(inst, "pe", 1, label=("p7dn", dix, ttl))
                em.mark("pe", ("p7dnl", dix))

    def _p7gix(self, half, e_, si, oc):
        if not hasattr(self, "_p7gixmap"):
            self._p7gixmap = {}
            for h2, e2, si2, oc2, gix in self.p7gu:
                self._p7gixmap[(h2, e2, si2, oc2)] = gix
        return self._p7gixmap[(half, e_, si, oc)]

    def p7_s(self, s):
        em, t = self.em, self.t
        em.join(s, "p6")
        for half, e_, si, oc, gix in self.p7gu:
            em.w(s, "pe", ("p7gm", gix))
            em.wopt(s, "dve", ("p7hh", gix - 1))
            em.i(s.activation(t["hsil"][:, :], t["PS"][:, 0:1024], AF.Silu,
                              bias=0.0, scale=1.0),
                 "act", 1, label=("p7sil", gix))

    def p7_v(self, v):
        em, t = self.em, self.t
        em.join(v, "p6")
        PS, PS1 = t["PS"], t["PS1"]
        pdown = [PS[:, 2048:2560], PS[:, 2560:3072], PS[:, 3072:3584], PS1[:, :]]
        gu_iter = iter(self.p7gu)
        first_writer = {}
        for half in range(2):
            for e_ in range(NEXP + 1):
                subs = self.p7subs[(half, e_)]
                for si, s0, slen, dix in subs:
                    for k in range(slen):
                        half2, e2, si2, oc, gix = next(gu_iter)
                        em.w(v, "act", ("p7sil", gix))
                        em.w(v, "pe", ("p7um", gix))
                        em.i(v.tensor_mul(t["hh"][:, k * 1024:(k + 1) * 1024],
                                          t["hsil"][:, :], PS[:, 1024:2048]),
                             "dve", 1, label=("p7hh", gix))
                    for ttl in range(8):
                        em.w(v, "pe", ("p7dn", dix, ttl))
                        em.wopt(v, "dve", ("p7ev", dix - 1, ttl))
                        cwcol = 256 if e_ == NEXP else (half * 8 + ttl) * 16 + e_
                        fw = first_writer.get((half, ttl))
                        for dg in range(4):
                            if fw is None:
                                inst = v.tensor_scalar_mul(
                                    t["acc"][:, ttl * D + dg * 512: ttl * D + (dg + 1) * 512],
                                    pdown[dg], t["s_cw"][:, cwcol:cwcol + 1])
                            else:
                                inst = v.scalar_tensor_tensor(
                                    t["acc"][:, ttl * D + dg * 512: ttl * D + (dg + 1) * 512],
                                    pdown[dg], t["s_cw"][:, cwcol:cwcol + 1],
                                    t["acc"][:, ttl * D + dg * 512: ttl * D + (dg + 1) * 512],
                                    ALU.mult, ALU.add)
                        first_writer[(half, ttl)] = True
                        em.i(inst, "dve", 1, label=("p7ev", dix, ttl))
            for ttl in range(8):
                last_dix = self.p7subs[(half, NEXP)][-1][3]
                em.w(v, "ac7", ("p7ac", half, ttl))
                em.w(v, "dve", ("p7ev", last_dix, ttl))
                em.i(v.tensor_add(t["fxs"][:, :], t["fxs"][:, :],
                                  t["acc"][:, ttl * D:(ttl + 1) * D]),
                     "dve", 1, label=("p7out", half, ttl))
            first_writer = {}

    def p7_y(self, y):
        em, t, io = self.em, self.t, self.io
        em.join(y, "p6")
        for half in range(2):
            for ttl in range(8):
                r = half * 8 + ttl
                if r >= 1:
                    em.w(y, "st7", ("p7st", (r - 1) // 8, (r - 1) % 8))
                em.i(y.dma_start(t["fxs"][:, :], io["x_own"][r * 128:(r + 1) * 128, :]),
                     "fx7", 16, label=("p7f1", half, ttl))
                em.w(y, "dve", ("p7out", half, ttl))
                em.i(y.dma_start(io["out"][r * 128:(r + 1) * 128, :], t["fxs"][:, :]),
                     "st7", 16, label=("p7st", half, ttl))
        em.w(y, "st7", ("p7st", 1, 7))

    def build(self, nphases=99):
        self.declare_dram()
        nc, em = self.nc, self.em
        with ExitStack() as persist:
            self.alloc_persist(persist)
            em.attach(nc, persist)
            block = persist.enter_context(nc.Block())
            mocks = {k: MockEng() for k in "tvsgy"}

            def run_phase(name, fns, allocf=None, pre=None, post=None):
                with ExitStack() as st:
                    if allocf:
                        allocf(st)
                    if pre:
                        pre()
                    em.p1()
                    for key in ("t", "v", "s", "g", "y"):
                        if key in fns:
                            fns[key](mocks[key])
                    em.endphase(name)
                    em.p2()
                    for key, section in (("t", block.tensor), ("v", block.vector),
                                         ("s", block.scalar), ("g", block.gpsimd),
                                         ("y", block.sync)):
                        if key in fns:
                            section(fns[key])
                    em.endphase(name)
                if post:
                    post()

            plist = [
                ("p0", lambda: run_phase("p0", dict(v=self.p0_v, g=self.p0_g, y=self.p0_y))),
                ("p1", lambda: (self.big_alloc("xn_fm", (128, ND * SEQ), BF16),
                                run_phase("p1", dict(t=self.p1_t, v=self.p1_v, s=self.p1_s,
                                                     g=self.p1_g, y=self.p1_y), self.alloc_p1))),
                ("p2", lambda: run_phase("p2", dict(t=self.p2_t, v=self.p2_v, s=self.p2_s,
                                                    y=self.p2_y), self.alloc_p2)),
                ("p3", lambda: run_phase("p3", dict(t=self.p3_t, v=self.p3_v, g=self.p3_g,
                                                    y=self.p3_y), self.alloc_p3,
                                         post=lambda: self.big_free("xn_fm"))),
                ("p4", lambda: run_phase("p4", dict(t=self.p4_t, v=self.p4_v, s=self.p4_s,
                                                    g=self.p4_g, y=self.p4_y), self.alloc_p4)),
                ("p5", lambda: (self.big_alloc("OFM", (128, NH_F * OWN), BF16),
                                run_phase("p5", dict(t=self.p5_t, v=self.p5_v, s=self.p5_s,
                                                     g=self.p5_g, y=self.p5_y), self.alloc_p5))),
                ("p6", lambda: run_phase("p6", dict(t=self.p6_t, v=self.p6_v, s=self.p6_s,
                                                    g=self.p6_g, y=self.p6_y), self.alloc_p6)),
                ("p7", lambda: run_phase("p7", dict(t=self.p7_t, v=self.p7_v, s=self.p7_s,
                                                    g=self.p7_g, y=self.p7_y), self.alloc_p7,
                                         post=lambda: self.big_free("OFM"))),
            ]
            lastnm = None
            for i, (nm_, fn_) in enumerate(plist):
                if i < nphases:
                    fn_()
                    lastnm = nm_
            fin = lastnm

            def mkfin(engname):
                def f(eng):
                    em.join(eng, fin)
                return f
            run_phase("fin", {k: mkfin(k) for k in ("t", "v", "s", "g", "y")})
            for nm_ in list(reversed(list(self.guards.keys()))):
                self.big_free(nm_)
        return nc


def build_program(dev=False, nphases=99):
    return Builder(dev=dev).build(nphases)


_BF = ml_dtypes.bfloat16


def _prep_weights(inputs):
    w = {}
    w["Wqkv"] = np.ascontiguousarray(inputs["L0_attn_Wqkv"], np.float32)
    bqkv = np.asarray(inputs["L0_attn_bqkv"], np.float32)
    w["bqkv_col"] = np.ascontiguousarray(bqkv.reshape(24, 128).T)
    w["bv_row"] = np.ascontiguousarray(bqkv[2560:3072][None, :])
    w["Wo"] = np.ascontiguousarray(inputs["L0_attn_Wo"], np.float32)
    w["bo_row"] = np.ascontiguousarray(np.asarray(inputs["L0_attn_bo"], np.float32)[None, :])
    w["wq"] = np.ascontiguousarray(inputs["L0_ffn_wq"], np.float32)
    w["wk"] = np.ascontiguousarray(inputs["L0_ffn_wk"], np.float32)
    w["wv"] = np.ascontiguousarray(inputs["L0_ffn_wv"], np.float32)
    w["wo"] = np.ascontiguousarray(inputs["L0_ffn_wo"], np.float32)
    w["ln1_col"] = np.ascontiguousarray(
        np.asarray(inputs["L0_ffn_ln1_w"], np.float32).reshape(16, 128).T)
    w["ln2_col"] = np.ascontiguousarray(
        np.asarray(inputs["L0_ffn_ln2_w"], np.float32).reshape(16, 128).T)
    w["gate_T"] = np.ascontiguousarray(np.asarray(inputs["L0_ffn_gate_w"], np.float32).T)
    w["egT"] = np.ascontiguousarray(
        np.asarray(inputs["L0_ffn_eg_w"], np.float32).transpose(0, 2, 1))
    w["euT"] = np.ascontiguousarray(
        np.asarray(inputs["L0_ffn_eu_w"], np.float32).transpose(0, 2, 1))
    w["edT"] = np.ascontiguousarray(
        np.asarray(inputs["L0_ffn_ed_w"], np.float32).transpose(0, 2, 1))
    w["sgT"] = np.ascontiguousarray(np.asarray(inputs["L0_ffn_sg_w"], np.float32).T)
    w["suT"] = np.ascontiguousarray(np.asarray(inputs["L0_ffn_su_w"], np.float32).T)
    w["sdT"] = np.ascontiguousarray(np.asarray(inputs["L0_ffn_sd_w"], np.float32).T)
    w["consts_in"] = np.tile(np.array([[EPS_LN, EPS_RMS, 1.0 / D, -0.5, SC_A, SC_F,
                                        0.0, 0.0]], np.float32), (128, 1))
    w["ident_bf"] = np.eye(128, dtype=_BF)
    w["ones_bf"] = np.ones((1, 128), _BF)
    w["ones_f32"] = np.ones((1, 128), np.float32)
    return w


def kernel(**inputs):
    x = np.asarray(inputs["L0_attn_x"], np.float32)
    w = _prep_weights(inputs)
    nc = build_program(dev=False)
    in_maps = []
    for c in range(8):
        b = c // 2
        h = c % 2
        m = dict(w)
        m["x_own"] = np.ascontiguousarray(x[b, h * 2048:(h + 1) * 2048, :])
        m["x_oth"] = np.ascontiguousarray(x[b, (1 - h) * 2048:(2 - h) * 2048, :])
        m["qbase_col"] = np.full((128, 1), float(h * 2048), np.float32)
        m["othbase_col"] = np.full((128, 1), float(-h * 2048), np.float32)
        in_maps.append(m)
    res = run_bass_kernel_spmd(nc, in_maps, core_ids=list(range(8)))
    out = np.empty((4, 4096, 2048), np.float32)
    for c in range(8):
        b = c // 2
        h = c % 2
        out[b, h * 2048:(h + 1) * 2048, :] = res.results[c]["out"]
    return out
